# version 02 (final)
# baseline (speedup 1.0000x reference)
import contextlib
import numpy as np
import ml_dtypes
import concourse.bass as bass
import concourse.mybir as mybir
from concourse.bass_utils import run_bass_kernel_spmd

F32 = mybir.dt.float32
BF16 = mybir.dt.bfloat16
U32 = mybir.dt.uint32
AF = mybir.ActivationFunctionType
ALU = mybir.AluOpType
AX = mybir.AxisListType.X
BF = ml_dtypes.bfloat16

P = 128
D = 1024
S = 4096
B = 4
TH = S // 2
NT = TH // P
NTF = S // P
NC = D // P

ENGINES = ("sync", "tensor", "vector", "scalar", "gpsimd")
DMA_SEMS = ("dS", "dG", "dA")
PAIR_GROUPS = [[0, 1], [2, 3], [4, 5], [6, 7]]
ALL_GROUP = [list(range(8))]


class Bld:
    def __init__(self, nc, es):
        self.nc = nc
        self.es = es
        self.thunks = {e: [] for e in ENGINES}
        self.sem_handles = {}
        self.cnt = {}
        self.wm = {}
        self.buf_id = 0
        self._sb_bytes = 0

    def sem(self, name):
        if name not in self.sem_handles:
            self.sem_handles[name] = self.es.enter_context(self.nc.semaphore(name))
            self.cnt[name] = 0
        return self.sem_handles[name]

    def sb(self, name, shape, dt):
        t = self.es.enter_context(self.nc.sbuf_tensor(name, shape, dt))
        self._sb_bytes += shape[0] * int(np.prod(shape[1:])) * np.dtype(mybir.dt.np(dt)).itemsize
        return t

    def ps(self, name, shape, dt):
        return self.es.enter_context(self.nc.psum_tensor(name, shape, dt))

    def emit(self, eng, fn, waits=(), inc=None, inc_n=1):
        wl = []
        for t in waits:
            if t is None:
                continue
            s, v = t
            if s[0] == "d":
                v = max(v, self.cnt.get(s, 0))
            if self.wm.get((eng, s), 0) < v:
                self.wm[(eng, s)] = v
                wl.append((s, v))
        tok = None
        if inc is not None:
            self.sem(inc)
            self.cnt[inc] += inc_n
            tok = (inc, self.cnt[inc])
        sems = self.sem_handles

        def thunk(e, wl=wl, fn=fn, inc=inc, inc_n=inc_n):
            for s, v in wl:
                e.wait_ge(sems[s], v)
            ins = fn(e)
            if inc is not None:
                if ins is None:
                    raise RuntimeError("inc requested but fn returned no instruction")
                if inc_n == 1 and inc.startswith("cc"):
                    ins.then_inc(sems[inc])
                else:
                    ins.then_inc(sems[inc], inc_n)
        self.thunks[eng].append(thunk)
        return tok

    def barrier_tok(self, eng, waits, inc):
        return self.emit(eng, lambda e: e.nop(), waits=waits, inc=inc)

    def finalize(self, block):
        for eng in ENGINES:
            ths = self.thunks[eng]
            if not ths:
                continue
            def runner(e, ths=ths):
                for t in ths:
                    t(e)
            getattr(block, eng)(runner)


DMAQ = {"sync": "sync", "gp": "gpsimd", "act": "scalar"}


def dma(b, queue, out_ap, in_ap, sem="dX", waits=()):
    eng = DMAQ[queue]
    if not hasattr(b, "last_dma"):
        b.last_dma = {}
    prev = b.last_dma.get(sem)
    waits = list(waits) + ([prev] if prev is not None else [])
    tok = b.emit(eng, lambda e: e.dma_start(out=out_ap, in_=in_ap),
                 waits=waits, inc=sem, inc_n=16)
    b.last_dma[sem] = tok
    return tok


class Psum:
    def __init__(self, b):
        self.b = b
        self.f = [b.ps(f"psf{i}", [P, 512], F32) for i in range(6)]
        self.bf = [b.ps(f"psb{i}", [P, 1024], BF16) for i in range(2)]
        self.ready = {}
        self.free = {}
        self.rr_f = 0
        self.rr_bf = 0

    def take_f(self):
        for _ in range(6):
            i = self.rr_f % 6
            self.rr_f += 1
            if i not in getattr(self, "reserved", set()):
                return ("f", i)
        raise RuntimeError("no free psum bank")

    def reserve_f(self):
        key = self.take_f()
        if not hasattr(self, "reserved"):
            self.reserved = set()
        self.reserved.add(key[1])
        return key

    def release_f(self, key):
        self.reserved.discard(key[1])

    def take_bf(self):
        i = self.rr_bf % 2
        self.rr_bf += 1
        return ("bf", i)

    def t(self, key):
        kind, i = key
        return (self.f if kind == "f" else self.bf)[i]

    def write_waits(self, key):
        w = []
        if key in self.free:
            w.append(self.free[key])
        return w

    def set_ready(self, key, tok):
        self.ready[key] = tok

    def read_waits(self, key):
        return [self.ready[key]]

    def set_free(self, key, tok):
        self.free[key] = tok


def mm_group(b, ps, key, mms, extra_waits=(), eng_sem="t"):
    outt = ps.t(key)
    n = len(mms)
    war = ps.write_waits(key)
    tok = None
    for i, fn in enumerate(mms):
        start = (i == 0)
        stop = (i == n - 1)
        waits = list(extra_waits) + (war if i == 0 else [])
        if stop:
            tok = b.emit("tensor", lambda e, fn=fn, s=start, p_=stop: fn(e, outt, s, p_),
                         waits=waits, inc=eng_sem)
        else:
            b.emit("tensor", lambda e, fn=fn, s=start, p_=stop: fn(e, outt, s, p_),
                   waits=waits)
    ps.set_ready(key, tok)
    return tok


class CC:
    def __init__(self, b):
        self.b = b
        self.n = 0

    def allgather_pair(self, dram_in, dram_out, in_tok):
        semn = f"cc{self.n % 4}"
        self.n += 1
        return self.b.emit(
            "gpsimd",
            lambda e: e.collective_compute(
                "AllGather", ALU.bypass, replica_groups=PAIR_GROUPS,
                ins=[dram_in.ap().opt()], outs=[dram_out.ap().opt()]),
            waits=[in_tok], inc=semn)

    def allreduce8(self, dram_in, dram_out, in_tok):
        semn = f"cc{self.n % 4}"
        self.n += 1
        return self.b.emit(
            "gpsimd",
            lambda e: e.collective_compute(
                "AllReduce", ALU.add, replica_groups=ALL_GROUP,
                ins=[dram_in.ap().opt()], outs=[dram_out.ap().opt()]),
            waits=[in_tok], inc=semn)


class MK:

    def __init__(self, b: Bld, ps: Psum, cc: CC, core_inputs):
        self.b = b
        self.ps = ps
        self.cc = cc
        self.inp = core_inputs
        nc = b.nc
        self.xt = b.sb("xt", [P, NT * D], F32)
        self.gA = b.sb("gA", [P, 8 * S], BF16)
        self.gB = b.sb("gB", [P, 4 * S], BF16)
        self.wA = b.sb("wA", [P, 8192], BF16)
        self.wB = b.sb("wB", [P, 8192], BF16)
        self.sA = b.sb("sA", [P, 2048], F32)
        self.sB = self.gB.bitcast(F32)
        self.xf = self.xt.bitcast(F32)
        self.sm = b.sb("sm", [P, 64], F32)
        self.sm2 = b.sb("sm2", [P, 256], F32)
        self.tl = b.sb("tl", [P, D], BF16)
        self.tgb = b.sb("tgb", [P, 256], BF16)
        self.xb = b.sb("xb", [P, 1536], BF16)
        self.ident = b.sb("ident", [P, P], BF16)
        self.cf32 = b.sb("cf32", [P, 8], F32)
        self.ones1f = b.sb("ones1f", [1, P], F32)
        self.ones1 = b.sb("ones1", [1, P], BF16)
        self.onesK = b.sb("onesK", [P, 1], BF16)
        self.tok = {}

    def w(self, key):
        t = self.tok.get(key)
        return [t] if t is not None else []

    def setw(self, key, tok):
        self.tok[key] = tok


CF_COLS = {1e-5: 0, 1e-6: 1, 1e-8: 2, 64e-5: 3, 1.0: 4, 0.5: 5, -1e30: 6, 0.0: 7}


def load_consts(mk):
    b = mk.b
    t1 = dma(b, "gp", mk.ident[:, :], mk.inp("c_ident"), sem="dC")
    t2 = dma(b, "gp", mk.ones1[:, :], mk.inp("c_ones_row"), sem="dC")
    t3 = dma(b, "gp", mk.onesK[:, :], mk.inp("c_ones_col"), sem="dC")
    t4 = dma(b, "gp", mk.cf32[:, :], mk.inp("c_f32"), sem="dC")
    t5 = dma(b, "gp", mk.ones1f[:, :], mk.inp("c_ones_row_f32"), sem="dC")
    mk.setw("consts", t3)
    mk.setw("consts1", t1)
    mk.setw("consts2", t2)
    mk.setw("consts3", t4)
    return [t1, t2, t3, t4]


def cf(mk, val):
    j = CF_COLS[val]
    return mk.cf32[:, j:j + 1]


def load_x(mk):
    b = mk.b
    src = mk.inp("x_own")
    srcr = src.rearrange("(n p) c -> n p c", p=P)
    toks = []
    for i in range(NT):
        toks.append(dma(b, "sync", mk.xt[:, i * D:(i + 1) * D], srcr[i], sem="dXt"))
    mk.setw("xt", toks[-1])
    mk.tok["xt_all"] = toks
    return toks


def store_x(mk, out_name="x_out"):
    b = mk.b
    dst = mk.inp(out_name)
    dstr = dst.rearrange("(n p) c -> n p c", p=P)
    toks = []
    for i in range(NT):
        toks.append(dma(b, "sync", dstr[i], mk.xt[:, i * D:(i + 1) * D], sem="dOut",
                        waits=mk.w("xt") + mk.w(f"xt{i}")))
    return toks


def ln_stats(mk, src, n_tiles, eps=1e-5, dst_off=0, width=D):
    b = mk.b
    nh = width // 512
    stats = mk.sm2
    t_in = mk.w("xt") + join_all(mk)
    tok2 = None
    for i in range(n_tiles):
        for h in range(nh):
            sap = src[:, i * width + h * 512:i * width + (h + 1) * 512]
            dap = stats[:, i * nh * 6 + h * 6:i * nh * 6 + (h + 1) * 6]
            t_in = [b.emit("vector", lambda e, sap=sap, dap=dap: e.bn_stats(dap, sap),
                           waits=t_in, inc="v")]
        agw = stats[:, i * nh * 6:(i + 1) * nh * 6]
        mvw = mk.sm[:, dst_off + 2 * i:dst_off + 2 * i + 2]
        tok2 = b.emit("vector", lambda e, agw=agw, mvw=mvw: e.bn_aggr(mvw, agw),
                      waits=t_in, inc="v")
    varap = mk.sm[:, dst_off + 1:dst_off + 2 * n_tiles:2]
    stdap = mk.sm[:, dst_off + 1:dst_off + 2 * n_tiles:2]
    tok3 = b.emit("scalar", lambda e: e.activation(stdap, varap, AF.Sqrt, bias=cf(mk, eps)),
                  waits=[tok2] + mk.w("consts3"), inc="s")
    tok4 = b.emit("vector", lambda e: e.reciprocal(stdap, stdap), waits=[tok3], inc="v")
    mk.setw("lnstat", tok4)
    return tok4


def ln_apply_tile(mk, src, i, dst, stat_off=0, gain=None, dst_dt_bf=True, waits=()):
    b = mk.b
    m = mk.sm[:, stat_off + 2 * i:stat_off + 2 * i + 1]
    r = mk.sm[:, stat_off + 2 * i + 1:stat_off + 2 * i + 2]
    return b.emit("vector",
                  lambda e: e.tensor_scalar(dst, src, m, r, ALU.subtract, ALU.mult),
                  waits=list(waits) + mk.w("lnstat"), inc="v")


def transpose_tile_to(mk, src_tile_ap, dst_arena, dst_view, i, tok_src):
    b, ps = mk.b, mk.ps
    key = ps.take_bf()
    bank = ps.t(key)
    war = ps.write_waits(key)
    tok = None
    for j in range(NC):
        waits = [tok_src] + (war if j == 0 else []) + mk.w("consts1")
        if j == NC - 1:
            tok = b.emit("tensor",
                         lambda e, j=j: e.transpose(bank[:, j * P:(j + 1) * P],
                                                    src_tile_ap[:, j * P:(j + 1) * P],
                                                    mk.ident[:, :]),
                         waits=waits, inc="t")
        else:
            b.emit("tensor",
                   lambda e, j=j: e.transpose(bank[:, j * P:(j + 1) * P],
                                              src_tile_ap[:, j * P:(j + 1) * P],
                                              mk.ident[:, :]),
                   waits=waits)
    ps.set_ready(key, tok)
    bview = bank.rearrange("p (j f) -> p j f", j=NC)
    etok = b.emit("vector", lambda e: e.tensor_copy(dst_view, bview),
                  waits=ps.read_waits(key), inc="v")
    ps.set_free(key, etok)
    return tok, etok


def ln_and_transpose(mk, n_src_tiles=NT, arena=None, t_span=TH, eps=1e-5):
    b = mk.b
    arena = arena if arena is not None else mk.gA
    ln_stats(mk, mk.xt, n_src_tiles, eps=eps)
    last = None
    av = arena[:, 0:NC * t_span].rearrange("p (j t) -> p j t", j=NC)
    for i in range(n_src_tiles):
        ta = ln_apply_tile(mk, mk.xt[:, i * D:(i + 1) * D], i, mk.tl[:, :],
                           waits=mk.w(f"tl_free"))
        dstv = av[:, :, i * P:(i + 1) * P]
        ttok, last = transpose_tile_to(mk, mk.tl[:, :], arena, dstv, i, ta)
        mk.setw("tl_free", ttok)
    mk.setw("xlnT", last)
    return last


def join_all(mk):
    return [(s, mk.b.cnt[s]) for s in ("v", "s", "t", "g")
            if mk.b.cnt.get(s, 0) > 0]


def xln_lhsT(mk, k, t0, tw=P, arena=None, t_span=TH):
    a = arena if arena is not None else mk.gA
    return a[:, k * t_span + t0:k * t_span + t0 + tw]


def load_w_kf(mk, name, dst, nK, f0, fw, ftot, sem="dwA", waits=(), dst_off=0):
    src = mk.inp(name).rearrange("(k p) f -> p k f", p=P)[:, :, f0:f0 + fw]
    dv = dst[:, dst_off:dst_off + nK * fw].rearrange("p (k f) -> p k f", k=nK)
    return dma(mk.b, "gp", dv, src, sem=sem, waits=waits)


def bias_mm(mk, bank, brow_ap, M, n0, nw, start=False, stop=False, waits=()):
    outt = mk.ps.t(bank)
    return lambda e, o, s, p_: e.matmul(
        o[0:M, n0:n0 + nw], mk.ones1[0:1, 0:M], brow_ap, start=s, stop=p_)


L3_STAGE = [99]


def emit_l3_ffn(mk, tag):
    b, ps = mk.b, mk.ps
    STG = L3_STAGE[0]
    ln_and_transpose(mk)
    wg = mk.wB
    t_wg = dma(b, "gp", wg[:, 0:64].rearrange("p (k e) -> p k e", k=NC),
               mk.inp("l3f_wg").rearrange("(k p) e -> p k e", p=P),
               sem="dwB", waits=mk.w(f"wB_free"))
    t_wd = load_w_kf(mk, "l3f_wd", mk.wA, NC, 0, 512, 512, sem="dwA",
                     waits=mk.w("wA_free"))
    wu = mk.gB
    t_wu = dma(b, "gp", wu[:, 8192:8192 + 4 * D].rearrange("p (j c) -> p j c", j=4),
               mk.inp("l3f_wu"), sem="dgB", waits=mk.w("gB_free"))
    t_bd = dma(b, "gp", wg[0:1, 64:64 + 512], mk.inp("l3f_bd"), sem="dwB")
    t_bu = dma(b, "gp", wg[0:8, 576:576 + D], mk.inp("l3f_bu"), sem="dwB")
    wtoks = [t_wg, t_wd, t_wu, t_bd, t_bu]

    pad = mk.sA
    m8 = mk.sm2
    newx_toks = []
    if STG < 1:
        return []
    for i in range(NT):
        t0 = i * P
        kk = ps.take_f()
        mms = []
        for k in range(NC):
            mms.append(lambda e, o, s, p_, k=k, t0=t0: e.matmul(
                o[:, 0:8], xln_lhsT(mk, k, t0), wg[:, k * 8:(k + 1) * 8],
                start=s, stop=p_))
        tmm = mm_group(b, ps, kk, mms, extra_waits=[t_wg] + mk.w("xlnT"))
        if STG < 2:
            ps.set_free(kk, b.barrier_tok("vector", ps.read_waits(kk), "v"))
            continue
        tp0 = b.emit("vector", lambda e: e.memset(pad[:, 0:512], -1e30),
                     waits=mk.w("pad_free"), inc="v")
        tp1 = b.emit("vector", lambda e, kt=ps.t(kk): e.tensor_copy(pad[:, 0:8], kt[:, 0:8]),
                     waits=ps.read_waits(kk) + [tp0], inc="v")
        ps.set_free(kk, tp1)
        tp2 = b.emit("vector", lambda e: e.max(m8[:, 0:8], pad[:, 0:512]),
                     waits=[tp1], inc="v")
        mk.setw("pad_free", tp2)
        tp3 = b.emit("vector", lambda e: e.tensor_scalar_mul(m8[:, 8:9], m8[:, 0:1], -1.0),
                     waits=[tp2], inc="v")
        tp4 = b.emit("scalar", lambda e: e.activation(m8[:, 9:10], m8[:, 1:2], AF.Exp,
                                                      bias=m8[:, 8:9]),
                     waits=[tp3], inc="s")
        tp5 = b.emit("vector", lambda e: e.tensor_scalar_add(m8[:, 9:10], m8[:, 9:10], 1.0),
                     waits=[tp4], inc="v")
        tp6 = b.emit("vector", lambda e: e.reciprocal(m8[:, 10:11], m8[:, 9:10]),
                     waits=[tp5], inc="v")
        tp7 = b.emit("vector", lambda e: e.tensor_scalar_mul(m8[:, 10:11], m8[:, 10:11], 0.5),
                     waits=[tp6], inc="v")
        tg1 = b.emit("scalar", lambda e: e.activation(m8[:, 16:24], pad[:, 0:8], AF.Exp,
                                                      bias=m8[:, 8:9]),
                     waits=[tp7], inc="s")
        tg2 = b.emit("vector", lambda e: e.tensor_scalar(m8[:, 24:32], pad[:, 0:8],
                                                         m8[:, 1:2], None, ALU.is_ge),
                     waits=[tg1], inc="v")
        tg3 = b.emit("vector", lambda e: e.tensor_mul(m8[:, 16:24], m8[:, 16:24], m8[:, 24:32]),
                     waits=[tg2], inc="v")
        tg4 = b.emit("vector", lambda e: e.tensor_scalar(m8[:, 16:24], m8[:, 16:24],
                                                         m8[:, 10:11], None, ALU.mult),
                     waits=[tg3], inc="v")
        tg5 = b.emit("vector", lambda e: e.tensor_copy(mk.tgb[:, 0:8], m8[:, 16:24]),
                     waits=[tg4] + mk.w("tgb_free"), inc="v")
        if STG < 3:
            continue
        hk = ps.take_f()
        mms = [lambda e, o, s, p_, k=k, t0=t0: e.matmul(
            o[:, :], xln_lhsT(mk, k, t0), mk.wA[:, 0:NC * 512].rearrange(
                "p (k f) -> p k f", k=NC)[:, k], start=s, stop=p_) for k in range(NC)]
        mms.append(lambda e, o, s, p_: e.matmul(
            o[:, :], mk.ones1[0:1, 0:P], wg[0:1, 64:64 + 512], start=s, stop=p_))
        tmh = mm_group(b, ps, hk, mms, extra_waits=[t_wd, t_bd] + mk.w("xlnT"))
        hmt = mk.gB
        ev = None
        for e_i in range(8):
            ev = b.emit("vector",
                        lambda e, e_i=e_i, ht=ps.t(hk): e.tensor_scalar(
                            hmt[:, e_i * 64:(e_i + 1) * 64],
                            ht[:, e_i * 64:(e_i + 1) * 64],
                            m8[:, 16 + e_i:17 + e_i], 0.0, ALU.mult, ALU.max),
                        waits=ps.read_waits(hk) + [tg4] + mk.w("gB_free"), inc="v")
        ps.set_free(hk, ev)
        if STG < 4:
            continue
        tk = ps.take_bf()
        bank = ps.t(tk)
        war = ps.write_waits(tk)
        ttok = None
        for j in range(4):
            waits = [ev] + (war if j == 0 else []) + mk.w("consts1")
            fn = lambda e, j=j, bank=bank: e.transpose(bank[:, j * P:(j + 1) * P],
                                           hmt[:, j * P:(j + 1) * P], mk.ident[:, :])
            if j == 3:
                ttok = b.emit("tensor", fn, waits=waits, inc="t")
            else:
                b.emit("tensor", fn, waits=waits)
        ps.set_ready(tk, ttok)
        ehm = b.emit("vector", lambda e, bank=bank: e.tensor_copy(hmt[:, 512:1024], bank[:, 0:512]),
                     waits=ps.read_waits(tk), inc="v")
        ps.set_free(tk, ehm)
        mk.setw("gB_free", ehm)
        tk2 = ps.take_bf()
        bank2 = ps.t(tk2)
        tgt = b.emit("tensor", lambda e, bank2=bank2: e.transpose(bank2[0:8, 0:P], mk.tgb[:, 0:8],
                                                     mk.ident[:, :]),
                     waits=[tg5] + ps.write_waits(tk2) + mk.w("consts1"), inc="t")
        ps.set_ready(tk2, tgt)
        egt = b.emit("vector", lambda e, bank2=bank2: e.tensor_copy(mk.tgb[0:8, 16:144], bank2[0:8, 0:P]),
                     waits=ps.read_waits(tk2), inc="v")
        ps.set_free(tk2, egt)
        mk.setw("tgb_free", egt)
        if STG < 5:
            continue
        for cch in range(2):
            yk = ps.take_f()
            mms = []
            for j in range(4):
                fb = 512 + j * P
                wc = 8192 + j * D + cch * 512
                mms.append(lambda e, o, s, p_, fb=fb, wc=wc: e.matmul(
                    o[:, :], hmt[:, fb:fb + P],
                    wu[:, wc:wc + 512], start=s, stop=p_))
            if STG >= 6:
                mms.append(lambda e, o, s, p_: e.matmul(
                    o[:, :], mk.tgb[0:8, 16:144],
                    wg[0:8, 576 + cch * 512:576 + cch * 512 + 512], start=s, stop=p_))
            tmy = mm_group(b, ps, yk, mms,
                           extra_waits=[ehm, egt, t_wu, t_bu])
            if STG >= 7:
                xa = mk.xt[:, i * D + cch * 512:i * D + cch * 512 + 512]
            else:
                xa = mk.sA[:, 0:512]
            tev = b.emit("vector", lambda e, xa=xa, yt=ps.t(yk): e.tensor_add(xa, xa, yt[:, :]),
                         waits=ps.read_waits(yk) + mk.w(f"xt{i}"), inc="v")
            ps.set_free(yk, tev)
            newx_toks.append(tev)
        mk.setw(f"xt{i}", newx_toks[-1])
    if STG >= 3:
        mk.setw("wA_free", tmh)
    if STG >= 5:
        mk.setw("wB_free", tmy)
    if newx_toks:
        mk.setw("xt", newx_toks[-1])
        mk.tok["xt_all"] = newx_toks
    return newx_toks


def shard_x(x_full, core):
    b_, half = core // 2, core % 2
    return np.ascontiguousarray(
        x_full[b_, half * TH:(half + 1) * TH, :]).astype(np.float32, copy=False)


def pack_consts(core):
    cvals = np.zeros((P, 8), np.float32)
    for v, j in CF_COLS.items():
        cvals[:, j] = v
    return {
        "c_ident": np.eye(P, dtype=BF),
        "c_ones_row": np.ones((1, P), dtype=BF),
        "c_ones_col": np.ones((P, 1), dtype=BF),
        "c_f32": cvals,
        "c_ones_row_f32": np.ones((1, P), np.float32),
    }


def pack_l3_ffn(inp, core):
    wd = np.transpose(inp["L3_ffn_Wd"], (1, 0, 2)).reshape(D, 512)
    wu0 = np.transpose(inp["L3_ffn_Wu"], (1, 0, 2))
    wu = np.concatenate([wu0[:, 0::2, :], wu0[:, 1::2, :]], axis=0)
    return {
        "l3f_wg": np.ascontiguousarray(inp["L3_ffn_w_gate"]),
        "l3f_wd": np.ascontiguousarray(wd),
        "l3f_bd": np.ascontiguousarray(inp["L3_ffn_bd"].reshape(1, 512)),
        "l3f_wu": np.ascontiguousarray(wu),
        "l3f_bu": np.ascontiguousarray(inp["L3_ffn_bu"]),
    }


SUB_EMITTERS = {}
SUB_PACK = {}


def register(name, emitter, pack):
    SUB_EMITTERS[name] = emitter
    SUB_PACK[name] = pack


register("l3_ffn", emit_l3_ffn, pack_l3_ffn)


def build_chain_nc(subs, in_specs):
    nc = bass.Bass("TRN2", target_bir_lowering=False, debug=False)
    dram = {}

    def inp(name):
        if name not in dram:
            shape, dt = in_specs[name]
            dram[name] = nc.dram_tensor(name, list(shape), dt, kind="ExternalInput").ap()
        return dram[name]

    x_out = nc.dram_tensor("x_out", [TH, D], F32, kind="ExternalOutput").ap()
    internal = {}

    es = contextlib.ExitStack()
    with es:
        b = Bld(nc, es)
        ps = Psum(b)
        cc = CC(b)
        mk = MK(b, ps, cc, inp)
        def _dram_internal(name, shape, dt):
            if name not in internal:
                internal[name] = nc.dram_tensor(name, list(shape), dt)
            return internal[name]
        mk.dram_internal = _dram_internal
        mk.x_out = x_out
        block = es.enter_context(nc.Block())
        load_consts(mk)
        load_x(mk)
        for i, s in enumerate(subs):
            SUB_EMITTERS[s](mk, tag=f"s{i}")
            if i < len(subs) - 1:
                full_barrier(mk)
        toks = store_x_to(mk, x_out)
        b.emit("gpsimd", lambda e: e.nop(), waits=toks)
        b.finalize(block)
    return nc


def full_barrier(mk):
    b = mk.b
    waits = [(s, c) for s, c in b.cnt.items() if c > 0]
    tok = b.emit("gpsimd", lambda e: e.nop(), waits=waits, inc="g")
    for eng in ("sync", "scalar", "vector", "tensor"):
        b.emit(eng, lambda e: e.nop(), waits=[tok])
    return tok


def store_x_to(mk, dst):
    b = mk.b
    dstr = dst.rearrange("(n p) c -> n p c", p=P)
    toks = []
    for i in range(NT):
        toks.append(dma(b, "sync", dstr[i], mk.xt[:, i * D:(i + 1) * D], sem="dOut",
                        waits=mk.w("xt") + mk.w(f"xt{i}") + join_all(mk)))
    return toks


def run_subchain(subs, inputs, x_full):
    maps = []
    for core in range(8):
        m = dict(pack_consts(core))
        m["x_own"] = shard_x(x_full, core)
        for s in subs:
            m.update(SUB_PACK[s](inputs, core))
        maps.append(m)
    specs = {}
    for k, v in maps[0].items():
        dt = {np.dtype(np.float32): F32, np.dtype(BF): BF16,
              np.dtype(np.int32): mybir.dt.int32}[v.dtype]
        specs[k] = (v.shape, dt)
    nc = build_chain_nc(subs, specs)
    res = run_bass_kernel_spmd(nc, maps, core_ids=list(range(8)))
    out = np.zeros((B, S, D), np.float32)
    for core in range(8):
        b_, half = core // 2, core % 2
        out[b_, half * TH:(half + 1) * TH, :] = res.results[core]["x_out"]
    return out


def run_subchain_sim(subs, inputs, x_full, cores=(0, 1)):
    from concourse import bass_interp
    maps = []
    for core in range(8):
        m = dict(pack_consts(core))
        m["x_own"] = shard_x(x_full, core)
        for s in subs:
            m.update(SUB_PACK[s](inputs, core))
        maps.append(m)
    specs = {}
    for k, v in maps[0].items():
        dt = {np.dtype(np.float32): F32, np.dtype(BF): BF16,
              np.dtype(np.int32): mybir.dt.int32}[v.dtype]
        specs[k] = (v.shape, dt)
    nc = build_chain_nc(subs, specs)
    out = np.zeros((B, S, D), np.float32)
    sim = bass_interp.MultiCoreSim(nc, 8)
    for core in range(8):
        for k, v in maps[core].items():
            try:
                sim.cores[core].tensor(k)[:] = v
            except KeyError:
                pass
    sim.simulate()
    for core in range(8):
        b_, half = core // 2, core % 2
        out[b_, half * TH:(half + 1) * TH, :] = sim.cores[core].mem_tensor("x_out")
    return out


def emit_nopln(mk, tag):
    ln_and_transpose(mk)
    return []


register("nopln", emit_nopln, lambda inp, core: {})
register("nop", lambda mk, tag: [], lambda inp, core: {})


def _ternarize(mk, src_name, dst_dram, rows, cols, tag):
    b, ps = mk.b, mk.ps
    nK = rows // P
    cw = 2048 if cols % 2048 == 0 else cols
    nch = cols // cw
    srcv = mk.inp(src_name).rearrange("(k p) f -> k p f", p=P)
    dstv = dst_dram.ap().rearrange("(k p) f -> k p f", p=P)
    sA = mk.sA
    sB = mk.sB
    acc = mk.sm2[:, 40:41]
    t0 = b.emit("vector", lambda e: e.memset(acc, 0.0), waits=join_all(mk), inc="v")
    for k in range(nK):
        for c in range(nch):
            tile = sB[:, 0:cw]
            tdma = dma(b, "gp", tile, srcv[k][:, c * cw:(c + 1) * cw],
                       sem="dgB", waits=[t0, mk.tok.get(f"tern{tag}")])
            tr = b.emit("vector",
                        lambda e, tile=tile: e.tensor_reduce(
                            mk.sm2[:, 41:42], tile, AX, ALU.add,
                            apply_absolute_value=True),
                        waits=[tdma], inc="v")
            t0 = b.emit("vector", lambda e: e.tensor_add(acc, acc, mk.sm2[:, 41:42]),
                        waits=[tr], inc="v")
            mk.setw(f"tern{tag}", t0)
    kk = ps.take_f()
    t1 = mm_group(b, ps, kk, [lambda e, o, s, p_: e.matmul(
        o[0:1, 0:1], acc, cf(mk, 1.0), start=s, stop=p_)],
        extra_waits=[t0] + mk.w("consts3"))
    thr1 = mk.sm2[0:1, 42:43]
    t2 = b.emit("vector", lambda e, kt=ps.t(kk): e.tensor_scalar_mul(
        thr1, kt[0:1, 0:1], 0.5 / (rows * cols)), waits=ps.read_waits(kk), inc="v")
    ps.set_free(kk, t2)
    k2 = ps.take_f()
    t3 = mm_group(b, ps, k2, [lambda e, o, s, p_: e.matmul(
        o[:, 0:1], mk.ones1f[0:1, :], thr1, start=s, stop=p_)],
        extra_waits=[t2] + mk.w("consts3"))
    thr = mk.sm2[:, 43:44]
    t4 = b.emit("vector", lambda e, kt=ps.t(k2): e.tensor_copy(thr, kt[:, 0:1]),
                waits=ps.read_waits(k2), inc="v")
    ps.set_free(k2, t4)
    tprev = t4
    for k in range(nK):
        for c in range(nch):
            tile = sB[:, 0:cw]
            ab = sB[:, 2048:2048 + cw]
            sg = sB[:, 4096:4096 + cw]
            wq = mk.gB[:, 12288:12288 + cw]
            tdma = dma(b, "gp", tile, srcv[k][:, c * cw:(c + 1) * cw],
                       sem="dgB", waits=[tprev])
            ta = b.emit("scalar", lambda e, tile=tile, ab=ab: e.activation(ab, tile, AF.Abs),
                        waits=[tdma], inc="s")
            tb = b.emit("scalar", lambda e, tile=tile, sg=sg: e.activation(sg, tile, AF.Sign),
                        waits=[tdma], inc="s")
            tc = b.emit("vector", lambda e, ab=ab: e.tensor_scalar(
                ab, ab, thr, None, ALU.is_gt), waits=[ta, t4], inc="v")
            td = b.emit("vector", lambda e, ab=ab, sg=sg, wq=wq: e.tensor_mul(wq, sg, ab),
                        waits=[tb, tc, mk.tok.get("ternw")], inc="v")
            tprev = dma(b, "gp", dstv[k][:, c * cw:(c + 1) * cw], wq,
                        sem="dtn", waits=[td])
            mk.setw("ternw", tprev)
    return tprev


def emit_l2_ffn(mk, tag):
    b, ps = mk.b, mk.ps
    wq_up = mk.dram_internal("l2f_wq_up", (D, 4096), BF16)
    wq_dn = mk.dram_internal("l2f_wq_dn", (4096, D), BF16)
    if "l2f_tern" not in mk.tok:
        tu = _ternarize(mk, "l2f_wup_cf", wq_up, D, 4096, "u")
        td = _ternarize(mk, "l2f_wdn_fc", wq_dn, 4096, D, "d")
        mk.setw("l2f_tern", b.barrier_tok("gpsimd", [tu, td], "g"))
    ttern = mk.tok["l2f_tern"]
    ln_stats(mk, mk.xt, NT)
    sc_off = 64
    for i in range(NT):
        r = mk.sm[:, 2 * i + 1:2 * i + 2]
        t1c = mk.sm2[:, 48:49]
        ta = b.emit("vector", lambda e, r=r: e.tensor_mul(mk.sm2[:, 48:49], r, r),
                    waits=mk.w("lnstat"), inc="v")
        tb = b.emit("vector", lambda e: e.reciprocal(mk.sm2[:, 49:50], mk.sm2[:, 48:49]),
                    waits=[ta], inc="v")
        tc = b.emit("vector", lambda e: e.tensor_scalar_add(mk.sm2[:, 49:50],
                                                            mk.sm2[:, 49:50], -1e-5),
                    waits=[tb], inc="v")
        tdk = b.emit("scalar", lambda e: e.activation(mk.sm2[:, 49:50], mk.sm2[:, 49:50],
                                                      AF.Sqrt), waits=[tc], inc="s")
        te = b.emit("vector", lambda e, r=r: e.tensor_mul(mk.sm2[:, 49:50], r,
                                                          mk.sm2[:, 49:50]),
                    waits=[tdk], inc="v")
        tf_ = b.emit("vector", lambda e: e.tensor_scalar_add(mk.sm2[:, 49:50],
                                                             mk.sm2[:, 49:50], 1e-8),
                     waits=[te], inc="v")
        tg = b.emit("vector", lambda e: e.reciprocal(mk.sm2[:, 49:50], mk.sm2[:, 49:50]),
                    waits=[tf_], inc="v")
        th = b.emit("vector", lambda e, r=r, i=i: e.tensor_mul(
            mk.sm2[:, sc_off + i:sc_off + i + 1], r, mk.sm2[:, 49:50]),
            waits=[tg], inc="v")
        mk.setw("l2scale", th)
    av = mk.gA[:, 0:NC * TH].rearrange("p (j t) -> p j t", j=NC)
    for i in range(NT):
        m = mk.sm[:, 2 * i:2 * i + 1]
        sc = mk.sm2[:, sc_off + i:sc_off + i + 1]
        ta = b.emit("vector",
                    lambda e, i=i, m=m, sc=sc: e.tensor_scalar(
                        mk.tl[:, 0:D], mk.xt[:, i * D:(i + 1) * D], m, sc,
                        ALU.subtract, ALU.mult),
                    waits=mk.w("l2scale") + mk.w("tl_free") + mk.w("xt") + mk.w(f"xt{i}"),
                    inc="v")
        dstv = av[:, :, i * P:(i + 1) * P]
        ttok, last = transpose_tile_to(mk, mk.tl[:, 0:D], mk.gA, dstv, i, ta)
        mk.setw("tl_free", ttok)
    mk.setw("xnT", last)
    wqu_v = wq_up.ap().rearrange("(k p) f -> k p f", p=P)
    wqd_v = wq_dn.ap().rearrange("(k p) f -> k p f", p=P)
    hbase = 16384
    for g in range(4):
        tg0 = g * 512
        sskey = ps.reserve_f()
        ss_war = ps.write_waits(sskey)
        outt = ps.t(sskey)
        sstok = None
        htoks = []
        for ft in range(32):
            wslot = mk.wA[:, (ft % 8) * 1024:(ft % 8) * 1024 + 1024]
            twl = dma(b, "gp", wslot.rearrange("p (k f) -> p k f", k=NC),
                      wqu_v[:, :, ft * P:(ft + 1) * P].rearrange("k p f -> p k f"),
                      sem="dwA", waits=[ttern] + mk.w(f"wAslot{ft % 8}")
                      + mk.w("wAdn"))
            hk = ps.take_f()
            mms = [lambda e, o, s, p_, k=k, wslot=wslot, tg0=tg0: e.matmul(
                o[:, :], wslot[:, k * P:(k + 1) * P],
                mk.gA[:, k * TH + tg0:k * TH + tg0 + 512], start=s, stop=p_)
                for k in range(NC)]
            tmm = mm_group(b, ps, hk, mms, extra_waits=[twl] + mk.w("xnT"))
            mk.setw(f"wAslot{ft % 8}", tmm)
            hdst = mk.gA[:, hbase + ft * 512:hbase + (ft + 1) * 512]
            tev = b.emit("scalar", lambda e, hdst=hdst, pt=ps.t(hk): e.activation(
                hdst, pt[:, :], AF.Gelu), waits=ps.read_waits(hk), inc="s")
            ps.set_free(hk, tev)
            tsq = b.emit("scalar", lambda e, hdst=hdst: e.activation(
                mk.tl[:, 0:512], hdst, AF.Square),
                waits=[tev] + mk.w("tlh") + mk.w("tl_free"), inc="s")
            for tb_ in range(4):
                start = (ft == 0 and tb_ == 0)
                stop = (ft == 31 and tb_ == 3)
                waits = [tsq] + (ss_war if start else []) + mk.w("consts")
                fnm = lambda e, tb_=tb_, s=start, p2=stop, outt=outt: e.matmul(
                    outt[:, tb_:tb_ + 1], mk.tl[:, tb_ * P:(tb_ + 1) * P],
                    mk.onesK[:, 0:1], start=s, stop=p2)
                if stop:
                    sstok = b.emit("tensor", fnm, waits=waits, inc="t")
                elif tb_ == 3:
                    mk.setw("tlh", b.emit("tensor", fnm, waits=waits, inc="t"))
                else:
                    b.emit("tensor", fnm, waits=waits)
            htoks.append(tev)
        ps.set_ready(sskey, sstok)
        t5 = b.emit("vector", lambda e, pt=ps.t(sskey): e.tensor_copy(mk.sm2[:, 52:56], pt[:, 0:4]),
                    waits=ps.read_waits(sskey), inc="v")
        ps.set_free(sskey, t5)
        ps.release_f(sskey)
        t6 = b.emit("scalar", lambda e: e.activation(mk.sm2[:, 52:56], mk.sm2[:, 52:56],
                                                     AF.Sqrt, scale=1.0 / 4096.0),
                    waits=[t5], inc="s")
        t7 = b.emit("vector", lambda e: e.tensor_scalar_add(mk.sm2[:, 52:56],
                                                            mk.sm2[:, 52:56], 1e-8),
                    waits=[t6], inc="v")
        t8 = b.emit("vector", lambda e: e.reciprocal(mk.sm2[:, 52:56], mk.sm2[:, 52:56]),
                    waits=[t7], inc="v")
        for nch in range(2):
            tw1 = dma(b, "gp", mk.wA.rearrange("p (k f) -> p k f", k=16),
                      wqd_v[0:16, :, nch * 512:nch * 512 + 512].rearrange("k p f -> p k f"),
                      sem="dwA", waits=[b.barrier_tok("gpsimd", htoks[-1:] + [
                          mk.tok.get(f"wAslot{j}") for j in range(8)], "g")]
                      + mk.w("wAdn"))
            tw2 = dma(b, "gp", mk.wB.rearrange("p (k f) -> p k f", k=16),
                      wqd_v[16:32, :, nch * 512:nch * 512 + 512].rearrange("k p f -> p k f"),
                      sem="dwB", waits=[ttern] + mk.w("wB_free"))
            for tb_ in range(4):
                i_glob = g * 4 + tb_
                yk = ps.take_f()
                mms = []
                for k in range(32):
                    wsl = (mk.wA if k < 16 else mk.wB)
                    kk_ = k % 16
                    mms.append(lambda e, o, s, p_, k=k, wsl=wsl, kk_=kk_, tb_=tb_: e.matmul(
                        o[:, :], mk.gA[:, hbase + k * 512 + tb_ * P:hbase + k * 512 + tb_ * P + P],
                        wsl[:, kk_ * 512:(kk_ + 1) * 512], start=s, stop=p_))
                tmy = mm_group(b, ps, yk, mms, extra_waits=[tw1, tw2] + htoks[-1:])
                xa = mk.xt[:, i_glob * D + nch * 512:i_glob * D + nch * 512 + 512]
                hs = mk.sm2[:, 52 + tb_:53 + tb_]
                tev = b.emit("vector", lambda e, xa=xa, hs=hs, yt=ps.t(yk):
                             e.scalar_tensor_tensor(xa, yt[:, :], hs, xa, ALU.mult, ALU.add),
                             waits=ps.read_waits(yk) + [t8] + mk.w(f"xt{i_glob}"), inc="v")
                ps.set_free(yk, tev)
                mk.setw(f"xt{i_glob}", tev)
                mk.setw("wB_free", tmy)
                mk.setw("wAdn", tmy)
    mk.setw("xt", tev)
    mk.setw("wA_free", tmy)
    mk.setw("wB_free", tmy)
    mk.setw("gB_free", ttern)
    return [tev]


def pack_l2_ffn(inp, core):
    return {
        "l2f_wup_cf": np.ascontiguousarray(inp["L2_ffn_w_up"].T),
        "l2f_wdn_fc": np.ascontiguousarray(inp["L2_ffn_w_down"].T),
    }


register("l2_ffn", emit_l2_ffn, pack_l2_ffn)


def emit_l0_ffn(mk, tag):
    b, ps, cc = mk.b, mk.ps, mk.cc
    ln_and_transpose(mk)
    wg = mk.wB
    t_wg = dma(b, "gp", wg[:, 0:64].rearrange("p (k e) -> p k e", k=NC),
               mk.inp("l0f_gate").rearrange("(k p) e -> p k e", p=P),
               sem="dwB", waits=mk.w("wB_free"))
    t_lb = dma(b, "gp", wg[0:1, 64:72], mk.inp("l0f_lb"), sem="dwB")
    pad = mk.sA
    m8 = mk.sm2
    selbase = 14336
    cntkey = ps.reserve_f()
    cnt_war = ps.write_waits(cntkey)
    cnt_out = ps.t(cntkey)
    cnttok = None
    sel_toks = []
    for i in range(NT):
        t0 = i * P
        kk = ps.take_f()
        mms = [lambda e, o, s, p_, k=k, t0=t0: e.matmul(
            o[:, 0:8], xln_lhsT(mk, k, t0), wg[:, k * 8:(k + 1) * 8],
            start=s, stop=p_) for k in range(NC)]
        mms.append(lambda e, o, s, p_: e.matmul(
            o[:, 0:8], mk.ones1[0:1, 0:P], wg[0:1, 64:72], start=s, stop=p_))
        tmm = mm_group(b, ps, kk, mms, extra_waits=[t_wg, t_lb] + mk.w("xlnT"))
        tp0 = b.emit("vector", lambda e: e.memset(pad[:, 0:512], -1e30),
                     waits=mk.w("pad_free"), inc="v")
        tp1 = b.emit("scalar", lambda e, kt=ps.t(kk): e.activation(
            pad[:, 0:8], kt[:, 0:8], AF.Sigmoid), waits=ps.read_waits(kk) + [tp0],
            inc="s")
        ps.set_free(kk, tp1)
        tp2 = b.emit("vector", lambda e: e.max(m8[:, 0:8], pad[:, 0:512]),
                     waits=[tp1], inc="v")
        mk.setw("pad_free", tp2)
        tq1 = b.emit("vector", lambda e: e.tensor_add(m8[:, 8:9], m8[:, 0:1], m8[:, 1:2]),
                     waits=[tp2], inc="v")
        tq2 = b.emit("vector", lambda e: e.tensor_scalar_add(m8[:, 8:9], m8[:, 8:9], 1e-8),
                     waits=[tq1], inc="v")
        tq3 = b.emit("vector", lambda e: e.reciprocal(m8[:, 8:9], m8[:, 8:9]),
                     waits=[tq2], inc="v")
        tq4 = b.emit("vector", lambda e: e.tensor_mul(m8[:, 9:10], m8[:, 1:2], m8[:, 8:9]),
                     waits=[tq3], inc="v")
        tq5 = b.emit("vector", lambda e: e.tensor_sub(m8[:, 10:11], m8[:, 0:1], m8[:, 1:2]),
                     waits=[tq4], inc="v")
        tq6 = b.emit("vector", lambda e: e.tensor_mul(m8[:, 10:11], m8[:, 10:11], m8[:, 8:9]),
                     waits=[tq5], inc="v")
        ts1 = b.emit("vector", lambda e: e.tensor_scalar(
            m8[:, 16:24], pad[:, 0:8], m8[:, 1:2], None, ALU.is_ge),
            waits=[tq6], inc="v")
        ts2 = b.emit("vector", lambda e: e.tensor_scalar(
            m8[:, 24:32], pad[:, 0:8], m8[:, 0:1], None, ALU.is_equal),
            waits=[ts1], inc="v")
        tc1 = b.emit("vector", lambda e: e.tensor_scalar(
            m8[:, 32:40], m8[:, 16:24], m8[:, 9:10], None, ALU.mult),
            waits=[ts2], inc="v")
        tc2 = b.emit("vector", lambda e: e.scalar_tensor_tensor(
            m8[:, 32:40], m8[:, 24:32], m8[:, 10:11], m8[:, 32:40], ALU.mult, ALU.add),
            waits=[tc1], inc="v")
        tb1 = b.emit("vector", lambda e, i=i: e.tensor_copy(
            mk.gB[:, selbase + i * 8:selbase + (i + 1) * 8], m8[:, 16:24]),
            waits=[tc2] + mk.w("gBsel"), inc="v")
        tb2 = b.emit("vector", lambda e: e.tensor_copy(mk.tgb[:, 0:8], m8[:, 32:40]),
                     waits=[tc2] + mk.w("tgb_free"), inc="v")
        mk.setw("tgb_free", tb2)
        sel_toks.append(tb1)
        st_ = (i == 0)
        sp = (i == NT - 1)
        w1_ = [tb1] + (cnt_war if st_ else []) + mk.w("consts")
        b.emit("tensor", lambda e, i=i, s=st_: e.matmul(
            cnt_out[0:8, 0:1], mk.gB[:, selbase + i * 8:selbase + (i + 1) * 8],
            mk.onesK[:, 0:1], start=s, stop=False), waits=w1_)
        fn2 = lambda e, s=sp: e.matmul(
            cnt_out[0:8, 1:2], mk.tgb[:, 0:8], mk.onesK[:, 0:1],
            start=False, stop=s)
        if sp:
            cnttok = b.emit("tensor", fn2, waits=[tb2], inc="t")
        else:
            b.emit("tensor", fn2, waits=[tb2])
    ps.set_ready(cntkey, cnttok)
    red_in = mk.dram_internal(f"l0red_in", (8, 2), F32)
    red_out = mk.dram_internal(f"l0red_out", (8, 2), F32)
    te1 = b.emit("vector", lambda e: e.tensor_copy(mk.sm2[0:8, 48:50], cnt_out[0:8, 0:2]),
                 waits=ps.read_waits(cntkey), inc="v")
    ps.set_free(cntkey, te1)
    ps.release_f(cntkey)
    te2 = dma(b, "gp", red_in.ap(), mk.sm2[0:8, 48:50], sem="dRed", waits=[te1])
    te3 = cc.allreduce8(red_in, red_out, te2)
    te4 = dma(b, "gp", mk.sm2[0:1, 56:72], red_out.ap().rearrange("a b -> (a b)").rearrange("(x f) -> x f", x=1), sem="dRed2", waits=[te3])
    tf1 = b.emit("vector", lambda e: e.tensor_scalar_max(
        mk.sm2[0:1, 56:72:2], mk.sm2[0:1, 56:72:2], 1.0), waits=[te4], inc="v")
    tf2 = b.emit("vector", lambda e: e.reciprocal(mk.sm2[0:1, 56:72:2],
                                                  mk.sm2[0:1, 56:72:2]),
                 waits=[tf1], inc="v")
    tf3 = b.emit("vector", lambda e: e.tensor_mul(mk.sm2[0:1, 72:80],
                                                  mk.sm2[0:1, 57:72:2],
                                                  mk.sm2[0:1, 56:72:2]),
                 waits=[tf2], inc="v")
    tf4 = b.emit("vector", lambda e: e.tensor_copy(mk.tgb[0:1, 8:16], mk.sm2[0:1, 72:80]),
                 waits=[tf3], inc="v")
    wmk = ps.take_f()
    twm = mm_group(b, ps, wmk, [lambda e, o, s, p_: e.matmul(
        o[:, 0:8], mk.ones1[0:1, 0:P], mk.tgb[0:1, 8:16], start=s, stop=p_)],
        extra_waits=[tf4] + mk.w("consts2"))
    twm2 = b.emit("vector", lambda e, kt=ps.t(wmk): e.tensor_copy(mk.sm2[:, 80:88], kt[:, 0:8]),
                  waits=ps.read_waits(wmk), inc="v")
    ps.set_free(wmk, twm2)
    sc_toks = []
    for i in range(NT):
        tsc = b.emit("vector", lambda e, i=i: e.tensor_mul(
            mk.gB[:, selbase + 128 + i * 8:selbase + 128 + (i + 1) * 8],
            mk.gB[:, selbase + i * 8:selbase + (i + 1) * 8].bitcast(BF16),
            mk.sm2[:, 80:88].bitcast(F32)),
            waits=[twm2] + sel_toks, inc="v")
        sc_toks.append(tsc)
    mk.setw("gBsel", sc_toks[-1])

    def mlp(eidx, gname, uname, dname, iw, scale_col):
        nf = iw // P
        t_g = dma(b, "gp", mk.wA[:, 0:NC * iw].rearrange("p (k f) -> p k f", k=NC),
                  gname, sem="dwA", waits=mk.w("wA_free"))
        t_u = dma(b, "gp", mk.wB[:, 0:NC * iw].rearrange("p (k f) -> p k f", k=NC),
                  uname, sem="dwB", waits=mk.w("wB_free2"))
        gu_toks = []
        for ft in range(nf):
            for ncch in range(TH // 512):
                gk = ps.take_f()
                mms = [lambda e, o, s, p_, k=k, ft=ft, ncch=ncch: e.matmul(
                    o[:, :], mk.wA[:, k * iw + ft * P:k * iw + (ft + 1) * P],
                    xln_lhsT(mk, k, ncch * 512, 512), start=s, stop=p_)
                    for k in range(NC)]
                tg_ = mm_group(b, ps, gk, mms, extra_waits=[t_g] + mk.w("xlnT"))
                tev = b.emit("scalar", lambda e, kt=ps.t(gk), ft=ft, ncch=ncch: e.activation(
                    mk.gB[:, ft * TH + ncch * 512:ft * TH + (ncch + 1) * 512],
                    kt[:, :], AF.Silu), waits=ps.read_waits(gk) + mk.w("gBg"), inc="s")
                ps.set_free(gk, tev)
                uk = ps.take_f()
                mms = [lambda e, o, s, p_, k=k, ft=ft, ncch=ncch: e.matmul(
                    o[:, :], mk.wB[:, k * iw + ft * P:k * iw + (ft + 1) * P],
                    xln_lhsT(mk, k, ncch * 512, 512), start=s, stop=p_)
                    for k in range(NC)]
                tu_ = mm_group(b, ps, uk, mms, extra_waits=[t_u] + mk.w("xlnT"))
                thv = b.emit("vector", lambda e, kt=ps.t(uk), ft=ft, ncch=ncch:
                             e.tensor_mul(
                                 mk.gB[:, ft * TH + ncch * 512:ft * TH + (ncch + 1) * 512],
                                 mk.gB[:, ft * TH + ncch * 512:ft * TH + (ncch + 1) * 512],
                                 kt[:, :]),
                             waits=ps.read_waits(uk) + [tev], inc="v")
                ps.set_free(uk, thv)
                gu_toks.append(thv)
        mk.setw("wA_free", tg_)
        t_d = dma(b, "gp", mk.wA[:, 0:nf * D].rearrange("p (k f) -> p k f", k=nf),
                  dname, sem="dwA", waits=[tg_] + gu_toks[-1:])
        last = None
        for i in range(NT):
            for ncch in range(2):
                yk = ps.take_f()
                mms = [lambda e, o, s, p_, k=k, i=i, ncch=ncch: e.matmul(
                    o[:, :], mk.gB[:, k * TH + i * P:k * TH + (i + 1) * P],
                    mk.wA[:, k * D + ncch * 512:k * D + (ncch + 1) * 512],
                    start=s, stop=p_) for k in range(nf)]
                ty = mm_group(b, ps, yk, mms, extra_waits=[t_d] + gu_toks[-1:])
                xa = mk.xt[:, i * D + ncch * 512:i * D + ncch * 512 + 512]
                if scale_col is not None:
                    sc = mk.gB[:, selbase + 128 + i * 8 + eidx:selbase + 128 + i * 8 + eidx + 1].bitcast(BF16)
                    tev = b.emit("vector", lambda e, xa=xa, sc=sc, yt=ps.t(yk):
                                 e.scalar_tensor_tensor(xa, yt[:, :], sc, xa,
                                                        ALU.mult, ALU.add),
                                 waits=ps.read_waits(yk) + mk.w(f"xt{i}") + sc_toks[-1:],
                                 inc="v")
                else:
                    tev = b.emit("vector", lambda e, xa=xa, yt=ps.t(yk):
                                 e.tensor_add(xa, xa, yt[:, :]),
                                 waits=ps.read_waits(yk) + mk.w(f"xt{i}"), inc="v")
                ps.set_free(yk, tev)
                mk.setw(f"xt{i}", tev)
                last = tev
        mk.setw("wA_free", ty)
        mk.setw("gBg", last)
        return last

    egv = mk.inp("l0f_eg")
    euv = mk.inp("l0f_eu")
    edv = mk.inp("l0f_ed")
    for e_i in range(8):
        g_ap = egv[:, e_i, :].rearrange("(k p) f -> p k f", p=P)
        u_ap = euv[:, e_i, :].rearrange("(k p) f -> p k f", p=P)
        d_ap = edv[e_i].rearrange("(k p) f -> p k f", p=P)
        last = mlp(e_i, g_ap, u_ap, d_ap, 512, scale_col=True)
        mk.setw("wB_free2", last)
    sg = mk.inp("l0f_sg").rearrange("(k p) f -> p k f", p=P)
    su = mk.inp("l0f_su").rearrange("(k p) f -> p k f", p=P)
    sd = mk.inp("l0f_sd").rearrange("(k p) f -> p k f", p=P)
    last = mlp(None, sg, su, sd, D, scale_col=None)
    mk.setw("xt", last)
    mk.setw("wB_free", last)
    return [last]


def pack_l0_ffn(inp, core):
    return {
        "l0f_gate": np.ascontiguousarray(inp["L0_ffn_gate_w"].T),
        "l0f_lb": np.ascontiguousarray(inp["L0_ffn_lb_bias"].reshape(1, 8)),
        "l0f_eg": np.ascontiguousarray(np.transpose(inp["L0_ffn_expert_gate_w"], (2, 0, 1))),
        "l0f_eu": np.ascontiguousarray(np.transpose(inp["L0_ffn_expert_up_w"], (2, 0, 1))),
        "l0f_ed": np.ascontiguousarray(np.transpose(inp["L0_ffn_expert_down_w"], (0, 2, 1))),
        "l0f_sg": np.ascontiguousarray(inp["L0_ffn_shared_gate_w"].T),
        "l0f_su": np.ascontiguousarray(inp["L0_ffn_shared_up_w"].T),
        "l0f_sd": np.ascontiguousarray(inp["L0_ffn_shared_down_w"].T),
    }


register("l0_ffn", emit_l0_ffn, pack_l0_ffn)


def emit_l1_ffn(mk, tag):
    b, ps = mk.b, mk.ps
    ln_and_transpose(mk)
    sAb = mk.sA.bitcast(BF16)
    wg = mk.wB
    t_wg = dma(b, "gp", wg[:, 0:256].rearrange("p (k e) -> p k e", k=NC),
               mk.inp("l1f_rw").rearrange("(k p) e -> p k e", p=P),
               sem="dwB", waits=mk.w("wB_free"))
    t_rb = dma(b, "gp", wg[0:1, 256:288], mk.inp("l1f_rb"), sem="dwB")
    oh_toks = []
    for i in range(NT):
        t0 = i * P
        kk = ps.take_f()
        mms = [lambda e, o, s, p_, k=k, t0=t0: e.matmul(
            o[:, 0:32], xln_lhsT(mk, k, t0), wg[:, k * 32:(k + 1) * 32],
            start=s, stop=p_) for k in range(NC)]
        mms.append(lambda e, o, s, p_: e.matmul(
            o[:, 0:32], mk.ones1[0:1, 0:P], wg[0:1, 256:288], start=s, stop=p_))
        tmm = mm_group(b, ps, kk, mms, extra_waits=[t_wg, t_rb] + mk.w("xlnT"))
        tp0 = b.emit("vector", lambda e: e.memset(mk.sA[:, 0:512], -1e30),
                     waits=mk.w("pad_free") + mk.w("sA_free"), inc="v")
        tp1 = b.emit("vector", lambda e, kt=ps.t(kk): e.tensor_copy(
            mk.sA[:, 0:32], kt[:, 0:32]), waits=ps.read_waits(kk) + [tp0], inc="v")
        tp2 = b.emit("vector", lambda e: e.max(mk.sm2[:, 0:8], mk.sA[:, 0:512]),
                     waits=[tp1], inc="v")
        toh = b.emit("vector", lambda e, kt=ps.t(kk), i=i: e.tensor_scalar(
            mk.xb[:, i * 32:(i + 1) * 32], kt[:, 0:32], mk.sm2[:, 0:1], None,
            ALU.is_ge), waits=[tp2] + mk.w("xb_free"), inc="v")
        ps.set_free(kk, toh)
        mk.setw("pad_free", toh)
        oh_toks.append(toh)
    mk.setw("wB_free", tmm)
    w1v = mk.inp("l1f_w1")
    w2v = mk.inp("l1f_w2")
    b1v = mk.inp("l1f_b1")
    b2v = mk.inp("l1f_b2")
    last = None
    for w in range(4):
        es_ = range(w * 8, (w + 1) * 8)
        tw1s = []
        for j, e_i in enumerate(es_):
            dst = (mk.wA if j < 4 else mk.wB)
            off = (j % 4) * 2048
            tw1s.append(dma(
                b, "gp", dst[:, off:off + 2048].rearrange("p (k f) -> p k f", k=NC),
                w1v[:, e_i, :].rearrange("(k p) f -> p k f", p=P),
                sem="dw1", waits=mk.w(f"w1slot{j}") + (mk.w("wB_free") if j == 4 else [])))
        tw2s = []
        for j, e_i in enumerate(es_):
            tw2s.append(dma(
                b, "gp", mk.gB[:, j * 2048:(j + 1) * 2048].rearrange(
                    "p (k f) -> p k f", k=2),
                w2v[e_i].rearrange("(k p) f -> p k f", p=P),
                sem="dw2", waits=mk.w(f"w2slot{j}")))
        tb1 = dma(b, "gp", sAb[0:1, 2048:4096],
                  b1v[:, w * 2048:(w + 1) * 2048], sem="db1",
                  waits=mk.w("sAb_b1"))
        tb2 = dma(b, "gp", mk.xb[0:8, 512:1536],
                  b2v[w * 8:(w + 1) * 8, :], sem="db2", waits=mk.w("xb2_free"))
        for i in range(NT):
            t0 = i * P
            tohf = b.emit("vector", lambda e, i=i: e.tensor_copy(
                mk.sm2[:, 64:96], mk.xb[:, i * 32:(i + 1) * 32]),
                waits=[oh_toks[i]] + mk.w("sm2oh"), inc="v")
            mk.setw("sm2oh", tohf)
            hm_toks = []
            for j, e_i in enumerate(es_):
                wsl = (mk.wA if j < 4 else mk.wB)
                off = (j % 4) * 2048
                hk = ps.take_f()
                mms = [lambda e, o, s, p_, k=k, wsl=wsl, off=off, t0=t0: e.matmul(
                    o[:, 0:256], xln_lhsT(mk, k, t0), wsl[:, off + k * 256:off + (k + 1) * 256],
                    start=s, stop=p_) for k in range(NC)]
                mms.append(lambda e, o, s, p_, j=j: e.matmul(
                    o[:, 0:256], mk.ones1[0:1, 0:P], sAb[0:1, 2048 + j * 256:2048 + (j + 1) * 256],
                    start=s, stop=p_))
                tmm = mm_group(b, ps, hk, mms,
                               extra_waits=[tw1s[j], tb1] + mk.w("xlnT"))
                ohcol = mk.sm2[:, 64 + e_i - w * 8 + w * 8:65 + e_i]
                ohcol = mk.sm2[:, 64 + e_i:65 + e_i]
                tev = b.emit("vector", lambda e, kt=ps.t(hk), ohcol=ohcol: e.tensor_scalar(
                    mk.tl[:, 0:256], kt[:, 0:256], ohcol, 0.0, ALU.mult, ALU.max),
                    waits=ps.read_waits(hk) + [tohf] + mk.w("tl_free"), inc="v")
                ps.set_free(hk, tev)
                tk = ps.take_bf()
                bank = ps.t(tk)
                war = ps.write_waits(tk)
                tt1 = b.emit("tensor", lambda e, bank=bank: e.transpose(
                    bank[:, 0:P], mk.tl[:, 0:P], mk.ident[:, :]),
                    waits=[tev] + war + mk.w("consts1"))
                tt2 = b.emit("tensor", lambda e, bank=bank: e.transpose(
                    bank[:, P:2 * P], mk.tl[:, P:2 * P], mk.ident[:, :]),
                    waits=[tev], inc="t")
                ps.set_ready(tk, tt2)
                mk.setw("tl_free", tt2)
                thm = b.emit("vector", lambda e, bank=bank, j=j: e.tensor_copy(
                    sAb[:, j * 256:(j + 1) * 256], bank[:, 0:256]),
                    waits=ps.read_waits(tk) + mk.w(f"sAbslot{j}"), inc="v")
                ps.set_free(tk, thm)
                hm_toks.append(thm)
            tk2 = ps.take_bf()
            bank2 = ps.t(tk2)
            tto = b.emit("tensor", lambda e, bank2=bank2, i=i, w=w: e.transpose(
                bank2[0:8, 0:P], mk.xb[:, i * 32 + w * 8:i * 32 + w * 8 + 8],
                mk.ident[:, :]),
                waits=[oh_toks[i]] + ps.write_waits(tk2) + mk.w("consts1"), inc="t")
            ps.set_ready(tk2, tto)
            toh2 = b.emit("vector", lambda e, bank2=bank2: e.tensor_copy(
                mk.tgb[0:8, 0:P], bank2[0:8, 0:P]),
                waits=ps.read_waits(tk2) + mk.w("tgb_free"), inc="v")
            ps.set_free(tk2, toh2)
            mk.setw("tgb_free", toh2)
            for cch in range(2):
                yk = ps.take_f()
                mms = []
                for j in range(8):
                    for k2 in range(2):
                        mms.append(lambda e, o, s, p_, j=j, k2=k2, cch=cch: e.matmul(
                            o[:, :], sAb[:, j * 256 + k2 * P:j * 256 + (k2 + 1) * P],
                            mk.gB[:, j * 2048 + k2 * D + cch * 512:j * 2048 + k2 * D + cch * 512 + 512],
                            start=s, stop=p_))
                mms.append(lambda e, o, s, p_, cch=cch: e.matmul(
                    o[:, :], mk.tgb[0:8, 0:P], mk.xb[0:8, 512 + cch * 512:1024 + cch * 512],
                    start=s, stop=p_))
                tmy = mm_group(b, ps, yk, mms,
                               extra_waits=hm_toks + [toh2, tb2] + tw2s)
                trl = b.emit("scalar", lambda e, kt=ps.t(yk), i=i: e.activation(
                    mk.tl[:, 512:1024], kt[:, :], AF.Relu),
                    waits=ps.read_waits(yk) + mk.w("tlr"), inc="s")
                xa = mk.xt[:, i * D + cch * 512:i * D + cch * 512 + 512]
                tev = b.emit("vector", lambda e, xa=xa: e.tensor_add(
                    xa, xa, mk.tl[:, 512:1024]),
                    waits=[trl] + mk.w(f"xt{i}"), inc="v")
                ps.set_free(yk, tev)
                mk.setw("tlr", tev)
                mk.setw(f"xt{i}", tev)
                last = tev
            for j in range(8):
                mk.setw(f"sAbslot{j}", tmy)
        for j in range(8):
            mk.setw(f"w1slot{j}", tmy)
            mk.setw(f"w2slot{j}", tmy)
        mk.setw("sAb_b1", tmy)
        mk.setw("xb2_free", tmy)
        mk.setw("wB_free", tmy)
    mk.setw("xt", last)
    mk.setw("sA_free", tmy)
    mk.setw("xb_free", last)
    mk.setw("gB_free", tmy)
    mk.setw("wA_free", tmy)
    return [last]


def pack_l1_ffn(inp, core):
    return {
        "l1f_rw": np.ascontiguousarray(inp["L1_ffn_router_w"].T),
        "l1f_rb": np.ascontiguousarray(inp["L1_ffn_router_b"].reshape(1, 32)),
        "l1f_w1": np.ascontiguousarray(np.transpose(inp["L1_ffn_weights1"], (1, 0, 2))),
        "l1f_b1": np.ascontiguousarray(inp["L1_ffn_bias1"].reshape(1, 32 * 256)),
        "l1f_w2": np.ascontiguousarray(inp["L1_ffn_weights2"]),
        "l1f_b2": np.ascontiguousarray(inp["L1_ffn_bias2"]),
    }


register("l1_ffn", emit_l1_ffn, pack_l1_ffn)


def get_offreg(mk, eng, col):
    b = mk.b
    key = (eng, col)
    if not hasattr(mk, "_regs"):
        mk._regs = {}
        mk._offs_tile = b.sb("offs", [1, 8], mybir.dt.int32)
        mk._offs_tok = dma(b, "gp", mk._offs_tile[:, :], mk.inp("c_offs"), sem="dC")
    if key not in getattr(mk, "_regs_planned", set()):
        if not hasattr(mk, "_regs_planned"):
            mk._regs_planned = set()
        mk._regs_planned.add(key)

        def fn(e, col=col):
            reg = e.alloc_register(f"off{eng}{col}")
            mk._regs[key] = reg
            return e.reg_load(reg, mk._offs_tile[0:1, col:col + 1])
        b.emit(eng, fn, waits=[mk._offs_tok])
    return lambda: mk._regs[key]


def gather_xln_full(mk, tag, src_cols=NC * TH):
    b, cc = mk.b, mk.cc
    cin = mk.dram_internal(f"gx_in_{tag}", (P, src_cols), BF16)
    cout = mk.dram_internal(f"gx_out_{tag}", (2 * P, src_cols), BF16)
    t1 = dma(b, "sync", cin.ap(), mk.gA[:, 0:src_cols], sem="dgx",
             waits=mk.w("xlnT"))
    t2 = cc.allgather_pair(cin, cout, t1)
    srcv = cout.ap().rearrange("(r p) (k t) -> p k r t", r=2, k=NC)
    dstv = mk.gA.rearrange("p (k rt) -> p k rt", k=NC).rearrange(
        "p k (r t) -> p k r t", r=2)
    t3 = dma(b, "sync", dstv, srcv, sem="dgxb", waits=[t2])
    mk.setw("xlnT_full", t3)
    return t3


def exchange_own_cols(mk, tag, sbuf_src, n_pair_blocks, blk_cols, dst_ap_fn,
                      src_tok, off_col=0):
    b, cc = mk.b, mk.cc
    cols = n_pair_blocks * blk_cols
    cin = mk.dram_internal(f"ex_in_{tag}", (P, cols), BF16)
    cout = mk.dram_internal(f"ex_out_{tag}", (2 * P, cols), BF16)
    t1 = dma(b, "sync", cin.ap(), sbuf_src, sem="dex", waits=[src_tok])
    t2 = cc.allgather_pair(cin, cout, t1)
    half = blk_cols // 2
    getreg = get_offreg(mk, "sync", off_col)
    dst = dst_ap_fn()

    def fn(e):
        src = bass.AP(cout.ap().tensor, mk._regs[("sync", off_col)],
                      [[cols, P], [P * cols, 2], [blk_cols, n_pair_blocks],
                       [1, half]])
        return e.dma_start(out=dst, in_=src)
    tok = b.emit("sync", fn, waits=[t2], inc="dexb", inc_n=16)
    return tok


def _flash_causal(mk, tag, pre, has_bias, mask_mode="causal"):
    b, ps, cc = mk.b, mk.ps, mk.cc
    ln_and_transpose(mk)
    gather_xln_full(mk, tag)
    gAv = mk.gA
    t_wq = load_w_kf(mk, pre["wq"], mk.wA, NC, 0, 512, 512, sem="dwA",
                     waits=mk.w("wA_free"))
    t_wk = load_w_kf(mk, pre["wk"], mk.wB, NC, 0, 512, 512, sem="dwB",
                     waits=mk.w("wB_free"))
    t_wv = dma(b, "gp", mk.wB[:, 4096:8192].rearrange("p (k f) -> p k f", k=NC),
               mk.inp(pre["wv"]).rearrange("(k p) f -> p k f", p=P),
               sem="dwB2", waits=mk.w("wB_free"))
    if has_bias:
        t_bias = dma(b, "gp", mk.sA.rearrange("p (h d f) -> p h d f", h=8, d=2),
                     mk.inp(pre["bias"]).rearrange("h d (p) f -> p h d f", p=P),
                     sem="dsA", waits=mk.w("sA_free"))
        t_c31 = dma(b, "gp", mk.sm2[0:1, 100:108], mk.inp(pre["c31"]), sem="dc31")
        ck = ps.take_f()
        tc31 = mm_group(b, ps, ck, [lambda e, o, s, p_: e.matmul(
            o[:, 0:8], mk.ones1f[0:1, :], mk.sm2[0:1, 100:108], start=s, stop=p_)],
            extra_waits=[t_c31] + mk.w("consts3"))
        tc31b = b.emit("vector", lambda e, kt=ps.t(ck): e.tensor_copy(
            mk.sm2[:, 108:116], kt[:, 0:8]), waits=ps.read_waits(ck), inc="v")
        ps.set_free(ck, tc31b)
    else:
        t_bias = None
        tc31b = None
    if mask_mode == "sparse":
        sAb_m = mk.sA.bitcast(BF16)
        t_m = dma(b, "gp", sAb_m.rearrange("p (kk f) -> p kk f", kk=32),
                  mk.inp(pre["mask"]).rearrange("kk (p) f -> p kk f", p=P),
                  sem="dsA", waits=mk.w("sA_free"))
    else:
        t_m = dma(b, "gp", mk.tl[:, 0:P], mk.inp("c_tril"), sem="dC2",
                  waits=mk.w("tl_free") + mk.w("tlr"))

    QT0, KT0, V0, PT0, ST0 = 0, 4096, 8192, 10272, 10784
    cc_in = mk.dram_internal(f"attx_in_{tag}", (P, 4 * S), BF16)
    strip_toks = []
    for h in range(8):
        tq = None
        for chk in range(8):
            qk = ps.take_f()
            mms = [lambda e, o, s, p_, k=k, h=h, chk=chk: e.matmul(
                o[0:64, :], mk.wA[:, k * 512 + h * 64:k * 512 + (h + 1) * 64],
                gAv[:, k * S + chk * 512:k * S + (chk + 1) * 512], start=s, stop=p_)
                for k in range(NC)]
            tmm = mm_group(b, ps, qk, mms,
                           extra_waits=[t_wq] + mk.w("xlnT_full"))
            tq = b.emit("vector", lambda e, kt=ps.t(qk), chk=chk: e.tensor_copy(
                mk.gB[0:64, QT0 + chk * 512:QT0 + (chk + 1) * 512], kt[0:64, :]),
                waits=ps.read_waits(qk) + mk.w("gBq"), inc="v")
            ps.set_free(qk, tq)
            kk2 = ps.take_f()
            mms = [lambda e, o, s, p_, k=k, h=h, chk=chk: e.matmul(
                o[0:64, :], mk.wB[:, k * 512 + h * 64:k * 512 + (h + 1) * 64],
                gAv[:, k * S + chk * 512:k * S + (chk + 1) * 512], start=s, stop=p_)
                for k in range(NC)]
            tmk_ = mm_group(b, ps, kk2, mms,
                            extra_waits=[t_wk] + mk.w("xlnT_full"))
            tkv = b.emit("vector", lambda e, kt=ps.t(kk2), chk=chk: e.tensor_copy(
                mk.gB[0:64, KT0 + chk * 512:KT0 + (chk + 1) * 512], kt[0:64, :]),
                waits=ps.read_waits(kk2) + mk.w("gBq"), inc="v")
            ps.set_free(kk2, tkv)
        tvm = b.emit("vector", lambda e: e.memset(
            mk.gB[:, V0:V0 + 32 * 65], 1.0), waits=mk.w("gBv"), inc="v")
        tv = None
        for it in range(32):
            vk = ps.take_f()
            mms = [lambda e, o, s, p_, k=k, h=h, it=it: e.matmul(
                o[:, 0:64], gAv[:, k * S + it * P:k * S + (it + 1) * P],
                mk.wB[:, 4096 + k * 512 + h * 64:4096 + k * 512 + (h + 1) * 64],
                start=s, stop=p_) for k in range(NC)]
            tmv = mm_group(b, ps, vk, mms,
                           extra_waits=[t_wv] + mk.w("xlnT_full"))
            tv = b.emit("vector", lambda e, kt=ps.t(vk), it=it: e.tensor_copy(
                mk.gB[:, V0 + it * 65:V0 + it * 65 + 64], kt[:, 0:64]),
                waits=ps.read_waits(vk) + [tvm], inc="v")
            ps.set_free(vk, tv)
        okeys = [ps.reserve_f() for _ in range(4)]
        for ichunk in range(8):
            o_war = []
            for a in range(4):
                o_war += ps.write_waits(okeys[a])
            ostart = [True] * 4
            otok = [None] * 4
            for jt in range(4 * ichunk + 4):
                sk = ps.take_f()
                tsm = mm_group(b, ps, sk, [lambda e, o, s, p_, jt=jt, ichunk=ichunk:
                    e.matmul(o[:, :], mk.gB[0:64, KT0 + jt * P:KT0 + (jt + 1) * P],
                             mk.gB[0:64, QT0 + ichunk * 512:QT0 + (ichunk + 1) * 512],
                             start=s, stop=p_)],
                    extra_waits=[tq, tkv] if jt == 0 else [])
                pt = ps.t(sk)
                ptile = mk.gB[:, PT0:PT0 + 512]
                etoks = []
                for a in range(4):
                    it_g = ichunk * 4 + a
                    if it_g < jt:
                        continue
                    K_off = it_g - jt
                    sl = slice(a * P, (a + 1) * P)
                    if has_bias and K_off <= 1:
                        bt = mk.sA[:, (h * 2 + K_off) * P:(h * 2 + K_off + 1) * P]
                        ta_ = b.emit("vector", lambda e, pt=pt, sl=sl, bt=bt:
                                     e.scalar_tensor_tensor(pt[:, sl], pt[:, sl],
                                                            0.125, bt, ALU.mult,
                                                            ALU.add),
                                     waits=ps.read_waits(sk) + [t_bias], inc="v")
                        te_ = b.emit("scalar", lambda e, pt=pt, sl=sl, ptile=ptile:
                                     e.activation(ptile[:, sl], pt[:, sl], AF.Exp),
                                     waits=[ta_] + mk.w("ptile_free"), inc="s")
                        if K_off == 0:
                            te_ = b.emit("vector", lambda e, sl=sl, ptile=ptile:
                                         e.tensor_mul(ptile[:, sl], ptile[:, sl],
                                                      mk.tl[:, 0:P]),
                                         waits=[te_, t_m], inc="v")
                    elif has_bias:
                        c31h = mk.sm2[:, 108 + h:109 + h]
                        te_ = b.emit("scalar", lambda e, pt=pt, sl=sl, ptile=ptile,
                                     c31h=c31h:
                                     e.activation(ptile[:, sl], pt[:, sl], AF.Exp,
                                                  scale=0.125, bias=c31h),
                                     waits=ps.read_waits(sk) + [tc31b] + mk.w("ptile_free"),
                                     inc="s")
                    else:
                        te_ = b.emit("scalar", lambda e, pt=pt, sl=sl, ptile=ptile:
                                     e.activation(ptile[:, sl], pt[:, sl], AF.Exp,
                                                  scale=0.125),
                                     waits=ps.read_waits(sk) + mk.w("ptile_free"),
                                     inc="s")
                        if mask_mode == "sparse":
                            mtile = mk.sA.bitcast(BF16)[:, K_off * P:(K_off + 1) * P]
                            te_ = b.emit("vector", lambda e, sl=sl, ptile=ptile,
                                         mtile=mtile:
                                         e.tensor_mul(ptile[:, sl], ptile[:, sl], mtile),
                                         waits=[te_, t_m], inc="v")
                        elif K_off == 0:
                            te_ = b.emit("vector", lambda e, sl=sl, ptile=ptile:
                                         e.tensor_mul(ptile[:, sl], ptile[:, sl],
                                                      mk.tl[:, 0:P]),
                                         waits=[te_, t_m], inc="v")
                    etoks.append((a, te_))
                ps.set_free(sk, etoks[-1][1])
                for idx_, (a, te_) in enumerate(etoks):
                    it_g = ichunk * 4 + a
                    fn = lambda e, a=a, jt=jt, s=ostart[a], ptile=ptile, okey=okeys[a]: \
                        e.matmul(ps.t(okey)[:, 0:65], ptile[:, a * P:(a + 1) * P],
                                 mk.gB[:, V0 + jt * 65:V0 + (jt + 1) * 65],
                                 start=s, stop=(jt == a + (jt // 4) * 0 + (okeys[a] and 0) or (jt == it_g)))
                    fn = lambda e, a=a, jt=jt, s=ostart[a], ptile=ptile, okey=okeys[a], it_g=it_g: \
                        e.matmul(ps.t(okey)[:, 0:65], ptile[:, a * P:(a + 1) * P],
                                 mk.gB[:, V0 + jt * 65:V0 + (jt + 1) * 65],
                                 start=s, stop=(jt == it_g))
                    w_ = [te_] + (o_war if ostart[a] else [])
                    need_inc = (jt == it_g) or (idx_ == len(etoks) - 1)
                    if need_inc:
                        tokm = b.emit("tensor", fn, waits=w_, inc="t")
                        if jt == it_g:
                            otok[a] = tokm
                        if idx_ == len(etoks) - 1:
                            mk.setw("ptile_free", tokm)
                    else:
                        b.emit("tensor", fn, waits=w_)
                    ostart[a] = False
            for a in range(4):
                ps.set_ready(okeys[a], otok[a])
                it_g = ichunk * 4 + a
                tr1 = b.emit("vector", lambda e, okey=okeys[a], a=a: e.reciprocal(
                    mk.sm2[:, 116 + a:117 + a], ps.t(okey)[:, 64:65]),
                    waits=[otok[a]], inc="v")
                dst = mk.gB[:, ST0 + it_g * P + (h % 2) * 64:
                            ST0 + it_g * P + (h % 2) * 64 + 64]
                tr2 = b.emit("vector", lambda e, okey=okeys[a], dst=dst, a=a: e.tensor_scalar(
                    dst, ps.t(okey)[:, 0:64], mk.sm2[:, 116 + a:117 + a], None, ALU.mult),
                    waits=[tr1] + mk.w("strip_war"), inc="v")
                ps.set_free(okeys[a], tr2)
                strip_toks.append(tr2)
        for a in range(4):
            ps.release_f(okeys[a])
        mk.setw("gBq", tr2)
        mk.setw("gBv", tr2)
        if h % 2 == 1:
            pairi = h // 2
            dtoks = []
            for it_g in range(32):
                tk = ps.take_bf()
                bank = ps.t(tk)
                tt = b.emit("tensor", lambda e, bank=bank, it_g=it_g: e.transpose(
                    bank[:, 0:P], mk.gB[:, ST0 + it_g * P:ST0 + (it_g + 1) * P],
                    mk.ident[:, :]),
                    waits=strip_toks[-1:] + ps.write_waits(tk) + mk.w("consts1"),
                    inc="t")
                ps.set_ready(tk, tt)
                tcp = b.emit("vector", lambda e, bank=bank, it_g=it_g: e.tensor_copy(
                    mk.gB[:, ST0 + it_g * P:ST0 + (it_g + 1) * P], bank[:, 0:P]),
                    waits=ps.read_waits(tk), inc="v")
                ps.set_free(tk, tcp)
                dtoks.append(tcp)
            tdma = dma(b, "sync", cc_in.ap().rearrange("p (a t) -> p a t", a=4)[:, pairi],
                       mk.gB[:, ST0:ST0 + S], sem="dccin", waits=dtoks)
            mk.setw("strip_war", tdma)
            mk.setw("ccin_done", tdma)
    cc_out = mk.dram_internal(f"attx_out_{tag}", (2 * P, 4 * S), BF16)
    t2 = cc.allgather_pair(cc_in, cc_out, mk.tok["ccin_done"])
    get_offreg(mk, "sync", 0)
    get_offreg(mk, "sync", 1)
    tback = None
    for r in range(2):
        dstv = mk.gA[:, r * 4 * TH:(r + 1) * 4 * TH].rearrange(
            "p (a t) -> p a t", a=4)

        def fnback(e, r=r, dstv=dstv, cc_out=cc_out):
            src = bass.AP(cc_out.ap().tensor, mk._regs[("sync", r)],
                          [[4 * S, P], [S, 4], [1, TH]])
            return e.dma_start(out=dstv, in_=src)
        tback = b.emit("sync", fnback, waits=[t2], inc="dab", inc_n=16)
    t_wo = load_w_kf(mk, pre["wo"], mk.wA, NC, 0, D, D, sem="dwA",
                     waits=[tq, tkv, tv])
    if "bo" in pre:
        t_bo = dma(b, "gp", mk.wB[0:1, 0:D], mk.inp(pre["bo"]), sem="dwB3",
                   waits=[tmk_])
    else:
        t_bo = b.emit("vector", lambda e: e.memset(mk.wB[0:1, 0:D], 0.0),
                      waits=[tmk_], inc="v")
    last = None
    for i in range(NT):
        for cch in range(2):
            yk = ps.take_f()
            mms = [lambda e, o, s, p_, k=k, i=i, cch=cch: e.matmul(
                o[:, :], mk.gA[:, k * TH + i * P:k * TH + (i + 1) * P],
                mk.wA[:, k * D + cch * 512:k * D + (cch + 1) * 512],
                start=s, stop=p_) for k in range(NC)]
            mms.append(lambda e, o, s, p_, cch=cch: e.matmul(
                o[:, :], mk.ones1[0:1, 0:P], mk.wB[0:1, cch * 512:(cch + 1) * 512],
                start=s, stop=p_))
            tmy = mm_group(b, ps, yk, mms, extra_waits=[tback, t_wo, t_bo])
            xa = mk.xt[:, i * D + cch * 512:i * D + cch * 512 + 512]
            tev = b.emit("vector", lambda e, xa=xa, yt=ps.t(yk): e.tensor_add(
                xa, xa, yt[:, :]), waits=ps.read_waits(yk) + mk.w(f"xt{i}"), inc="v")
            ps.set_free(yk, tev)
            mk.setw(f"xt{i}", tev)
            last = tev
    mk.setw("xt", last)
    mk.setw("wA_free", tmy)
    mk.setw("wB_free", tmy)
    mk.setw("gB_free", last)
    mk.setw("sA_free", last)
    mk.setw("xlnT", None)
    return [last]


def _t5_bucket(rel):
    n = np.maximum(-rel, 0)
    max_exact = 16
    is_small = n < max_exact
    val_large = max_exact + (np.log(np.maximum(n, 1) / max_exact)
                             / np.log(128 / max_exact) * 16).astype(np.int32)
    val_large = np.minimum(val_large, 31)
    return np.where(is_small, n, val_large)


def pack_l0_attn_b(inp, core):
    hb = (core % 2) * 8
    wq = inp["L0_attn_b_Wq"][:, hb * 64:(hb + 8) * 64]
    wk = inp["L0_attn_b_Wk"][:, hb * 64:(hb + 8) * 64]
    wv = inp["L0_attn_b_Wv"][:, hb * 64:(hb + 8) * 64]
    rel = inp["L0_attn_b_rel_emb"]
    r = np.arange(P)
    bias = np.zeros((8, 2, P, P), np.float32)
    for d in range(2):
        K = d * P
        relpos = (r[:, None]) - (K + r[None, :])
        bkt = _t5_bucket(relpos)
        for h in range(8):
            bias[h, d] = rel[bkt, hb + h]
    c31 = rel[31, hb:hb + 8].reshape(1, 8)
    return {
        "l0ab_wq": np.ascontiguousarray(wq),
        "l0ab_wk": np.ascontiguousarray(wk),
        "l0ab_wv": np.ascontiguousarray(wv),
        "l0ab_wo": np.ascontiguousarray(inp["L0_attn_b_Wo"]),
        "l0ab_bo": np.ascontiguousarray(inp["L0_attn_b_bo"].reshape(1, D)),
        "l0ab_bias": np.ascontiguousarray(bias),
        "l0ab_c31": np.ascontiguousarray(c31),
        "c_tril": np.triu(np.ones((P, P))).astype(BF),
        "c_offs": np.array([[(core % 2) * TH, (core % 2) * TH + P * 4 * S, (core % 2) * TH + P * 8 * S] + [0] * 5],
                           np.int32),
    }





def emit_l0_attn_b(mk, tag):
    return _flash_causal(mk, tag, {"wq": "l0ab_wq", "wk": "l0ab_wk",
                                   "wv": "l0ab_wv", "wo": "l0ab_wo",
                                   "bo": "l0ab_bo", "bias": "l0ab_bias",
                                   "c31": "l0ab_c31"}, has_bias=True)


def emit_l1_attn_a(mk, tag):
    return _flash_causal(mk, tag, {"wq": "l1aa_wq", "wk": "l1aa_wk",
                                   "wv": "l1aa_wv", "wo": "l1aa_wo"},
                         has_bias=False)


def pack_l1_attn_a(inp, core):
    hb = (core % 2) * 8
    def conv(w):
        return np.ascontiguousarray(
            np.transpose(w[hb:hb + 8], (2, 0, 1)).reshape(D, 512))
    return {
        "l1aa_wq": conv(inp["L1_attn_a_w_q"]),
        "l1aa_wk": conv(inp["L1_attn_a_w_k"]),
        "l1aa_wv": conv(inp["L1_attn_a_w_v"]),
        "l1aa_wo": np.ascontiguousarray(inp["L1_attn_a_w_o"]),
        "c_tril": np.triu(np.ones((P, P))).astype(BF),
        "c_offs": np.array([[(core % 2) * TH, (core % 2) * TH + P * 4 * S, (core % 2) * TH + P * 8 * S] + [0] * 5],
                           np.int32),
    }


register("l1_attn_a", emit_l1_attn_a, pack_l1_attn_a)

register("l0_attn_b", emit_l0_attn_b, pack_l0_attn_b)


def emit_l3_attn_a(mk, tag):
    return _flash_causal(mk, tag, {"wq": "l3aa_wq", "wk": "l3aa_wk",
                                   "wv": "l3aa_wv", "wo": "l3aa_wo",
                                   "mask": "l3aa_mask"},
                         has_bias=False, mask_mode="sparse")


def _l3aa_mask():
    r = np.arange(P)
    masks = np.zeros((32, P, P), np.float32)
    for K in range(32):
        i = K * P + r[None, :]
        j = r[:, None]
        window = (j >= i - 127) & (j <= i)
        dil = (j <= i - 132) & ((i - 128 - j) % 4 == 0)
        masks[K] = (window | dil).astype(np.float32)
    return masks.astype(BF)


def pack_l3_attn_a(inp, core):
    hb = (core % 2) * 8
    qkv = inp["L3_attn_a_qkv"]
    def part(idx):
        rows = qkv[idx * D + hb * 64:idx * D + (hb + 8) * 64]
        return np.ascontiguousarray(rows.T)
    return {
        "l3aa_wq": part(0),
        "l3aa_wk": part(1),
        "l3aa_wv": part(2),
        "l3aa_wo": np.ascontiguousarray(inp["L3_attn_a_wo"].T),
        "l3aa_mask": _l3aa_mask(),
        "c_tril": np.triu(np.ones((P, P))).astype(BF),
        "c_offs": np.array([[(core % 2) * TH, (core % 2) * TH + P * 4 * S, (core % 2) * TH + P * 8 * S] + [0] * 5],
                           np.int32),
    }


register("l3_attn_a", emit_l3_attn_a, pack_l3_attn_a)


def emit_l0_attn_a(mk, tag):
    b, ps, cc = mk.b, mk.ps, mk.cc
    ln_and_transpose(mk)
    gather_xln_full(mk, tag)
    gAv = mk.gA
    t_wq = load_w_kf(mk, "l0aa_wq", mk.wA, NC, 0, 512, 512, sem="dwA",
                     waits=mk.w("wA_free"))
    t_wk = load_w_kf(mk, "l0aa_wk", mk.wB, NC, 0, 512, 512, sem="dwB",
                     waits=mk.w("wB_free"))
    t_wv = dma(b, "gp", mk.wB[:, 4096:8192].rearrange("p (k f) -> p k f", k=NC),
               mk.inp("l0aa_wv").rearrange("(k p) f -> p k f", p=P),
               sem="dwB2", waits=mk.w("wB_free"))
    sAb = mk.sA.bitcast(BF16)
    t_wo2 = dma(b, "gp", sAb[:, 2048:2048 + 512].rearrange("p (a f) -> p a f", a=4),
                mk.inp("l0aa_wop").rearrange("a p f -> p a f"),
                sem="dsA2", waits=mk.w("sA_free"))
    t_dist = dma(b, "gp", mk.sA[:, 0:384].rearrange("p (kk f) -> p kk f", kk=3),
                 mk.inp("l0aa_dist").rearrange("kk p f -> p kk f"),
                 sem="dsA", waits=mk.w("sA_free"))
    t_bqkv = dma(b, "gp", mk.xb[0:1, 0:1536].rearrange("p (z f) -> p z f", z=3),
                 mk.inp("l0aa_bqkv"), sem="dxb", waits=mk.w("xb_free"))
    t_alpha = dma(b, "gp", mk.sm2[0:1, 120:121], mk.inp("l0aa_alpha"), sem="dal")
    t_sel = dma(b, "gp", mk.sm2[0:1, 121:123], mk.inp("l0aa_sel"), sem="dal")
    gv = mk.gA.rearrange("p (k t) -> p k t", k=NC)
    f1 = b.emit("vector", lambda e: e.tensor_sub(
        mk.sm2[:, 124:132].rearrange("p (k o) -> p k o", k=NC),
        gv[:, :, S - 1:S], gv[:, :, S - 2:S - 1]),
        waits=mk.w("xlnT_full") + join_all(mk), inc="v")
    f2 = b.emit("scalar", lambda e: e.activation(mk.sm2[:, 124:132], mk.sm2[:, 124:132],
                                                 AF.Abs), waits=[f1], inc="s")
    f3 = b.emit("vector", lambda e: e.reduce_sum(mk.sm2[:, 132:133],
                                                 mk.sm2[:, 124:132], axis=AX),
                waits=[f2], inc="v")
    fk = ps.take_f()
    tf1 = mm_group(b, ps, fk, [lambda e, o, s, p_: e.matmul(
        o[0:1, 0:1], mk.sm2[:, 132:133], cf(mk, 1.0), start=s, stop=p_)],
        extra_waits=[f3] + mk.w("consts3"))
    g1 = b.emit("vector", lambda e, kt=ps.t(fk): e.tensor_mul(
        mk.sm2[0:1, 134:135], kt[0:1, 0:1], mk.sm2[0:1, 121:122]),
        waits=ps.read_waits(fk) + [t_sel], inc="v")
    ps.set_free(fk, g1)
    g2 = b.emit("vector", lambda e: e.tensor_mul(
        mk.sm2[0:1, 135:136], mk.sm2[0:1, 124:125], mk.sm2[0:1, 122:123]),
        waits=[g1, f2], inc="v")
    red_in = mk.dram_internal(f"aa_red_in", (1, 2), F32)
    red_out = mk.dram_internal(f"aa_red_out", (1, 2), F32)
    g3 = dma(b, "gp", red_in.ap(), mk.sm2[0:1, 134:136], sem="dRaa", waits=[g2])
    g4 = cc.allreduce8(red_in, red_out, g3)
    g5 = dma(b, "gp", mk.sm2[0:1, 136:138], red_out.ap(), sem="dRaa2", waits=[g4])
    g6 = b.emit("scalar", lambda e: e.activation(mk.sm2[0:1, 138:139],
                                                 mk.sm2[0:1, 120:121], AF.Sigmoid),
                waits=[t_alpha, g5], inc="s")
    g7 = b.emit("vector", lambda e: e.tensor_scalar_mul(mk.sm2[0:1, 139:140],
                                                        mk.sm2[0:1, 136:137],
                                                        1.0 / 4096.0),
                waits=[g6], inc="v")
    g8 = b.emit("vector", lambda e: e.tensor_mul(mk.sm2[0:1, 139:140],
                                                 mk.sm2[0:1, 139:140],
                                                 mk.sm2[0:1, 138:139]),
                waits=[g7], inc="v")
    g9 = b.emit("vector", lambda e: e.tensor_single_scalar(
        mk.sm2[0:1, 140:141], mk.sm2[0:1, 138:139], 1.0, ALU.subtract),
        waits=[g8], inc="v")
    g9b = b.emit("vector", lambda e: e.tensor_scalar_mul(mk.sm2[0:1, 140:141],
                                                         mk.sm2[0:1, 140:141], -1.0),
                 waits=[g9], inc="v")
    g10 = b.emit("vector", lambda e: e.tensor_mul(mk.sm2[0:1, 140:141],
                                                  mk.sm2[0:1, 140:141],
                                                  mk.sm2[0:1, 137:138]),
                 waits=[g9b], inc="v")
    g11 = b.emit("vector", lambda e: e.tensor_add(mk.sm2[0:1, 141:142],
                                                  mk.sm2[0:1, 139:140],
                                                  mk.sm2[0:1, 140:141]),
                 waits=[g10], inc="v")
    g12 = b.emit("scalar", lambda e: e.activation(mk.sm2[0:1, 142:143],
                                                  mk.sm2[0:1, 141:142], AF.Sigmoid),
                 waits=[g11], inc="s")
    g13 = b.emit("vector", lambda e: e.tensor_scalar(mk.sm2[0:1, 143:144],
                                                     mk.sm2[0:1, 142:143], 48.0, 12.0,
                                                     ALU.mult, ALU.add),
                 waits=[g12], inc="v")
    hk_ = ps.take_f()
    th1 = mm_group(b, ps, hk_, [lambda e, o, s, p_: e.matmul(
        o[:, 0:1], mk.ones1f[0:1, :], mk.sm2[0:1, 143:144], start=s, stop=p_)],
        extra_waits=[g13] + mk.w("consts3"))
    th2 = b.emit("vector", lambda e, kt=ps.t(hk_): e.tensor_copy(
        mk.sm2[:, 144:145], kt[:, 0:1]), waits=ps.read_waits(hk_), inc="v")
    ps.set_free(hk_, th2)
    tm = []
    for kk in range(3):
        tm.append(b.emit("vector", lambda e, kk=kk: e.tensor_scalar(
            mk.tl[:, kk * P:(kk + 1) * P], mk.sA[:, kk * P:(kk + 1) * P],
            mk.sm2[:, 144:145], None, ALU.is_le),
            waits=[th2, t_dist] + mk.w("tl_free") + mk.w("tlr"), inc="v"))
    t_mask = tm[-1]
    t_o512 = b.emit("vector", lambda e: e.memset(mk.tl[0:1, 384:896], 1.0),
                    waits=mk.w("tl_free") + mk.w("tlr"), inc="v")

    QT0, KT0, V0, PT0, ST0 = 0, 4096, 8192, 10272, 10784
    cc_in = mk.dram_internal(f"attx_in_{tag}", (P, 4 * S), BF16)
    strip_toks = [None]
    for h in range(8):
        tq = tkv = None
        for chk in range(8):
            qk = ps.take_f()
            mms = [lambda e, o, s, p_, k=k, h=h, chk=chk: e.matmul(
                o[0:64, :], mk.wA[:, k * 512 + h * 64:k * 512 + (h + 1) * 64],
                gAv[:, k * S + chk * 512:k * S + (chk + 1) * 512], start=s, stop=p_)
                for k in range(NC)]
            mms.append(lambda e, o, s, p_, h=h: e.matmul(
                o[0:64, :], mk.xb[0:1, h * 64:(h + 1) * 64], mk.tl[0:1, 384:896],
                start=s, stop=p_))
            tmm = mm_group(b, ps, qk, mms,
                           extra_waits=[t_wq, t_bqkv, t_o512] + mk.w("xlnT_full"))
            tq = b.emit("vector", lambda e, kt=ps.t(qk), chk=chk: e.tensor_copy(
                mk.gB[0:64, QT0 + chk * 512:QT0 + (chk + 1) * 512], kt[0:64, :]),
                waits=ps.read_waits(qk) + mk.w("gBq"), inc="v")
            ps.set_free(qk, tq)
            kk2 = ps.take_f()
            mms = [lambda e, o, s, p_, k=k, h=h, chk=chk: e.matmul(
                o[0:64, :], mk.wB[:, k * 512 + h * 64:k * 512 + (h + 1) * 64],
                gAv[:, k * S + chk * 512:k * S + (chk + 1) * 512], start=s, stop=p_)
                for k in range(NC)]
            mms.append(lambda e, o, s, p_, h=h: e.matmul(
                o[0:64, :], mk.xb[0:1, 512 + h * 64:512 + (h + 1) * 64],
                mk.tl[0:1, 384:896], start=s, stop=p_))
            tmk_ = mm_group(b, ps, kk2, mms, extra_waits=[t_wk, t_bqkv] + mk.w("xlnT_full"))
            tkv = b.emit("vector", lambda e, kt=ps.t(kk2), chk=chk: e.tensor_copy(
                mk.gB[0:64, KT0 + chk * 512:KT0 + (chk + 1) * 512], kt[0:64, :]),
                waits=ps.read_waits(kk2) + mk.w("gBq"), inc="v")
            ps.set_free(kk2, tkv)
        tvm = b.emit("vector", lambda e: e.memset(mk.gB[:, V0:V0 + 32 * 65], 1.0),
                     waits=mk.w("gBv"), inc="v")
        tv = None
        for it in range(32):
            vk = ps.take_f()
            mms = [lambda e, o, s, p_, k=k, h=h, it=it: e.matmul(
                o[:, 0:64], gAv[:, k * S + it * P:k * S + (it + 1) * P],
                mk.wB[:, 4096 + k * 512 + h * 64:4096 + k * 512 + (h + 1) * 64],
                start=s, stop=p_) for k in range(NC)]
            mms.append(lambda e, o, s, p_, h=h: e.matmul(
                o[:, 0:64], mk.ones1[0:1, 0:P], mk.xb[0:1, 1024 + h * 64:1024 + (h + 1) * 64],
                start=s, stop=p_))
            tmv = mm_group(b, ps, vk, mms, extra_waits=[t_wv, t_bqkv] + mk.w("xlnT_full"))
            tv = b.emit("vector", lambda e, kt=ps.t(vk), it=it: e.tensor_copy(
                mk.gB[:, V0 + it * 65:V0 + it * 65 + 64], kt[:, 0:64]),
                waits=ps.read_waits(vk) + [tvm], inc="v")
            ps.set_free(vk, tv)
        for it in range(32):
            jts = [j for j in (it - 1, it, it + 1) if 0 <= j < 32]
            ok = ps.take_f()
            o_war = ps.write_waits(ok)
            otok = None
            for idx_, jt in enumerate(jts):
                sk = ps.take_f()
                tsm = mm_group(b, ps, sk, [lambda e, o, s, p_, jt=jt, it=it:
                    e.matmul(o[:, 0:P], mk.gB[0:64, KT0 + jt * P:KT0 + (jt + 1) * P],
                             mk.gB[0:64, QT0 + it * P:QT0 + (it + 1) * P],
                             start=s, stop=p_)],
                    extra_waits=([tq, tkv] if it == 0 and idx_ == 0 else []))
                ptile = mk.gB[:, PT0:PT0 + P]
                te_ = b.emit("scalar", lambda e, pt=ps.t(sk), ptile=ptile:
                             e.activation(ptile[:, :], pt[:, 0:P], AF.Exp, scale=0.125),
                             waits=ps.read_waits(sk) + mk.w("ptile_free"), inc="s")
                Kr = it - jt + 1
                te2 = b.emit("vector", lambda e, ptile=ptile, Kr=Kr:
                             e.tensor_mul(ptile[:, :], ptile[:, :],
                                          mk.tl[:, Kr * P:(Kr + 1) * P]),
                             waits=[te_, t_mask], inc="v")
                ps.set_free(sk, te2)
                fn = lambda e, jt=jt, s=(idx_ == 0), ptile=ptile, ok=ok, last=(idx_ == len(jts) - 1): \
                    e.matmul(ps.t(ok)[:, 0:65], ptile[:, 0:P],
                             mk.gB[:, V0 + jt * 65:V0 + (jt + 1) * 65],
                             start=s, stop=last)
                w_ = [te2] + (o_war if idx_ == 0 else [])
                otok = b.emit("tensor", fn, waits=w_, inc="t")
                mk.setw("ptile_free", otok)
            ps.set_ready(ok, otok)
            tr1 = b.emit("vector", lambda e, ok=ok: e.reciprocal(
                mk.sm2[:, 146:147], ps.t(ok)[:, 64:65]), waits=[otok], inc="v")
            dst = mk.gB[:, ST0 + it * P + (h % 2) * 64:ST0 + it * P + (h % 2) * 64 + 64]
            tr2 = b.emit("vector", lambda e, ok=ok, dst=dst: e.tensor_scalar(
                dst, ps.t(ok)[:, 0:64], mk.sm2[:, 146:147], None, ALU.mult),
                waits=[tr1] + mk.w("strip_war"), inc="v")
            ps.set_free(ok, tr2)
            strip_toks.append(tr2)
        mk.setw("gBq", tr2)
        mk.setw("gBv", tr2)
        if h % 2 == 1:
            pairi = h // 2
            dtoks = []
            for it_g in range(32):
                tk = ps.take_bf()
                bank = ps.t(tk)
                tt = b.emit("tensor", lambda e, bank=bank, it_g=it_g: e.transpose(
                    bank[:, 0:P], mk.gB[:, ST0 + it_g * P:ST0 + (it_g + 1) * P],
                    mk.ident[:, :]),
                    waits=strip_toks[-1:] + ps.write_waits(tk) + mk.w("consts1"),
                    inc="t")
                ps.set_ready(tk, tt)
                tcp = b.emit("vector", lambda e, bank=bank, it_g=it_g: e.tensor_copy(
                    mk.gB[:, ST0 + it_g * P:ST0 + (it_g + 1) * P], bank[:, 0:P]),
                    waits=ps.read_waits(tk), inc="v")
                ps.set_free(tk, tcp)
                wk_ = ps.take_f()
                two = mm_group(b, ps, wk_, [lambda e, o, s, p_, it_g=it_g, pairi=pairi:
                    e.matmul(o[:, 0:P], sAb[:, 2048 + pairi * P:2048 + (pairi + 1) * P],
                             mk.gB[:, ST0 + it_g * P:ST0 + (it_g + 1) * P],
                             start=s, stop=p_)],
                    extra_waits=[tcp, t_wo2])
                tcp2 = b.emit("vector", lambda e, kt=ps.t(wk_), it_g=it_g: e.tensor_copy(
                    mk.gB[:, ST0 + it_g * P:ST0 + (it_g + 1) * P], kt[:, 0:P]),
                    waits=ps.read_waits(wk_), inc="v")
                ps.set_free(wk_, tcp2)
                dtoks.append(tcp2)
            tdma = dma(b, "sync", cc_in.ap().rearrange("p (a t) -> p a t", a=4)[:, pairi],
                       mk.gB[:, ST0:ST0 + S], sem="dccin", waits=dtoks)
            mk.setw("strip_war", tdma)
            mk.setw("ccin_done", tdma)
    cc_out = mk.dram_internal(f"attx_out_{tag}", (2 * P, 4 * S), BF16)
    t2 = cc.allgather_pair(cc_in, cc_out, mk.tok["ccin_done"])
    get_offreg(mk, "sync", 0)
    get_offreg(mk, "sync", 1)
    tback = None
    for r in range(2):
        dstv = mk.gA[:, r * 4 * TH:(r + 1) * 4 * TH].rearrange("p (a t) -> p a t", a=4)

        def fnback(e, r=r, dstv=dstv, cc_out=cc_out):
            srcap = bass.AP(cc_out.ap().tensor, mk._regs[("sync", r)],
                            [[4 * S, P], [S, 4], [1, TH]])
            return e.dma_start(out=dstv, in_=srcap)
        tback = b.emit("sync", fnback, waits=[t2], inc="dab", inc_n=16)
    t_wo = load_w_kf(mk, "l0aa_wfc", mk.wA, NC, 0, D, D, sem="dwA",
                     waits=[tq, tkv, tv])
    t_bo = dma(b, "gp", mk.wB[0:1, 0:D], mk.inp("l0aa_bfc"), sem="dwB3",
               waits=[tmk_])
    last = None
    for i in range(NT):
        for cch in range(2):
            yk = ps.take_f()
            mms = [lambda e, o, s, p_, k=k, i=i, cch=cch: e.matmul(
                o[:, :], mk.gA[:, k * TH + i * P:k * TH + (i + 1) * P],
                mk.wA[:, k * D + cch * 512:k * D + (cch + 1) * 512],
                start=s, stop=p_) for k in range(NC)]
            mms.append(lambda e, o, s, p_, cch=cch: e.matmul(
                o[:, :], mk.ones1[0:1, 0:P], mk.wB[0:1, cch * 512:(cch + 1) * 512],
                start=s, stop=p_))
            tmy = mm_group(b, ps, yk, mms, extra_waits=[tback, t_wo, t_bo])
            xa = mk.xt[:, i * D + cch * 512:i * D + cch * 512 + 512]
            tev = b.emit("vector", lambda e, xa=xa, yt=ps.t(yk): e.tensor_add(
                xa, xa, yt[:, :]), waits=ps.read_waits(yk) + mk.w(f"xt{i}"), inc="v")
            ps.set_free(yk, tev)
            mk.setw(f"xt{i}", tev)
            last = tev
    mk.setw("xt", last)
    mk.setw("wA_free", tmy)
    mk.setw("wB_free", tmy)
    mk.setw("gB_free", last)
    mk.setw("sA_free", last)
    mk.setw("xlnT", None)
    return [last]


def pack_l0_attn_a(inp, core):
    hb = (core % 2) * 8
    wq = inp["L0_attn_a_Wq"][:, hb * 64:(hb + 8) * 64]
    wk = inp["L0_attn_a_Wk"][:, hb * 64:(hb + 8) * 64]
    wv = inp["L0_attn_a_Wv"][:, hb * 64:(hb + 8) * 64]
    wo = inp["L0_attn_a_W_o"]
    wop = np.zeros((4, P, P), np.float32)
    for a in range(4):
        wop[a, 0:64, 0:64] = wo[hb + 2 * a]
        wop[a, 64:128, 64:128] = wo[hb + 2 * a + 1]
    r = np.arange(P)
    dist = np.zeros((3, P, P), np.float32)
    for idx, Kr in enumerate((-1, 0, 1)):
        dist[idx] = np.abs(r[:, None] - (Kr * P + r[None, :]))
    sel = np.zeros((1, 2), np.float32)
    if core % 2 == 0:
        sel[0, 0] = 1.0
    if core == 0:
        sel[0, 1] = 1.0
    return {
        "l0aa_wq": np.ascontiguousarray(wq),
        "l0aa_wk": np.ascontiguousarray(wk),
        "l0aa_wv": np.ascontiguousarray(wv),
        "l0aa_wop": wop,
        "l0aa_wfc": np.ascontiguousarray(inp["L0_attn_a_Wfc"]),
        "l0aa_bfc": np.ascontiguousarray(inp["L0_attn_a_bfc"].reshape(1, D)),
        "l0aa_dist": dist,
        "l0aa_alpha": np.asarray(inp["L0_attn_a_alpha"]).reshape(1, 1).astype(np.float32),
        "l0aa_bqkv": np.stack([inp["L0_attn_a_bq"][hb * 64:(hb + 8) * 64],
                               inp["L0_attn_a_bk"][hb * 64:(hb + 8) * 64],
                               inp["L0_attn_a_bv"][hb * 64:(hb + 8) * 64]])[None],
        "l0aa_sel": sel,
        "c_offs": np.array([[(core % 2) * TH, (core % 2) * TH + P * 4 * S, (core % 2) * TH + P * 8 * S] + [0] * 5],
                           np.int32),
    }


register("l0_attn_a", emit_l0_attn_a, pack_l0_attn_a)


def ln_affine_and_transpose(mk, gname, bname, eps=1e-5, src=None):
    b, ps = mk.b, mk.ps
    ln_stats(mk, mk.xt, NT, eps=eps)
    tgb_ = dma(b, "gp", mk.sm2[:, 148:156], mk.inp(gname + "_pk"), sem="dln",
               waits=mk.w("lnstat"))
    tbb_ = dma(b, "gp", mk.sm2[:, 156:164], mk.inp(bname + "_pk"), sem="dln")
    av = mk.gA[:, 0:NC * TH].rearrange("p (j t) -> p j t", j=NC)
    last = None
    for i in range(NT):
        ta = ln_apply_tile(mk, mk.xt[:, i * D:(i + 1) * D], i, mk.tl[:, 0:D],
                           waits=mk.w("tl_free") + mk.w("xt") + mk.w(f"xt{i}"))
        dstv = av[:, :, i * P:(i + 1) * P]
        ttok, last = transpose_tile_to(mk, mk.tl[:, 0:D], mk.gA, dstv, i, ta)
        mk.setw("tl_free", ttok)
    last2 = None
    for k in range(NC):
        g_ap = mk.sm2[:, 148 + k:149 + k]
        b_ap = mk.sm2[:, 156 + k:157 + k]
        last2 = b.emit("vector", lambda e, k=k, g_ap=g_ap, b_ap=b_ap: e.tensor_scalar(
            mk.gA[:, k * TH:(k + 1) * TH], mk.gA[:, k * TH:(k + 1) * TH],
            g_ap, b_ap, ALU.mult, ALU.add),
            waits=[last, tgb_, tbb_], inc="v")
    mk.setw("xlnT", last2)
    return last2


def emit_l1_attn_b(mk, tag):
    b, ps, cc = mk.b, mk.ps, mk.cc
    ln_affine_and_transpose(mk, "l1ab_ln1g", "l1ab_ln1b")
    gather_xln_full(mk, tag)
    gAv = mk.gA
    xsp = mk.dram_internal("xspill", (TH, D), F32)
    xspv = xsp.ap().rearrange("(n p) c -> n p c", p=P)
    sp_toks = []
    for i in range(NT):
        sp_toks.append(dma(b, "sync", xspv[i], mk.xt[:, i * D:(i + 1) * D],
                           sem="dxsp", waits=mk.w("xt") + mk.w(f"xt{i}") + join_all(mk)))
    xta = mk.xt
    t_tm = dma(b, "gp", mk.sm2[:, 164:188], mk.inp("l1ab_tm"), sem="dtm",
               waits=mk.w("lnstat") + mk.w("xlnT"))
    t_td = dma(b, "gp", mk.sm2[:, 188:196], mk.inp("l1ab_tdf"), sem="dtd")
    e1 = b.emit("scalar", lambda e: e.activation(mk.sm2[:, 196:200],
                                                 mk.sm2[:, 188:192], AF.Exp),
                waits=[t_td], inc="s")
    e2 = b.emit("scalar", lambda e: e.activation(mk.sm2[:, 196:200],
                                                 mk.sm2[:, 196:200], AF.Exp,
                                                 scale=-1.0), waits=[e1], inc="s")
    t_wk = load_w_kf(mk, "l1ab_wkT", mk.wA, NC, 0, 512, 512, sem="dwA",
                     waits=mk.w("wA_free"))
    t_wr = dma(b, "gp", mk.wA[:, 4096:8192].rearrange("p (k f) -> p k f", k=NC),
               mk.inp("l1ab_wrT").rearrange("(k p) f -> p k f", p=P),
               sem="dwA2", waits=mk.w("wA_free"))
    t_wv = load_w_kf(mk, "l1ab_wvT", mk.wB, NC, 0, 512, 512, sem="dwB",
                     waits=mk.w("wB_free"))
    cc_in = mk.dram_internal(f"rw_in_{tag}", (P, 4 * S), BF16)
    gBb = mk.gB
    sBf = mk.gB.bitcast(F32)
    xtf = mk.xt
    xtb = mk.xt.bitcast(BF16)
    KOF, VOF = 0, 8192
    mixc = {"k": (164, 0), "v": (172, 1), "r": (180, 2)}
    y_toks = []
    for half in range(2):
        t0g = half * TH
        def make_mix(which, prev):
            col0, widx = mixc[which]
            mix_toks = []
            for k in range(NC):
                tm_ap = mk.sm2[:, col0 + k:col0 + k + 1]
                gslice = gAv[:, k * S + t0g:k * S + t0g + TH]
                if half == 0:
                    hh_main = gAv[:, k * S:k * S + TH - 1]
                    dst_main = gBb[:, k * TH + 1:(k + 1) * TH]
                else:
                    hh_main = gAv[:, k * S + t0g - 1:k * S + t0g + TH - 1]
                    dst_main = gBb[:, k * TH:(k + 1) * TH]
                tmix0 = b.emit("vector", lambda e, k=k, tm_ap=tm_ap, gslice=gslice:
                               e.tensor_scalar(gBb[:, k * TH:(k + 1) * TH], gslice,
                                               tm_ap, None, ALU.mult),
                               waits=[t_tm, tm1d] + prev + mk.w("xlnT_full"),
                               inc="v")
                tm1_ap = mk.sm2[:, 200 + widx * 8 + k:201 + widx * 8 + k]
                tmix = b.emit("vector", lambda e, hh_main=hh_main, tm1_ap=tm1_ap,
                              dst_main=dst_main:
                              e.scalar_tensor_tensor(dst_main, hh_main, tm1_ap,
                                                     dst_main, ALU.mult, ALU.add),
                              waits=[tmix0], inc="v")
                mix_toks.append(tmix)
            return mix_toks

        if half == 0:
            tm1c = b.emit("vector", lambda e: e.tensor_single_scalar(
                mk.sm2[:, 200:224], mk.sm2[:, 164:188], 1.0, ALU.subtract),
                waits=[t_tm], inc="v")
            tm1d = b.emit("vector", lambda e: e.tensor_scalar_mul(
                mk.sm2[:, 200:224], mk.sm2[:, 200:224], -1.0), waits=[tm1c], inc="v")
        mixk = make_mix("k", mk.w("gBmix"))
        km_toks = []
        for oc in range(4):
            for tchk in range(4):
                pk = ps.take_f()
                mms = [lambda e, o, s, p_, k=k, oc=oc, tchk=tchk: e.matmul(
                    o[:, :], mk.wA[:, k * 512 + oc * P:k * 512 + oc * P + P],
                    gBb[:, k * TH + tchk * 512:k * TH + (tchk + 1) * 512],
                    start=s, stop=p_) for k in range(NC)]
                tmmk = mm_group(b, ps, pk, mms, extra_waits=[t_wk] + mixk)
                tev = b.emit("vector", lambda e, kt=ps.t(pk), oc=oc, tchk=tchk:
                             e.tensor_copy(
                                 xtf[:, KOF + oc * TH + tchk * 512:KOF + oc * TH + (tchk + 1) * 512],
                                 kt[:, :]),
                             waits=ps.read_waits(pk) + (sp_toks[-1:] if half == 0 else []) + mk.w("xtk"),
                             inc="v")
                ps.set_free(pk, tev)
                km_toks.append(tev)
        mixv = make_mix("v", km_toks[-1:])
        for oc in range(4):
            for tchk in range(4):
                pk = ps.take_f()
                mms = [lambda e, o, s, p_, k=k, oc=oc, tchk=tchk: e.matmul(
                    o[:, :], mk.wB[:, k * 512 + oc * P:k * 512 + oc * P + P],
                    gBb[:, k * TH + tchk * 512:k * TH + (tchk + 1) * 512],
                    start=s, stop=p_) for k in range(NC)]
                tmmv = mm_group(b, ps, pk, mms, extra_waits=[t_wv] + mixv)
                tev = b.emit("vector", lambda e, kt=ps.t(pk), oc=oc, tchk=tchk:
                             e.tensor_copy(
                                 xtf[:, VOF + oc * TH + tchk * 512:VOF + oc * TH + (tchk + 1) * 512],
                                 kt[:, :]),
                             waits=ps.read_waits(pk) + mk.w("xtv"), inc="v")
                ps.set_free(pk, tev)
                km_toks.append(tev)
        mixr = make_mix("r", km_toks[-1:])
        srst = mk.dram_internal(f"srstash", (512, S), BF16)
        srv = srst.ap().rearrange("(a p) t -> a p t", p=P)
        sr_toks = []
        for oc in range(4):
            for tchk in range(4):
                pk = ps.take_f()
                mms = [lambda e, o, s, p_, k=k, oc=oc, tchk=tchk: e.matmul(
                    o[:, :], mk.wA[:, 4096 + k * 512 + oc * P:4096 + k * 512 + oc * P + P],
                    gBb[:, k * TH + tchk * 512:k * TH + (tchk + 1) * 512],
                    start=s, stop=p_) for k in range(NC)]
                tmmr = mm_group(b, ps, pk, mms, extra_waits=[t_wr] + mixr)
                tev = b.emit("scalar", lambda e, kt=ps.t(pk): e.activation(
                    mk.tl[:, 0:512], kt[:, :], AF.Sigmoid),
                    waits=ps.read_waits(pk) + mk.w("tlsr"), inc="s")
                ps.set_free(pk, tev)
                tsd = dma(b, "sync", srv[oc][:, t0g + tchk * 512:t0g + (tchk + 1) * 512],
                          mk.tl[:, 0:512], sem="dsr", waits=[tev])
                mk.setw("tlsr", tsd)
                sr_toks.append(tsd)
        mk.setw("gBmix2", sr_toks[-1])
        for oc in range(4):
            kT = xtf[:, KOF + oc * TH:KOF + (oc + 1) * TH]
            vT = xtf[:, VOF + oc * TH:VOF + (oc + 1) * TH]
            ew = mk.sm2[:, 196 + oc:197 + oc]
            u_ap = mk.sm2[:, 192 + oc:193 + oc]
            ekb = sBf[:, 0:TH]
            Db = sBf[:, 2048:2048 + TH]
            Nb = sBf[:, 0:TH]
            ewr = mk.sA[:, 0:TH]
            sm_ = b.emit("vector", lambda e, kT=kT, oc=oc: e.reduce_max(
                mk.sm2[:, 224 + oc:225 + oc], kT, axis=AX),
                waits=km_toks[-1:] + mk.w("scanbuf"), inc="v")
            Mneg = mk.sm2[:, 228 + oc:229 + oc]
            sm2_ = b.emit("vector", lambda e, oc=oc, Mneg=Mneg: e.tensor_scalar_mul(
                Mneg, mk.sm2[:, 224 + oc:225 + oc], -1.0), waits=[sm_], inc="v")
            if half == 1:
                dM = mk.sm2[:, 232 + oc:233 + oc]
                c1 = b.emit("vector", lambda e, dM=dM, oc=oc: e.tensor_sub(
                    dM, mk.sm2[:, 236 + oc:237 + oc], mk.sm2[:, 224 + oc:225 + oc]),
                    waits=[sm2_], inc="v")
                c2 = b.emit("scalar", lambda e, dM=dM: e.activation(dM, dM, AF.Exp),
                            waits=[c1], inc="s")
                c3 = b.emit("vector", lambda e, oc=oc, dM=dM: e.tensor_mul(
                    mk.sm2[:, 240 + oc:241 + oc], mk.sm2[:, 240 + oc:241 + oc], dM),
                    waits=[c2], inc="v")
                c4 = b.emit("vector", lambda e, oc=oc, dM=dM: e.tensor_mul(
                    mk.sm2[:, 244 + oc:245 + oc], mk.sm2[:, 244 + oc:245 + oc], dM),
                    waits=[c3], inc="v")
                carry_tok = c4
            else:
                z1 = b.emit("vector", lambda e, oc=oc: e.memset(
                    mk.sm2[:, 240 + oc:241 + oc], 0.0), waits=[sm2_], inc="v")
                carry_tok = b.emit("vector", lambda e, oc=oc: e.memset(
                    mk.sm2[:, 244 + oc:245 + oc], 0.0), waits=[z1], inc="v")
            msave = b.emit("vector", lambda e, oc=oc: e.tensor_copy(
                mk.sm2[:, 236 + oc:237 + oc], mk.sm2[:, 224 + oc:225 + oc]),
                waits=[carry_tok], inc="v")
            s1 = b.emit("scalar", lambda e, kT=kT, ekb=ekb, Mneg=Mneg: e.activation(
                ekb, kT, AF.Exp, bias=Mneg), waits=[msave], inc="s")
            s2a = b.emit("vector", lambda e, ewr=ewr: e.memset(ewr, 0.0),
                         waits=mk.w("scanbuf"), inc="v")
            s2b = b.emit("vector", lambda e, ewr=ewr, ew=ew: e.tensor_scalar_add(
                ewr, ewr, ew), waits=[s2a, e2], inc="v")
            s3 = b.emit("vector", lambda e, Db=Db, ewr=ewr, ekb=ekb, oc=oc:
                        e.tensor_tensor_scan(Db, ewr, ekb,
                                             mk.sm2[:, 244 + oc:245 + oc],
                                             ALU.mult, ALU.add),
                        waits=[s1, s2b], inc="v")
            s4 = b.emit("vector", lambda e, ekb=ekb, vT=vT: e.tensor_mul(ekb, ekb, vT),
                        waits=[s3], inc="v")
            s5 = b.emit("vector", lambda e, Nb=Nb, ewr=ewr, ekb=ekb, oc=oc:
                        e.tensor_tensor_scan(Nb, ewr, ekb,
                                             mk.sm2[:, 240 + oc:241 + oc],
                                             ALU.mult, ALU.add),
                        waits=[s4], inc="v")
            s5a = b.emit("vector", lambda e, oc=oc: e.tensor_copy(
                mk.sm2[:, 252 + oc:253 + oc], mk.sm2[:, 240 + oc:241 + oc]),
                waits=[s5] + mk.w("carry0"), inc="v")
            s5b = b.emit("vector", lambda e, oc=oc: e.tensor_copy(
                mk.sm2[:, 232 + oc:233 + oc], mk.sm2[:, 244 + oc:245 + oc]),
                waits=[s5a], inc="v")
            s6 = b.emit("vector", lambda e, Nb=Nb, oc=oc: e.tensor_copy(
                mk.sm2[:, 240 + oc:241 + oc], Nb[:, TH - 1:TH]), waits=[s5b], inc="v")
            s7 = b.emit("vector", lambda e, Db=Db, oc=oc: e.tensor_copy(
                mk.sm2[:, 244 + oc:245 + oc], Db[:, TH - 1:TH]), waits=[s6], inc="v")
            ub = mk.sm2[:, 248 + oc:249 + oc]
            u1_ = b.emit("vector", lambda e, ub=ub, u_ap=u_ap, Mneg=Mneg: e.tensor_add(
                ub, u_ap, Mneg), waits=[s7, t_td], inc="v")
            s8 = b.emit("scalar", lambda e, kT=kT, ub=ub: e.activation(
                kT, kT, AF.Exp, bias=ub), waits=[u1_], inc="s")
            s9 = b.emit("vector", lambda e, kT=kT, vT=vT: e.tensor_mul(vT, kT, vT),
                        waits=[s8], inc="v")
            s10 = b.emit("vector", lambda e, vT=vT, Nb=Nb: e.tensor_tensor(
                vT[:, 1:TH], vT[:, 1:TH], Nb[:, 0:TH - 1], ALU.add),
                waits=[s9], inc="v")
            s11 = b.emit("vector", lambda e, vT=vT, oc=oc: e.scalar_tensor_tensor(
                vT[:, 0:1], mk.cf32[:, 4:5], mk.sm2[:, 252 + oc:253 + oc],
                vT[:, 0:1], ALU.mult, ALU.add),
                waits=[s10] + mk.w("consts3"), inc="v")
            s12 = b.emit("vector", lambda e, kT=kT, Db=Db: e.tensor_tensor(
                kT[:, 1:TH], kT[:, 1:TH], Db[:, 0:TH - 1], ALU.add),
                waits=[s11], inc="v")
            s13 = b.emit("vector", lambda e, kT=kT, oc=oc: e.scalar_tensor_tensor(
                kT[:, 0:1], mk.cf32[:, 4:5], mk.sm2[:, 232 + oc:233 + oc],
                kT[:, 0:1], ALU.mult, ALU.add),
                waits=[s12], inc="v")
            mk.setw("carry0", s13)
            tsl = dma(b, "sync", gBb[:, 8192:8192 + TH],
                      srv[oc][:, t0g:t0g + TH], sem="dsrl",
                      waits=sr_toks[-1:] + mk.w("srslot"))
            s16 = b.emit("vector", lambda e, kT=kT: e.reciprocal(kT, kT),
                         waits=[s13], inc="v")
            s17 = b.emit("vector", lambda e, kT=kT, vT=vT: e.tensor_mul(vT, vT, kT),
                         waits=[s16], inc="v")
            ydst = xtb[:, 2 * (KOF + oc * TH):2 * (KOF + oc * TH) + TH]
            s18 = b.emit("vector", lambda e, vT=vT, ydst=ydst:
                         e.tensor_mul(ydst, vT, gBb[:, 8192:8192 + TH]),
                         waits=[s17, tsl], inc="v")
            mk.setw("srslot", s18)
            tdy = dma(b, "sync",
                      cc_in.ap().rearrange("p (a t) -> p a t", a=4)[:, oc, t0g:t0g + TH],
                      ydst, sem="dyinR", waits=[s18])
            y_toks.append(tdy)
            mk.setw("scanbuf", s18)
            mk.setw("xtk", tdy)
            mk.setw("xtv", s18)
        mk.setw("gBmix", sr_toks[-1])
    cc_out = mk.dram_internal(f"rw_out_{tag}", (2 * P, 4 * S), BF16)
    t2 = cc.allgather_pair(cc_in, cc_out, y_toks[-1])
    get_offreg(mk, "sync", 0)
    get_offreg(mk, "sync", 1)
    tback = None
    for r in range(2):
        dstv = mk.gA[:, r * 4 * TH:(r + 1) * 4 * TH].rearrange("p (a t) -> p a t", a=4)

        def fnback(e, r=r, dstv=dstv, cc_out=cc_out):
            srcap = bass.AP(cc_out.ap().tensor, mk._regs[("sync", r)],
                            [[4 * S, P], [S, 4], [1, TH]])
            return e.dma_start(out=dstv, in_=srcap)
        tback = b.emit("sync", fnback, waits=[t2], inc="drb", inc_n=16)
    rl_toks = []
    for i in range(NT):
        rl_toks.append(dma(b, "sync", mk.xt[:, i * D:(i + 1) * D], xspv[i],
                           sem="dxrl", waits=[y_toks[-1], tback]))
    mk.setw("xt", rl_toks[-1])
    for i in range(NT):
        mk.setw(f"xt{i}", rl_toks[-1])
    ln_stats(mk, mk.xt, NT)
    xtoks = []
    for i in range(NT):
        m = mk.sm[:, 2 * i:2 * i + 1]
        r = mk.sm[:, 2 * i + 1:2 * i + 2]
        xtoks.append(b.emit("vector", lambda e, i=i, m=m, r=r: e.tensor_scalar(
            mk.xt[:, i * D:(i + 1) * D], mk.xt[:, i * D:(i + 1) * D], m, r,
            ALU.subtract, ALU.mult), waits=mk.w("lnstat"), inc="v"))
        mk.setw(f"xt{i}", xtoks[-1])
    t_wo = load_w_kf(mk, "l1ab_woT", mk.wA, NC, 0, D, D, sem="dwA", waits=[tmmr])
    last = None
    for i in range(NT):
        for cch in range(2):
            yk = ps.take_f()
            mms = [lambda e, o, s, p_, k=k, i=i, cch=cch: e.matmul(
                o[:, :], mk.gA[:, k * TH + i * P:k * TH + (i + 1) * P],
                mk.wA[:, k * D + cch * 512:k * D + (cch + 1) * 512],
                start=s, stop=p_) for k in range(NC)]
            tmy = mm_group(b, ps, yk, mms, extra_waits=[tback, t_wo])
            xa = mk.xt[:, i * D + cch * 512:i * D + cch * 512 + 512]
            tev = b.emit("vector", lambda e, xa=xa, yt=ps.t(yk): e.tensor_add(
                xa, xa, yt[:, :]),
                waits=ps.read_waits(yk) + [xtoks[i]] + mk.w(f"xt{i}"), inc="v")
            ps.set_free(yk, tev)
            mk.setw(f"xt{i}", tev)
            last = tev
    mk.setw("xt", last)
    mk.setw("wA_free", tmy)
    mk.setw("wB_free", tmy)
    mk.setw("gB_free", last)
    mk.setw("xlnT", None)
    _l1ab_channel_mix(mk, tag)
    return [mk.tok["xt"]]


def _l1ab_channel_mix(mk, tag):
    b, ps, cc = mk.b, mk.ps, mk.cc
    ln_affine_and_transpose(mk, "l1ab_ln2g", "l1ab_ln2b")
    gAv = mk.gA
    t_cm = dma(b, "gp", mk.sm2[:, 180:196], mk.inp("l1ab_cm"), sem="dtm",
               waits=mk.w("lnstat"))
    c1 = b.emit("vector", lambda e: e.tensor_single_scalar(
        mk.sm2[:, 196:212], mk.sm2[:, 180:196], 1.0, ALU.subtract),
        waits=[t_cm], inc="v")
    c2 = b.emit("vector", lambda e: e.tensor_scalar_mul(
        mk.sm2[:, 196:212], mk.sm2[:, 196:212], -1.0), waits=[c1], inc="v")
    t_bm = dma(b, "gp", mk.sm2[:, 212:214], mk.inp("l1ab_bmask"), sem="dbm")
    gv = mk.gA[:, 0:NC * TH].rearrange("p (k t) -> p k t", k=NC)
    b1 = b.emit("vector", lambda e: e.tensor_scalar(
        mk.sm2[:, 164:172].rearrange("p (k o) -> p k o", k=NC),
        gv[:, :, TH - 1:TH], mk.sm2[:, 212:213], None, ALU.mult),
        waits=mk.w("xlnT") + [t_bm], inc="v")
    bd_in = mk.dram_internal(f"bnd_in_{tag}", (P, 8), F32)
    bd_out = mk.dram_internal(f"bnd_out_{tag}", (2 * P, 8), F32)
    b2 = dma(b, "gp", bd_in.ap(), mk.sm2[:, 164:172], sem="dbnd", waits=[b1])
    b3 = cc.allgather_pair(bd_in, bd_out, b2)
    b4 = dma(b, "gp", mk.sm2[:, 164:172], bd_out.ap()[0:P, :], sem="dbnd2",
             waits=[b3])
    b5 = b.emit("vector", lambda e: e.tensor_scalar(
        mk.sm2[:, 172:180], mk.sm2[:, 164:172], mk.sm2[:, 213:214], None, ALU.mult),
        waits=[b4], inc="v")

    def mix2(col0, col1m, dst_off, prev):
        toks = []
        for k in range(NC):
            cm_ap = mk.sm2[:, col0 + k:col0 + k + 1]
            cm1_ap = mk.sm2[:, col1m + k:col1m + k + 1]
            dst = mk.gA[:, dst_off + k * TH:dst_off + (k + 1) * TH]
            tm0 = b.emit("vector", lambda e, k=k, cm_ap=cm_ap, dst=dst:
                         e.tensor_scalar(dst, gAv[:, k * TH:(k + 1) * TH],
                                         cm_ap, None, ALU.mult),
                         waits=[t_cm, c2, b5] + prev + mk.w("xlnT"), inc="v")
            tm1 = b.emit("vector", lambda e, k=k, cm1_ap=cm1_ap, dst=dst:
                         e.scalar_tensor_tensor(
                             dst[:, 1:TH], gAv[:, k * TH:k * TH + TH - 1], cm1_ap,
                             dst[:, 1:TH], ALU.mult, ALU.add),
                         waits=[tm0], inc="v")
            tm2 = b.emit("vector", lambda e, k=k, cm1_ap=cm1_ap, dst=dst:
                         e.scalar_tensor_tensor(
                             dst[:, 0:1], mk.sm2[:, 172 + k:173 + k], cm1_ap,
                             dst[:, 0:1], ALU.mult, ALU.add),
                         waits=[tm1], inc="v")
            toks.append(tm2)
        return toks

    XK2 = NC * TH
    mr = mix2(188, 204, XK2, mk.w("gB_free"))
    t_wcr = load_w_kf(mk, "l1ab_wcrT", mk.wA, NC, 0, D, D, sem="dwA",
                      waits=mk.w("wA_free"))
    sig_toks = []
    for i in range(NT):
        for cch in range(2):
            pk = ps.take_f()
            mms = [lambda e, o, s, p_, k=k, i=i, cch=cch: e.matmul(
                o[:, :], mk.gA[:, XK2 + k * TH + i * P:XK2 + k * TH + (i + 1) * P],
                mk.wA[:, k * D + cch * 512:k * D + (cch + 1) * 512],
                start=s, stop=p_) for k in range(NC)]
            tmm = mm_group(b, ps, pk, mms, extra_waits=[t_wcr] + mr)
            tev = b.emit("scalar", lambda e, kt=ps.t(pk), i=i, cch=cch: e.activation(
                mk.gB[:, i * D + cch * 512:i * D + cch * 512 + 512], kt[:, :],
                AF.Sigmoid), waits=ps.read_waits(pk) + mk.w("gBsig"), inc="s")
            ps.set_free(pk, tev)
            sig_toks.append(tev)
    sig_mm_last = tmm
    mkx = mix2(180, 196, XK2, [tmm])
    wckv = mk.inp("l1ab_wckT").rearrange("(k p) f -> p k f", p=P)
    wcvv = mk.inp("l1ab_wcvT").rearrange("(g p) f -> g p f", p=P)
    last = None
    for tg in range(8):
        kvkeys = [ps.reserve_f() for _ in range(4)]
        kv_war = []
        for kk_ in kvkeys:
            kv_war += ps.write_waits(kk_)
        started = [True] * 4
        kvtok = [None] * 4
        for ht in range(32):
            slot = ht % 8
            twk = dma(b, "gp", mk.wA[:, slot * 1024:slot * 1024 + 1024].rearrange(
                "p (k f) -> p k f", k=NC), wckv[:, :, ht * P:(ht + 1) * P],
                sem="dck",
                waits=mk.w(f"ckslot{slot}") + ([sig_mm_last] if ht < 8 else []))
            twv = dma(b, "gp", mk.wB[:, slot * 1024:slot * 1024 + 1024],
                      wcvv[ht], sem="dcv", waits=mk.w(f"cvslot{slot}") +
                      (mk.w("wB_free") if ht < 8 and tg == 0 else []))
            sk = ps.take_f()
            mms = [lambda e, o, s, p_, k=k, slot=slot, tg=tg: e.matmul(
                o[:, 0:256], mk.wA[:, slot * 1024 + k * P:slot * 1024 + (k + 1) * P],
                mk.gA[:, XK2 + k * TH + tg * 256:XK2 + k * TH + (tg + 1) * 256],
                start=s, stop=p_) for k in range(NC)]
            tsq = mm_group(b, ps, sk, mms, extra_waits=[twk] + mkx)
            tr1 = b.emit("vector", lambda e, kt=ps.t(sk): e.tensor_scalar_max(
                mk.sA[:, 0:256], kt[:, 0:256], 0.0),
                waits=ps.read_waits(sk) + mk.w("sAkk"), inc="v")
            tr2 = b.emit("scalar", lambda e: e.activation(
                mk.tl[:, 0:256], mk.sA[:, 0:256], AF.Square),
                waits=[tr1] + mk.w("tlkk"), inc="s")
            ps.set_free(sk, tr1)
            mk.setw("sAkk", tr2)
            for tt in range(2):
                for cch in range(2):
                    qi = tt * 2 + cch
                    fn = lambda e, tt=tt, cch=cch, slot=slot, s=started[qi], \
                        kk_=kvkeys[qi], last_=(ht == 31): e.matmul(
                        ps.t(kk_)[:, :], mk.tl[:, tt * P:(tt + 1) * P],
                        mk.wB[:, slot * 1024 + cch * 512:slot * 1024 + (cch + 1) * 512],
                        start=s, stop=last_)
                    w_ = [tr2, twv] + (kv_war if started[qi] else [])
                    if ht == 31 or (tt == 1 and cch == 1):
                        tokm = b.emit("tensor", fn, waits=w_, inc="t")
                        if ht == 31:
                            kvtok[qi] = tokm
                        if tt == 1 and cch == 1:
                            mk.setw("tlkk", tokm)
                    else:
                        b.emit("tensor", fn, waits=w_)
                    started[qi] = False
            mk.setw(f"ckslot{slot}", mk.tok["tlkk"])
            mk.setw(f"cvslot{slot}", mk.tok["tlkk"])
        for qi in range(4):
            ps.set_ready(kvkeys[qi], kvtok[qi])
        for tt in range(2):
            i_glob = tg * 2 + tt
            for cch in range(2):
                qi = tt * 2 + cch
                tv1 = b.emit("vector", lambda e, qi=qi, i_glob=i_glob, cch=cch:
                             e.tensor_mul(
                                 mk.sA[:, 256:768],
                                 ps.t(kvkeys[qi])[:, :],
                                 mk.gB[:, i_glob * D + cch * 512:i_glob * D + cch * 512 + 512]),
                             waits=[kvtok[qi], sig_toks[-1]] + mk.w("sAkv"), inc="v")
                xa = mk.xt[:, i_glob * D + cch * 512:i_glob * D + cch * 512 + 512]
                tv2 = b.emit("vector", lambda e, xa=xa: e.tensor_add(
                    xa, xa, mk.sA[:, 256:768]),
                    waits=[tv1] + mk.w(f"xt{i_glob}"), inc="v")
                ps.set_free(kvkeys[qi], tv2)
                mk.setw("sAkv", tv2)
                mk.setw(f"xt{i_glob}", tv2)
                last = tv2
        for qi in range(4):
            ps.release_f(kvkeys[qi])
    xsp2 = mk.dram_internal("xspill", (TH, D), F32)
    xspv2 = xsp2.ap().rearrange("(n p) c -> n p c", p=P)
    for i in range(NT):
        trl = dma(b, "sync", mk.sA[:, 0:D].bitcast(F32) if False else mk.sA[:, 0:1024],
                  xspv2[i], sem="dxrl2",
                  waits=[last] + mk.w("sA_free") + mk.w("sAkk") + mk.w("sAkv"))
        last = b.emit("vector", lambda e, i=i: e.tensor_add(
            mk.xt[:, i * D:(i + 1) * D], mk.xt[:, i * D:(i + 1) * D],
            mk.sA[:, 0:1024]), waits=[trl] + mk.w(f"xt{i}"), inc="v")
        mk.setw(f"xt{i}", last)
        mk.setw("sA_free", last)
    mk.setw("xt", last)
    mk.setw("wA_free", tsq)
    mk.setw("wB_free", kvtok[-1])
    mk.setw("gB_free", last)
    mk.setw("gBsig", last)
    mk.setw("sA_free", last)
    mk.setw("xlnT", None)
    return [last]


def pack_l1_attn_b(inp, core):
    par = core % 2

    def pk_(v):
        return np.ascontiguousarray(np.asarray(v).reshape(8, 128).T.astype(np.float32))

    tm = np.concatenate([pk_(inp["L1_attn_b_tm_k"]), pk_(inp["L1_attn_b_tm_v"]),
                         pk_(inp["L1_attn_b_tm_r"])], axis=1)
    tdf = np.concatenate([
        inp["L1_attn_b_time_decay"].reshape(8, 128).T[:, par * 4:par * 4 + 4],
        inp["L1_attn_b_time_first"].reshape(8, 128).T[:, par * 4:par * 4 + 4]],
        axis=1).astype(np.float32)
    cm = np.concatenate([pk_(inp["L1_attn_b_cm_k"]), pk_(inp["L1_attn_b_cm_r"])],
                        axis=1)
    bmask = np.zeros((P, 2), np.float32)
    bmask[:, 0] = 1.0 if par == 0 else 0.0
    bmask[:, 1] = 0.0 if par == 0 else 1.0
    o0 = par * 512
    return {
        "l1ab_ln1g_pk": pk_(inp["L1_attn_b_ln1_g"]),
        "l1ab_ln1b_pk": pk_(inp["L1_attn_b_ln1_b"]),
        "l1ab_ln2g_pk": pk_(inp["L1_attn_b_ln2_g"]),
        "l1ab_ln2b_pk": pk_(inp["L1_attn_b_ln2_b"]),
        "l1ab_tm": np.ascontiguousarray(tm),
        "l1ab_tdf": np.ascontiguousarray(tdf),
        "l1ab_cm": np.ascontiguousarray(cm),
        "l1ab_bmask": bmask,
        "l1ab_wkT": np.ascontiguousarray(inp["L1_attn_b_Wk"].T[:, o0:o0 + 512]),
        "l1ab_wvT": np.ascontiguousarray(inp["L1_attn_b_Wv"].T[:, o0:o0 + 512]),
        "l1ab_wrT": np.ascontiguousarray(inp["L1_attn_b_Wr"].T[:, o0:o0 + 512]),
        "l1ab_woT": np.ascontiguousarray(inp["L1_attn_b_Wo"].T),
        "l1ab_wckT": np.ascontiguousarray(inp["L1_attn_b_Wck"].T),
        "l1ab_wcvT": np.ascontiguousarray(inp["L1_attn_b_Wcv"].T),
        "l1ab_wcrT": np.ascontiguousarray(inp["L1_attn_b_Wcr"].T),
        "c_offs": np.array([[par * TH, par * TH + P * 4 * S, par * TH + P * 8 * S] + [0] * 5], np.int32),
    }


register("l1_attn_b", emit_l1_attn_b, pack_l1_attn_b)


def emit_l3_attn_b(mk, tag):
    b, ps, cc = mk.b, mk.ps, mk.cc
    ln_and_transpose(mk)
    gather_xln_full(mk, tag)
    gAv = mk.gA
    gBb = mk.gB
    t_maa = dma(b, "gp", mk.sm2[:, 164:196], mk.inp("l3ab_maa"), sem="dtm",
                waits=mk.w("lnstat"))
    c1 = b.emit("vector", lambda e: e.tensor_single_scalar(
        mk.sm2[:, 196:228], mk.sm2[:, 164:196], 1.0, ALU.subtract),
        waits=[t_maa], inc="v")
    c2 = b.emit("vector", lambda e: e.tensor_scalar_mul(
        mk.sm2[:, 196:228], mk.sm2[:, 196:228], -1.0), waits=[c1], inc="v")
    t_du = dma(b, "gp", mk.sm2[0:1, 228:244], mk.inp("l3ab_du"), sem="ddu")
    xbf = mk.xb.bitcast(F32)
    t_e64 = dma(b, "gp", xbf[0:64, 0:3], mk.inp("l3ab_qc"), sem="dxb0",
                waits=mk.w("xb_free"))
    t_ET = dma(b, "gp", xbf[0:64, 8:72], mk.inp("l3ab_ET"), sem="dxb0")
    t_MT = dma(b, "gp", mk.xb[0:64, 704:768], mk.inp("l3ab_MTb"), sem="dxb0")
    t_EY = dma(b, "gp", xbf[0:64, 72:136], mk.inp("l3ab_eye"), sem="dxb0")
    t_lw = dma(b, "gp", mk.sm2[:, 244:252], mk.inp("l3ab_lnwb"), sem="dlw")
    st_kT = mk.dram_internal(f"3b_kT", (P, 4 * S), BF16)
    st_rT = mk.dram_internal(f"3b_rT", (P, 4 * S), BF16)
    st_gT = mk.dram_internal(f"3b_gT", (P, 4 * S), BF16)
    st_kk = mk.dram_internal(f"3b_ktok", (S, 512), BF16)
    st_vv = mk.dram_internal(f"3b_vtok", (S, 512), BF16)

    def mix_to_gB(widx, half, prev):
        col0 = 164 + widx * 8
        col1 = 196 + widx * 8
        t0g = half * TH
        toks = []
        for k in range(NC):
            maa = mk.sm2[:, col0 + k:col0 + k + 1]
            m1 = mk.sm2[:, col1 + k:col1 + k + 1]
            tm0 = b.emit("vector", lambda e, k=k, m1=m1, t0g=t0g: e.tensor_scalar(
                gBb[:, k * TH:(k + 1) * TH], gAv[:, k * S + t0g:k * S + t0g + TH],
                m1, None, ALU.mult), waits=[c2] + prev + mk.w("xlnT_full"), inc="v")
            if half == 0:
                tm1 = b.emit("vector", lambda e, k=k, maa=maa: e.scalar_tensor_tensor(
                    gBb[:, k * TH + 1:(k + 1) * TH], gAv[:, k * S:k * S + TH - 1],
                    maa, gBb[:, k * TH + 1:(k + 1) * TH], ALU.mult, ALU.add),
                    waits=[tm0], inc="v")
            else:
                tm1 = b.emit("vector", lambda e, k=k, maa=maa, t0g=t0g:
                             e.scalar_tensor_tensor(
                                 gBb[:, k * TH:(k + 1) * TH],
                                 gAv[:, k * S + t0g - 1:k * S + t0g + TH - 1],
                                 maa, gBb[:, k * TH:(k + 1) * TH], ALU.mult, ALU.add),
                             waits=[tm0], inc="v")
            toks.append(tm1)
        return toks

    t_wr = load_w_kf(mk, "l3ab_wrT", mk.wA, NC, 0, 512, 512, sem="dwA",
                     waits=mk.w("wA_free"))
    t_wk = dma(b, "gp", mk.wA[:, 4096:8192].rearrange("p (k f) -> p k f", k=NC),
               mk.inp("l3ab_wkT").rearrange("(k p) f -> p k f", p=P),
               sem="dwA2", waits=mk.w("wA_free"))
    t_wv = load_w_kf(mk, "l3ab_wvT", mk.wB, NC, 0, 512, 512, sem="dwB",
                     waits=mk.w("wB_free"))
    t_wg = dma(b, "gp", mk.wB[:, 4096:8192].rearrange("p (k f) -> p k f", k=NC),
               mk.inp("l3ab_wgT").rearrange("(k p) f -> p k f", p=P),
               sem="dwB2", waits=mk.w("wB_free"))

    def proj_chan(wbuf, woff, wtok, mtoks, dst_dram, sem, half, act=None):
        toks = []
        t0g = half * TH
        for ot in range(4):
            for tch in range(4):
                pk = ps.take_f()
                mms = [lambda e, o, s, p_, k=k, ot=ot, tch=tch, wbuf=wbuf, woff=woff:
                       e.matmul(o[:, :],
                                wbuf[:, woff + k * 512 + ot * P:woff + k * 512 + (ot + 1) * P],
                                gBb[:, k * TH + tch * 512:k * TH + (tch + 1) * 512],
                                start=s, stop=p_) for k in range(NC)]
                tmm = mm_group(b, ps, pk, mms, extra_waits=[wtok] + mtoks)
                if act is None:
                    tev = b.emit("vector", lambda e, kt=ps.t(pk): e.tensor_copy(
                        mk.tl[:, 0:512], kt[:, :]),
                        waits=ps.read_waits(pk) + mk.w("tlp"), inc="v")
                else:
                    tev = b.emit("scalar", lambda e, kt=ps.t(pk), act=act:
                                 e.activation(mk.tl[:, 0:512], kt[:, :], act),
                                 waits=ps.read_waits(pk) + mk.w("tlp"), inc="s")
                ps.set_free(pk, tev)
                td_ = dma(b, "sync",
                          dst_dram.ap().rearrange("p (o t) -> p o t", o=4)[:, ot,
                              t0g + tch * 512:t0g + (tch + 1) * 512],
                          mk.tl[:, 0:512], sem=sem, waits=[tev])
                mk.setw("tlp", td_)
                toks.append(td_)
        return toks

    def proj_tok(wbuf, woff, wtok, mtoks, dst_dram, sem, half):
        toks = []
        for tt in range(NT):
            pk = ps.take_f()
            mms = [lambda e, o, s, p_, k=k, tt=tt, wbuf=wbuf, woff=woff: e.matmul(
                o[:, :], gBb[:, k * TH + tt * P:k * TH + (tt + 1) * P],
                wbuf[:, woff + k * 512:woff + (k + 1) * 512],
                start=s, stop=p_) for k in range(NC)]
            tmm = mm_group(b, ps, pk, mms, extra_waits=[wtok] + mtoks)
            tev = b.emit("vector", lambda e, kt=ps.t(pk): e.tensor_copy(
                mk.tl[:, 0:512], kt[:, :]),
                waits=ps.read_waits(pk) + mk.w("tlp"), inc="v")
            ps.set_free(pk, tev)
            td_ = dma(b, "sync",
                      dst_dram.ap().rearrange("(tt p) o -> tt p o", p=P)[half * NT + tt],
                      mk.tl[:, 0:512], sem=sem, waits=[tev])
            mk.setw("tlp", td_)
            toks.append(td_)
        return toks

    tk1 = tk2 = tv1 = tr1 = tg1 = [None]
    for half in range(2):
        mt = mix_to_gB(0, half, mk.w("gB_free") + tk1[-1:] + tk2[-1:])
        tk1 = proj_chan(mk.wA, 4096, t_wk, mt, st_kT, "dstk", half)
        tk2 = proj_tok(mk.wA, 4096, t_wk, mt, st_kk, "dstk2", half)
        mt = mix_to_gB(1, half, tk1[-1:] + tk2[-1:])
        tv1 = proj_tok(mk.wB, 0, t_wv, mt, st_vv, "dstv", half)
        mt = mix_to_gB(2, half, tv1[-1:])
        tr1 = proj_chan(mk.wA, 0, t_wr, mt, st_rT, "dstr", half)
        mt = mix_to_gB(3, half, tr1[-1:])
        tg1 = proj_chan(mk.wB, 4096, t_wg, mt, st_gT, "dstg", half,
                        act=AF.Gelu)
        mk.setw("gB_free", tg1[-1])
    tldg = dma(b, "sync", mk.gA[:, 0:4 * S], st_gT.ap(), sem="dldg",
               waits=tg1[-1:] + mk.w("xlnT_full"))
    sAb = mk.sA.bitcast(BF16)
    cc_in = mk.dram_internal(f"attx_in_{tag}", (P, 4 * S), BF16)
    strip_done = None
    for h in range(8):
        pbase = (h % 2) * 64
        slot = h // 2
        tkk = dma(b, "gp", mk.wA[0:64, 0:4096].rearrange("q (ct o) -> q ct o", o=64),
                  st_kk.ap().rearrange("(ct q) (hh o) -> q ct hh o", q=64, o=64)[:, :, h],
                  sem="dktok", waits=tk2[-1:] + tg1[-1:] + mk.w("wAk"))
        tvv = dma(b, "gp", mk.wA[0:64, 4096:8192].rearrange("q (ct o) -> q ct o", o=64),
                  st_vv.ap().rearrange("(ct q) (hh o) -> q ct hh o", q=64, o=64)[:, :, h],
                  sem="dvtok", waits=tv1[-1:] + mk.w("wAk"))
        if h == 0:
            mk.setw("wBk", tg1[-1])
            mk.setw("wAk", tr1[-1])
        trr = dma(b, "sync", mk.wB[0:64, 0:4096],
                  st_rT.ap().rearrange("p (o t) -> p o t", o=4)[
                      (h % 2) * 64:(h % 2) * 64 + 64, h // 2],
                  sem="drh", waits=tr1[-1:] + mk.w("wBk"))
        tkr = dma(b, "sync", mk.wB[0:64, 4096:8192],
                  st_kT.ap().rearrange("p (o t) -> p o t", o=4)[
                      (h % 2) * 64:(h % 2) * 64 + 64, h // 2],
                  sem="dkh", waits=tk1[-1:] + mk.w("wBk"))
        dk2 = ps.take_f()
        du_row = mk.sm2[0:1, 228 + h * 2:230 + h * 2]
        tb2_ = mm_group(b, ps, dk2, [lambda e, o, s, p_, du_row=du_row: e.matmul(
            o[0:64, 0:2], mk.ones1f[0:1, 0:64], du_row, start=s, stop=p_)],
            extra_waits=[t_du] + mk.w("consts3"))
        tb3 = b.emit("vector", lambda e, kt=ps.t(dk2): e.tensor_copy(
            mk.sm2[0:64, 252:254], kt[0:64, 0:2]), waits=ps.read_waits(dk2), inc="v")
        ps.set_free(dk2, tb3)
        dcol = mk.sm2[0:64, 252:253]
        ucol = mk.sm2[0:64, 253:254]
        tb4 = b.emit("scalar", lambda e, dcol=dcol: e.activation(dcol, dcol, AF.Exp),
                     waits=[tb3], inc="s")
        tb5 = b.emit("vector", lambda e, dcol=dcol: e.tensor_scalar_mul(dcol, dcol, -1.0),
                     waits=[tb4], inc="v")
        tb6 = b.emit("scalar", lambda e, dcol=dcol: e.activation(
            mk.sm2[0:64, 254:255], xbf[0:64, 0:1], AF.Exp, scale=dcol),
            waits=[tb5, t_e64], inc="s")
        tb7 = b.emit("scalar", lambda e, dcol=dcol: e.activation(
            mk.sm2[0:64, 255:256], xbf[0:64, 1:2], AF.Exp, scale=dcol),
            waits=[tb6], inc="s")
        tb8 = b.emit("scalar", lambda e, dcol=dcol: e.activation(
            mk.sm2[0:64, 105:106], xbf[0:64, 2:3], AF.Exp, scale=dcol),
            waits=[tb7], inc="s")
        wkv_ = mk.sm2[0:64, 254:255]
        wbv_ = mk.sm2[0:64, 255:256]
        wsv_ = mk.sm2[0:64, 105:106]
        tw1 = b.emit("scalar", lambda e, dcol=dcol: e.activation(
            xbf[0:64, 136:200], xbf[0:64, 8:72], AF.Exp, scale=dcol),
            waits=[tb8, t_ET], inc="s")
        tw2 = b.emit("vector", lambda e: e.tensor_mul(
            xbf[0:64, 136:200], xbf[0:64, 136:200], mk.xb[0:64, 704:768]),
            waits=[tw1, t_MT], inc="v")
        tw3 = b.emit("vector", lambda e, ucol=ucol: e.scalar_tensor_tensor(
            xbf[0:64, 136:200], xbf[0:64, 72:136], ucol,
            xbf[0:64, 136:200], ALU.mult, ALU.add),
            waits=[tw2, t_EY], inc="v")
        twm = b.emit("vector", lambda e: e.tensor_copy(
            mk.tl[0:64, 512:576], xbf[0:64, 136:200]),
            waits=[tw3] + mk.w("tlwm"), inc="v")
        ts0 = b.emit("vector", lambda e: e.memset(xbf[0:64, 200:264], 0.0),
                     waits=[twm] + mk.w("Sbuf"), inc="v")
        ts1 = b.emit("vector", lambda e: e.memset(mk.tl[0:64, 576:640], 0.0),
                     waits=[ts0], inc="v")
        prev_state = ts1
        for ct in range(64):
            rsl = mk.wB[0:64, ct * 64:(ct + 1) * 64]
            ksl = mk.wB[0:64, 4096 + ct * 64:4096 + (ct + 1) * 64]
            ktk = mk.wA[0:64, ct * 64:(ct + 1) * 64]
            vtk = mk.wA[0:64, 4096 + ct * 64:4096 + (ct + 1) * 64]
            ak = ps.take_f()
            ta1 = mm_group(b, ps, ak, [lambda e, o, s, p_, ksl=ksl, rsl=rsl: e.matmul(
                o[0:64, 0:64], ksl, rsl, start=s, stop=p_)],
                extra_waits=[trr, tkr, tkk, tvv, prev_state])
            ta2 = b.emit("vector", lambda e, kt=ps.t(ak): e.tensor_mul(
                mk.tl[0:64, 640:704], kt[0:64, 0:64], mk.tl[0:64, 512:576]),
                waits=ps.read_waits(ak) + [twm] + mk.w("awfree"), inc="v")
            ps.set_free(ak, ta2)
            y1k = ps.take_f()
            ty1 = mm_group(b, ps, y1k, [lambda e, o, s, p_, vtk=vtk: e.matmul(
                o[0:64, 0:64], mk.tl[0:64, 640:704], vtk, start=s, stop=p_)],
                extra_waits=[ta2])
            y2k = ps.take_f()
            ty2 = mm_group(b, ps, y2k, [lambda e, o, s, p_, rsl=rsl: e.matmul(
                o[0:64, 0:64], rsl, mk.tl[0:64, 576:640], start=s, stop=p_)],
                extra_waits=[prev_state])
            te1 = b.emit("vector", lambda e, kt=ps.t(y2k), wbv_=wbv_: e.tensor_scalar(
                xbf[0:64, 264:328], kt[0:64, 0:64], wbv_, None, ALU.mult),
                waits=ps.read_waits(y2k) + [tb8] + mk.w("ytmp"), inc="v")
            ps.set_free(y2k, te1)
            sdst = sAb[(ct % 2) * 64:(ct % 2) * 64 + 64,
                       (ct // 2) * P + (h % 2) * 64:(ct // 2) * P + (h % 2) * 64 + 64]
            te2 = b.emit("vector", lambda e, kt=ps.t(y1k), sdst=sdst: e.tensor_tensor(
                sdst, kt[0:64, 0:64], xbf[0:64, 264:328], ALU.add),
                waits=ps.read_waits(y1k) + [te1] + mk.w("stripw"), inc="v")
            ps.set_free(y1k, te2)
            mk.setw("ytmp", te2)
            tk_ = b.emit("vector", lambda e, ktk=ktk, wkv_=wkv_: e.tensor_scalar(
                mk.tl[0:64, 704:768], ktk, wkv_, None, ALU.mult),
                waits=[tb6, tkk] + mk.w("kwfree"), inc="v")
            sk_ = ps.take_f()
            ts2 = mm_group(b, ps, sk_, [lambda e, o, s, p_, vtk=vtk: e.matmul(
                o[0:64, 0:64], mk.tl[0:64, 704:768], vtk, start=s, stop=p_)],
                extra_waits=[tk_])
            mk.setw("kwfree", ts2)
            ts3 = b.emit("vector", lambda e, kt=ps.t(sk_), wsv_=wsv_:
                         e.scalar_tensor_tensor(
                             xbf[0:64, 200:264], xbf[0:64, 200:264], wsv_,
                             kt[0:64, 0:64], ALU.mult, ALU.add),
                         waits=ps.read_waits(sk_) + [te2], inc="v")
            ps.set_free(sk_, ts3)
            ts4 = b.emit("vector", lambda e: e.tensor_copy(
                mk.tl[0:64, 576:640], xbf[0:64, 200:264]), waits=[ts3], inc="v")
            prev_state = ts4
            mk.setw("awfree", ty1)
        mk.setw("Sbuf", prev_state)
        mk.setw("wAk", prev_state)
        mk.setw("wBk", prev_state)
        mk.setw("tlwm", prev_state)
        gtoks = []
        for tt in range(32):
            ssl = sAb[:, tt * P + (h % 2) * 64:tt * P + (h % 2) * 64 + 64]
            g1_ = b.emit("vector", lambda e, ssl=ssl: e.bn_stats(
                mk.sm2[:, 96:102], ssl),
                waits=[prev_state, te2] + mk.w("gnst"), inc="v")
            g2_ = b.emit("vector", lambda e: e.bn_aggr(mk.sm2[:, 102:104],
                                                       mk.sm2[:, 96:102]),
                         waits=[g1_], inc="v")
            g3_ = b.emit("scalar", lambda e: e.activation(
                mk.sm2[:, 103:104], mk.sm2[:, 103:104], AF.Sqrt, bias=cf(mk, 64e-5)),
                waits=[g2_] + mk.w("consts3"), inc="s")
            g4_ = b.emit("vector", lambda e: e.reciprocal(
                mk.sm2[:, 103:104], mk.sm2[:, 103:104]), waits=[g3_], inc="v")
            g5_ = b.emit("vector", lambda e, ssl=ssl: e.tensor_scalar(
                ssl, ssl, mk.sm2[:, 102:103], mk.sm2[:, 103:104],
                ALU.subtract, ALU.mult), waits=[g4_], inc="v")
            mk.setw("gnst", g5_)
            gtoks.append(g5_)
        mk.setw("stripw", gtoks[-1])
        strip_done = gtoks[-1]
        if h % 2 == 1:
            pairi = h // 2
            lnw_ap = mk.sm2[:, 244 + pairi:245 + pairi]
            lnb_ap = mk.sm2[:, 248 + pairi:249 + pairi]
            dtoks = []
            for tt in range(32):
                tk3 = ps.take_bf()
                bank = ps.t(tk3)
                tt1 = b.emit("tensor", lambda e, bank=bank, tt=tt: e.transpose(
                    bank[:, 0:P], sAb[:, tt * P:(tt + 1) * P], mk.ident[:, :]),
                    waits=[strip_done] + ps.write_waits(tk3) + mk.w("consts1"),
                    inc="t")
                ps.set_ready(tk3, tt1)
                tt2 = b.emit("vector", lambda e, bank=bank, tt=tt, lnw_ap=lnw_ap,
                             lnb_ap=lnb_ap: e.tensor_scalar(
                                 sAb[:, tt * P:(tt + 1) * P], bank[:, 0:P],
                                 lnw_ap, lnb_ap, ALU.mult, ALU.add),
                             waits=ps.read_waits(tk3) + [t_lw], inc="v")
                ps.set_free(tk3, tt2)
                tt3 = b.emit("vector", lambda e, tt=tt, pairi=pairi: e.tensor_mul(
                    sAb[:, tt * P:(tt + 1) * P], sAb[:, tt * P:(tt + 1) * P],
                    mk.gA[:, pairi * S + tt * P:pairi * S + (tt + 1) * P]),
                    waits=[tt2, tldg], inc="v")
                dtoks.append(tt3)
            tdma = dma(b, "sync", cc_in.ap().rearrange("p (a t) -> p a t", a=4)[:, pairi],
                       sAb[:, 0:S], sem="dccin", waits=dtoks)
            mk.setw("stripw", tdma)
            mk.setw("ccin_done", tdma)
    cc_out = mk.dram_internal(f"attx_out_{tag}", (2 * P, 4 * S), BF16)
    t2 = cc.allgather_pair(cc_in, cc_out, mk.tok["ccin_done"])
    get_offreg(mk, "sync", 0)
    get_offreg(mk, "sync", 1)
    tback = None
    for r in range(2):
        dstv = mk.gA[:, r * 4 * TH:(r + 1) * 4 * TH].rearrange("p (a t) -> p a t", a=4)

        def fnback(e, r=r, dstv=dstv, cc_out=cc_out):
            srcap = bass.AP(cc_out.ap().tensor, mk._regs[("sync", r)],
                            [[4 * S, P], [S, 4], [1, TH]])
            return e.dma_start(out=dstv, in_=srcap)
        tback = b.emit("sync", fnback, waits=[t2], inc="dab", inc_n=16)
    t_wo = load_w_kf(mk, "l3ab_woT", mk.wA, NC, 0, D, D, sem="dwA",
                     waits=[ta1, ts2, tkk, tvv])
    last = None
    for i in range(NT):
        for cch in range(2):
            yk = ps.take_f()
            mms = [lambda e, o, s, p_, k=k, i=i, cch=cch: e.matmul(
                o[:, :], mk.gA[:, k * TH + i * P:k * TH + (i + 1) * P],
                mk.wA[:, k * D + cch * 512:k * D + (cch + 1) * 512],
                start=s, stop=p_) for k in range(NC)]
            tmy = mm_group(b, ps, yk, mms, extra_waits=[tback, t_wo])
            xa = mk.xt[:, i * D + cch * 512:i * D + cch * 512 + 512]
            tev = b.emit("vector", lambda e, xa=xa, yt=ps.t(yk): e.tensor_add(
                xa, xa, yt[:, :]), waits=ps.read_waits(yk) + mk.w(f"xt{i}"), inc="v")
            ps.set_free(yk, tev)
            mk.setw(f"xt{i}", tev)
            last = tev
    mk.setw("xt", last)
    mk.setw("wA_free", tmy)
    mk.setw("wB_free", tmy)
    mk.setw("gB_free", last)
    mk.setw("sA_free", last)
    mk.setw("tl_free", last)
    mk.setw("xb_free", last)
    mk.setw("xlnT", None)
    return [last]


def pack_l3_attn_b(inp, core):
    par = core % 2
    hb = par * 8

    def pk_(v):
        return np.asarray(v).reshape(8, 128).T.astype(np.float32)

    maa = np.concatenate([pk_(inp["L3_attn_b_time_maa_k"]),
                          pk_(inp["L3_attn_b_time_maa_v"]),
                          pk_(inp["L3_attn_b_time_maa_r"]),
                          pk_(inp["L3_attn_b_time_maa_g"])], axis=1)
    du = np.zeros((1, 16), np.float32)
    du[0, 0::2] = inp["L3_attn_b_time_decay"][hb:hb + 8, 0]
    du[0, 1::2] = inp["L3_attn_b_time_faaaa"][hb:hb + 8, 0]
    q = np.arange(64)
    qc = np.stack([63 - q, q, np.full(64, 64)], 1).astype(np.float32)
    i_idx = q[:, None]
    j_idx = q[None, :]
    ET = np.maximum(i_idx - j_idx - 1, 0).astype(np.float32).T
    MT = (i_idx > j_idx).astype(np.float32).T.astype(BF)
    eye = np.eye(64, dtype=np.float32)
    lnw = inp["L3_attn_b_ln_w"].reshape(16, 64)
    lnb = inp["L3_attn_b_ln_b"].reshape(16, 64)
    lnwp = np.zeros((P, 4), np.float32)
    lnbp = np.zeros((P, 4), np.float32)
    for a in range(4):
        lnwp[0:64, a] = lnw[hb + 2 * a]
        lnwp[64:128, a] = lnw[hb + 2 * a + 1]
        lnbp[0:64, a] = lnb[hb + 2 * a]
        lnbp[64:128, a] = lnb[hb + 2 * a + 1]
    o0 = hb * 64
    return {
        "l3ab_maa": np.ascontiguousarray(maa),
        "l3ab_du": du,
        "l3ab_qc": np.ascontiguousarray(qc),
        "l3ab_ET": np.ascontiguousarray(ET),
        "l3ab_MTb": np.ascontiguousarray(MT),
        "l3ab_eye": eye,
        "l3ab_lnwb": np.ascontiguousarray(np.concatenate([lnwp, lnbp], 1)),
        "l3ab_wrT": np.ascontiguousarray(inp["L3_attn_b_Wr"].T[:, o0:o0 + 512]),
        "l3ab_wkT": np.ascontiguousarray(inp["L3_attn_b_Wk"].T[:, o0:o0 + 512]),
        "l3ab_wvT": np.ascontiguousarray(inp["L3_attn_b_Wv"].T[:, o0:o0 + 512]),
        "l3ab_wgT": np.ascontiguousarray(inp["L3_attn_b_Wg"].T[:, o0:o0 + 512]),
        "l3ab_woT": np.ascontiguousarray(inp["L3_attn_b_Wo"].T),
        "c_offs": np.array([[par * TH, par * TH + P * 4 * S, par * TH + P * 8 * S] + [0] * 5], np.int32),
    }


register("l3_attn_b", emit_l3_attn_b, pack_l3_attn_b)


def transpose_x_to_gA(mk, waits=()):
    b = mk.b
    av = mk.gA[:, 0:NC * TH].rearrange("p (j t) -> p j t", j=NC)
    last = None
    for i in range(NT):
        ta = b.emit("vector", lambda e, i=i: e.tensor_copy(
            mk.tl[:, 0:D], mk.xt[:, i * D:(i + 1) * D]),
            waits=list(waits) + mk.w("tl_free") + mk.w("xt") + mk.w(f"xt{i}"),
            inc="v")
        dstv = av[:, :, i * P:(i + 1) * P]
        ttok, last = transpose_tile_to(mk, mk.tl[:, 0:D], mk.gA, dstv, i, ta)
        mk.setw("tl_free", ttok)
    mk.setw("xlnT", last)
    return last


def ln_affine_inplace(mk, gcol, bcol, eps=1e-5):
    b = mk.b
    ln_stats(mk, mk.xt, NT, eps=eps)
    last = None
    for i in range(NT):
        m = mk.sm[:, 2 * i:2 * i + 1]
        r = mk.sm[:, 2 * i + 1:2 * i + 2]
        t1 = b.emit("vector", lambda e, i=i, m=m, r=r: e.tensor_scalar(
            mk.xt[:, i * D:(i + 1) * D], mk.xt[:, i * D:(i + 1) * D], m, r,
            ALU.subtract, ALU.mult), waits=mk.w("lnstat") + mk.w(f"xt{i}"), inc="v")
        t2 = b.emit("vector", lambda e, i=i: e.tensor_mul(
            mk.xt[:, i * D:(i + 1) * D], mk.xt[:, i * D:(i + 1) * D],
            mk.sA[:, 0:1024]), waits=[t1] + mk.w("sAgb"), inc="v")
        t3 = b.emit("vector", lambda e, i=i: e.tensor_add(
            mk.xt[:, i * D:(i + 1) * D], mk.xt[:, i * D:(i + 1) * D],
            mk.sA[:, 1024:2048]), waits=[t2], inc="v")
        mk.setw(f"xt{i}", t3)
        last = t3
    mk.setw("xt", last)
    return last


def bcast_row_to_sA(mk, name, dst_off, prev=()):
    b, ps = mk.b, mk.ps
    t0 = dma(b, "gp", mk.sA[0:1, dst_off:dst_off + 1024], mk.inp(name),
             sem=f"dbr{dst_off}",
             waits=list(prev) + mk.w("xbrow") + mk.w("sA_free") + mk.w("xt")
             + join_all(mk))
    toks = []
    for cch in range(2):
        kk = ps.take_f()
        t1 = mm_group(b, ps, kk, [lambda e, o, s, p_, cch=cch: e.matmul(
            o[:, :], mk.ones1f[0:1, :],
            mk.sA[0:1, dst_off + cch * 512:dst_off + (cch + 1) * 512],
            start=s, stop=p_)], extra_waits=[t0] + mk.w("consts3"))
        t2 = b.emit("vector", lambda e, kt=ps.t(kk), cch=cch: e.tensor_copy(
            mk.sA[:, dst_off + cch * 512:dst_off + (cch + 1) * 512], kt[:, :]),
            waits=ps.read_waits(kk) + mk.w("sAgb2"), inc="v")
        ps.set_free(kk, t2)
        toks.append(t2)
    mk.setw("xbrow", toks[-1])
    return toks[-1]


def emit_l2_attn_b(mk, tag):
    b, ps, cc = mk.b, mk.ps, mk.cc
    xsp = mk.dram_internal("xspill2", (TH, D), F32)
    xspv = xsp.ap().rearrange("(n p) c -> n p c", p=P)
    sp_toks = []
    for i in range(NT):
        sp_toks.append(dma(b, "sync", xspv[i], mk.xt[:, i * D:(i + 1) * D],
                           sem="dxsp2", waits=mk.w("xt") + mk.w(f"xt{i}") + join_all(mk)))
    ln_stats(mk, mk.xt, NT)
    for i in range(NT):
        m = mk.sm[:, 2 * i:2 * i + 1]
        r = mk.sm[:, 2 * i + 1:2 * i + 2]
        tz = b.emit("vector", lambda e, i=i, m=m, r=r: e.tensor_scalar(
            mk.xt[:, i * D:(i + 1) * D], mk.xt[:, i * D:(i + 1) * D], m, r,
            ALU.subtract, ALU.mult),
            waits=mk.w("lnstat") + sp_toks[-1:] + [sp_toks[i]], inc="v")
        mk.setw(f"xt{i}", tz)
    mk.setw("xt", tz)

    QT0, KT0, V0, PT0, ST0 = 0, 4096, 8192, 10272, 10784
    import os
    nlayers = int(os.environ.get("L2B_LAYERS", "4"))
    for l in range(nlayers):
        pre = f"l2b{l}"
        transpose_x_to_gA(mk)
        gather_xln_full(mk, f"{tag}_{l}")
        gAv = mk.gA
        t_wq = load_w_kf(mk, pre + "_wq", mk.wA, NC, 0, 512, 512, sem="dwA",
                         waits=mk.w("wA_free"))
        t_wk = load_w_kf(mk, pre + "_wk", mk.wB, NC, 0, 512, 512, sem="dwB",
                         waits=mk.w("wB_free"))
        t_wv = dma(b, "gp", mk.wB[:, 4096:8192].rearrange("p (k f) -> p k f", k=NC),
                   mk.inp(pre + "_wv").rearrange("(k p) f -> p k f", p=P),
                   sem="dwB2", waits=mk.w("wB_free"))
        t_bqkv = dma(b, "gp", mk.xb[0:1, 0:1536].rearrange("p (z f) -> p z f", z=3),
                     mk.inp(pre + "_bqkv"), sem="dxb", waits=mk.w("xb_free"))
        t_o512 = b.emit("vector", lambda e: e.memset(mk.tl[0:1, 384:896], 1.0),
                        waits=mk.w("tl_free") + mk.w("tlr"), inc="v")
        t_m = dma(b, "gp", mk.tl[:, 0:P], mk.inp("c_tril"), sem="dC2",
                  waits=[t_o512])
        cc_in = mk.dram_internal(f"attx_in_{tag}_{l}", (P, 4 * S), BF16)
        strip_toks = [None]
        for h in range(8):
            tq = tkv = None
            for chk in range(8):
                qk = ps.take_f()
                mms = [lambda e, o, s, p_, k=k, h=h, chk=chk: e.matmul(
                    o[0:64, :], mk.wA[:, k * 512 + h * 64:k * 512 + (h + 1) * 64],
                    gAv[:, k * S + chk * 512:k * S + (chk + 1) * 512], start=s, stop=p_)
                    for k in range(NC)]
                mms.append(lambda e, o, s, p_, h=h: e.matmul(
                    o[0:64, :], mk.xb[0:1, h * 64:(h + 1) * 64], mk.tl[0:1, 384:896],
                    start=s, stop=p_))
                tmm = mm_group(b, ps, qk, mms,
                               extra_waits=[t_wq, t_bqkv, t_o512] + mk.w("xlnT_full"))
                tq0 = b.emit("vector", lambda e, kt=ps.t(qk): e.tensor_scalar_min(
                    mk.sA[0:64, 0:512], kt[0:64, :], 0.0),
                    waits=ps.read_waits(qk) + mk.w("sAfm"), inc="v")
                tq1 = b.emit("scalar", lambda e: e.activation(
                    mk.sA[0:64, 0:512], mk.sA[0:64, 0:512], AF.Exp),
                    waits=[tq0], inc="s")
                tq2 = b.emit("vector", lambda e, kt=ps.t(qk): e.tensor_scalar_max(
                    mk.sA[0:64, 512:1024], kt[0:64, :], 0.0),
                    waits=[tq1], inc="v")
                tq = b.emit("vector", lambda e, chk=chk: e.tensor_add(
                    mk.gB[0:64, QT0 + chk * 512:QT0 + (chk + 1) * 512],
                    mk.sA[0:64, 0:512], mk.sA[0:64, 512:1024]),
                    waits=[tq2] + mk.w("gBq"), inc="v")
                ps.set_free(qk, tq2)
                mk.setw("sAfm", tq)
                kk2 = ps.take_f()
                mms = [lambda e, o, s, p_, k=k, h=h, chk=chk: e.matmul(
                    o[0:64, :], mk.wB[:, k * 512 + h * 64:k * 512 + (h + 1) * 64],
                    gAv[:, k * S + chk * 512:k * S + (chk + 1) * 512], start=s, stop=p_)
                    for k in range(NC)]
                mms.append(lambda e, o, s, p_, h=h: e.matmul(
                    o[0:64, :], mk.xb[0:1, 512 + h * 64:512 + (h + 1) * 64],
                    mk.tl[0:1, 384:896], start=s, stop=p_))
                tmk_ = mm_group(b, ps, kk2, mms, extra_waits=[t_wk, t_bqkv] + mk.w("xlnT_full"))
                tk0 = b.emit("vector", lambda e, kt=ps.t(kk2): e.tensor_scalar_min(
                    mk.sA[0:64, 0:512], kt[0:64, :], 0.0),
                    waits=ps.read_waits(kk2) + [tq], inc="v")
                tk1 = b.emit("scalar", lambda e: e.activation(
                    mk.sA[0:64, 0:512], mk.sA[0:64, 0:512], AF.Exp),
                    waits=[tk0], inc="s")
                tk2_ = b.emit("vector", lambda e, kt=ps.t(kk2): e.tensor_scalar_max(
                    mk.sA[0:64, 512:1024], kt[0:64, :], 0.0),
                    waits=[tk1], inc="v")
                tkv = b.emit("vector", lambda e, chk=chk: e.tensor_add(
                    mk.gB[0:64, KT0 + chk * 512:KT0 + (chk + 1) * 512],
                    mk.sA[0:64, 0:512], mk.sA[0:64, 512:1024]),
                    waits=[tk2_] + mk.w("gBq"), inc="v")
                ps.set_free(kk2, tk2_)
                mk.setw("sAfm", tkv)
            tvm = b.emit("vector", lambda e: e.memset(mk.gB[:, V0:V0 + 32 * 65], 1.0),
                         waits=mk.w("gBv"), inc="v")
            tv = None
            for it in range(32):
                vk = ps.take_f()
                mms = [lambda e, o, s, p_, k=k, h=h, it=it: e.matmul(
                    o[:, 0:64], gAv[:, k * S + it * P:k * S + (it + 1) * P],
                    mk.wB[:, 4096 + k * 512 + h * 64:4096 + k * 512 + (h + 1) * 64],
                    start=s, stop=p_) for k in range(NC)]
                mms.append(lambda e, o, s, p_, h=h: e.matmul(
                    o[:, 0:64], mk.ones1[0:1, 0:P],
                    mk.xb[0:1, 1024 + h * 64:1024 + (h + 1) * 64], start=s, stop=p_))
                tmv = mm_group(b, ps, vk, mms, extra_waits=[t_wv, t_bqkv] + mk.w("xlnT_full"))
                tv = b.emit("vector", lambda e, kt=ps.t(vk), it=it: e.tensor_copy(
                    mk.gB[:, V0 + it * 65:V0 + it * 65 + 64], kt[:, 0:64]),
                    waits=ps.read_waits(vk) + [tvm], inc="v")
                ps.set_free(vk, tv)
            ts0 = b.emit("vector", lambda e: e.memset(mk.sA[0:64, 1024:1089], 0.0),
                         waits=[tv, tq, tkv] + mk.w("KVbuf"), inc="v")
            ts1 = b.emit("vector", lambda e: e.memset(mk.tl[0:64, 896:961], 0.0),
                         waits=[ts0], inc="v")
            prev_state = ts1
            for ct in range(32):
                ok = ps.take_f()
                o_war = ps.write_waits(ok)
                t1_ = b.emit("tensor", lambda e, ct=ct, ok=ok: e.matmul(
                    ps.t(ok)[:, 0:65], mk.gB[0:64, QT0 + ct * P:QT0 + (ct + 1) * P],
                    mk.tl[0:64, 896:961], start=True, stop=False),
                    waits=[prev_state, tq] + o_war)
                sk = ps.take_f()
                ts_ = mm_group(b, ps, sk, [lambda e, o, s, p_, ct=ct: e.matmul(
                    o[:, 0:P], mk.gB[0:64, KT0 + ct * P:KT0 + (ct + 1) * P],
                    mk.gB[0:64, QT0 + ct * P:QT0 + (ct + 1) * P],
                    start=s, stop=p_)], extra_waits=[tkv])
                tm1 = b.emit("vector", lambda e, kt=ps.t(sk): e.tensor_mul(
                    mk.gB[:, PT0:PT0 + P], kt[:, 0:P], mk.tl[:, 0:P]),
                    waits=ps.read_waits(sk) + [t_m] + mk.w("ptile_free"), inc="v")
                ps.set_free(sk, tm1)
                t2_ = b.emit("tensor", lambda e, ct=ct, ok=ok: e.matmul(
                    ps.t(ok)[:, 0:65], mk.gB[:, PT0:PT0 + P],
                    mk.gB[:, V0 + ct * 65:V0 + (ct + 1) * 65],
                    start=False, stop=True), waits=[tm1, tv], inc="t")
                mk.setw("ptile_free", t2_)
                ps.set_ready(ok, t2_)
                tr1 = b.emit("vector", lambda e, ok=ok: e.tensor_scalar_add(
                    mk.sm2[:, 104:105], ps.t(ok)[:, 64:65], 1e-6),
                    waits=[t2_], inc="v")
                tr2 = b.emit("vector", lambda e: e.reciprocal(
                    mk.sm2[:, 104:105], mk.sm2[:, 104:105]), waits=[tr1], inc="v")
                sdst = mk.gB[:, ST0 + ct * P + (h % 2) * 64:ST0 + ct * P + (h % 2) * 64 + 64]
                tr3 = b.emit("vector", lambda e, ok=ok, sdst=sdst: e.tensor_scalar(
                    sdst, ps.t(ok)[:, 0:64], mk.sm2[:, 104:105], None, ALU.mult),
                    waits=[tr2] + mk.w("strip_war"), inc="v")
                ps.set_free(ok, tr3)
                strip_toks.append(tr3)
                tkx = ps.take_bf()
                bank = ps.t(tkx)
                tt1 = b.emit("tensor", lambda e, bank=bank, ct=ct: e.transpose(
                    bank[0:P, 0:64], mk.gB[0:64, KT0 + ct * P:KT0 + (ct + 1) * P],
                    mk.ident[0:64, 0:64]),
                    waits=[tkv] + ps.write_waits(tkx) + mk.w("consts1"), inc="t")
                ps.set_ready(tkx, tt1)
                tt2 = b.emit("vector", lambda e, bank=bank: e.tensor_copy(
                    mk.tl[:, 640:704], bank[0:P, 0:64]),
                    waits=ps.read_waits(tkx) + mk.w("ktokf"), inc="v")
                ps.set_free(tkx, tt2)
                kvk = ps.take_f()
                tkv2 = mm_group(b, ps, kvk, [lambda e, o, s, p_, ct=ct: e.matmul(
                    o[0:64, 0:65], mk.tl[:, 640:704],
                    mk.gB[:, V0 + ct * 65:V0 + (ct + 1) * 65], start=s, stop=p_)],
                    extra_waits=[tt2, tv])
                mk.setw("ktokf", tkv2)
                tu1 = b.emit("vector", lambda e, kt=ps.t(kvk): e.tensor_add(
                    mk.sA[0:64, 1024:1089], mk.sA[0:64, 1024:1089], kt[0:64, 0:65]),
                    waits=ps.read_waits(kvk) + [tr3], inc="v")
                ps.set_free(kvk, tu1)
                tu2 = b.emit("vector", lambda e: e.tensor_copy(
                    mk.tl[0:64, 896:961], mk.sA[0:64, 1024:1089]),
                    waits=[tu1], inc="v")
                prev_state = tu2
            mk.setw("gBq", tr3)
            mk.setw("gBv", tr3)
            mk.setw("KVbuf", prev_state)
            if h % 2 == 1:
                pairi = h // 2
                dtoks = []
                for it_g in range(32):
                    tk3 = ps.take_bf()
                    bank = ps.t(tk3)
                    tt = b.emit("tensor", lambda e, bank=bank, it_g=it_g: e.transpose(
                        bank[:, 0:P], mk.gB[:, ST0 + it_g * P:ST0 + (it_g + 1) * P],
                        mk.ident[:, :]),
                        waits=strip_toks[-1:] + ps.write_waits(tk3) + mk.w("consts1"),
                        inc="t")
                    ps.set_ready(tk3, tt)
                    tcp = b.emit("vector", lambda e, bank=bank, it_g=it_g: e.tensor_copy(
                        mk.gB[:, ST0 + it_g * P:ST0 + (it_g + 1) * P], bank[:, 0:P]),
                        waits=ps.read_waits(tk3), inc="v")
                    ps.set_free(tk3, tcp)
                    dtoks.append(tcp)
                tdma = dma(b, "sync",
                           cc_in.ap().rearrange("p (a t) -> p a t", a=4)[:, pairi],
                           mk.gB[:, ST0:ST0 + S], sem="dccin", waits=dtoks)
                mk.setw("strip_war", tdma)
                mk.setw("ccin_done", tdma)
        cc_out = mk.dram_internal(f"attx_out_{tag}_{l}", (2 * P, 4 * S), BF16)
        t2x = cc.allgather_pair(cc_in, cc_out, mk.tok["ccin_done"])
        get_offreg(mk, "sync", 0)
        get_offreg(mk, "sync", 1)
        tback = None
        for r in range(2):
            dstv = mk.gA[:, r * 4 * TH:(r + 1) * 4 * TH].rearrange("p (a t) -> p a t", a=4)

            def fnback(e, r=r, dstv=dstv, cc_out=cc_out):
                srcap = bass.AP(cc_out.ap().tensor, mk._regs[("sync", r)],
                                [[4 * S, P], [S, 4], [1, TH]])
                return e.dma_start(out=dstv, in_=srcap)
            tback = b.emit("sync", fnback, waits=[t2x], inc="dab", inc_n=16)
        t_wo = load_w_kf(mk, pre + "_wo", mk.wA, NC, 0, D, D, sem="dwA",
                         waits=[tq, tkv, tv])
        t_bo = dma(b, "gp", mk.wB[0:1, 0:D], mk.inp(pre + "_bo"), sem="dwB3",
                   waits=[tmk_])
        last = None
        for i in range(NT):
            for cch in range(2):
                yk = ps.take_f()
                mms = [lambda e, o, s, p_, k=k, i=i, cch=cch: e.matmul(
                    o[:, :], mk.gA[:, k * TH + i * P:k * TH + (i + 1) * P],
                    mk.wA[:, k * D + cch * 512:k * D + (cch + 1) * 512],
                    start=s, stop=p_) for k in range(NC)]
                mms.append(lambda e, o, s, p_, cch=cch: e.matmul(
                    o[:, :], mk.ones1[0:1, 0:P], mk.wB[0:1, cch * 512:(cch + 1) * 512],
                    start=s, stop=p_))
                tmy = mm_group(b, ps, yk, mms, extra_waits=[tback, t_wo, t_bo])
                xa = mk.xt[:, i * D + cch * 512:i * D + cch * 512 + 512]
                tev = b.emit("vector", lambda e, xa=xa, yt=ps.t(yk): e.tensor_add(
                    xa, xa, yt[:, :]), waits=ps.read_waits(yk) + mk.w(f"xt{i}"), inc="v")
                ps.set_free(yk, tev)
                mk.setw(f"xt{i}", tev)
                last = tev
        mk.setw("xt", last)
        mk.setw("wA_free", tmy)
        mk.setw("wB_free", tmy)
        mk.setw("gB_free", last)
        mk.setw("sA_free", last)
        mk.setw("xlnT", None)
        bcast_row_to_sA(mk, pre + "_g1", 0, prev=[last])
        bcast_row_to_sA(mk, pre + "_be1", 1024)
        mk.setw("sAgb", mk.tok["xbrow"])
        ln_affine_inplace(mk, 0, 1024)
        transpose_x_to_gA(mk)
        w1v = mk.inp(pre + "_w1").rearrange("(k p) f -> p k f", p=P)
        w2v = mk.inp(pre + "_w2").rearrange("(k p) f -> p k f", p=P)
        t_b1 = dma(b, "gp", mk.sm2[:, 96:128], mk.inp(pre + "_b1pk"), sem="db1",
                   waits=mk.w("lnstat"))
        t_b2 = dma(b, "gp", mk.xb[0:1, 0:1024], mk.inp(pre + "_b2"), sem="dxb",
                   waits=mk.w("xb_free") + [mk.tok["xbrow"]])
        hbase = 16384
        for g in range(4):
            tg0 = g * 512
            htoks = []
            for ft in range(32):
                wslot = mk.wA[:, (ft % 8) * 1024:(ft % 8) * 1024 + 1024]
                twl = dma(b, "gp", wslot.rearrange("p (k f) -> p k f", k=NC),
                          w1v[:, :, ft * P:(ft + 1) * P],
                          sem="dwA",
                          waits=mk.w(f"wAslot{ft % 8}") + [tmy] + mk.w("wAdn"))
                hk = ps.take_f()
                mms = [lambda e, o, s, p_, k=k, wslot=wslot, tg0=tg0: e.matmul(
                    o[:, :], wslot[:, k * P:(k + 1) * P],
                    mk.gA[:, k * TH + tg0:k * TH + tg0 + 512], start=s, stop=p_)
                    for k in range(NC)]
                tmm = mm_group(b, ps, hk, mms, extra_waits=[twl] + mk.w("xlnT"))
                mk.setw(f"wAslot{ft % 8}", tmm)
                b1ap = mk.sm2[:, 96 + ft:97 + ft]
                tev = b.emit("scalar", lambda e, hdst_ft=ft, kt=ps.t(hk), b1ap=b1ap,
                             tg0=tg0: e.activation(
                                 mk.gB[:, hdst_ft * 512:(hdst_ft + 1) * 512], kt[:, :],
                                 AF.Gelu, bias=b1ap),
                             waits=ps.read_waits(hk) + [t_b1] + mk.w("gBh"), inc="s")
                ps.set_free(hk, tev)
                htoks.append(tev)
            for nch in range(2):
                tw1 = dma(b, "gp", mk.wA.rearrange("p (k f) -> p k f", k=16),
                          w2v[:, 0:16, nch * 512:nch * 512 + 512],
                          sem="dwAd", waits=[b.barrier_tok("gpsimd", htoks[-1:] + [
                              mk.tok.get(f"wAslot{j}") for j in range(8)], "g")]
                          + mk.w("wAdn"))
                tw2 = dma(b, "gp", mk.wB.rearrange("p (k f) -> p k f", k=16),
                          w2v[:, 16:32, nch * 512:nch * 512 + 512],
                          sem="dwBd", waits=mk.w("wB_free") + [tmy])
                for tb_ in range(4):
                    i_glob = g * 4 + tb_
                    yk = ps.take_f()
                    mms = []
                    for k in range(32):
                        wsl = (mk.wA if k < 16 else mk.wB)
                        kk_ = k % 16
                        mms.append(lambda e, o, s, p_, k=k, wsl=wsl, kk_=kk_, tb_=tb_:
                                   e.matmul(o[:, :],
                                            mk.gB[:, k * 512 + tb_ * P:k * 512 + tb_ * P + P],
                                            wsl[:, kk_ * 512:(kk_ + 1) * 512],
                                            start=s, stop=p_))
                    mms.append(lambda e, o, s, p_, nch=nch: e.matmul(
                        o[:, :], mk.ones1[0:1, 0:P],
                        mk.xb[0:1, nch * 512:(nch + 1) * 512], start=s, stop=p_))
                    tmy2 = mm_group(b, ps, yk, mms, extra_waits=[tw1, tw2, t_b2]
                                    + htoks[-1:])
                    xa = mk.xt[:, i_glob * D + nch * 512:i_glob * D + nch * 512 + 512]
                    tev = b.emit("vector", lambda e, xa=xa, yt=ps.t(yk): e.tensor_add(
                        xa, xa, yt[:, :]),
                        waits=ps.read_waits(yk) + mk.w(f"xt{i_glob}"), inc="v")
                    ps.set_free(yk, tev)
                    mk.setw(f"xt{i_glob}", tev)
                    mk.setw("wAdn", tmy2)
                    mk.setw("wB_free", tmy2)
                    last = tev
            mk.setw("gBh", tmy2)
            for j in range(8):
                mk.setw(f"wAslot{j}", tmy2)
        mk.setw("xt", last)
        mk.setw("wA_free", tmy2)
        mk.setw("gB_free", last)
        bcast_row_to_sA(mk, pre + "_g2", 0, prev=[last])
        bcast_row_to_sA(mk, pre + "_be2", 1024)
        mk.setw("sAgb", mk.tok["xbrow"])
        ln_affine_inplace(mk, 0, 1024)
    bcast_row_to_sA(mk, "l2b_gf", 0)
    bcast_row_to_sA(mk, "l2b_bef", 1024)
    mk.setw("sAgb", mk.tok["xbrow"])
    ln_affine_inplace(mk, 0, 1024)
    last = None
    for i in range(NT):
        trl = dma(b, "sync", mk.sA[:, 0:1024], xspv[i], sem="dxrl3",
                  waits=mk.w("sAgb2") + mk.w("xt") + mk.w(f"xt{i}") + mk.w("sA_free2"))
        last = b.emit("vector", lambda e, i=i: e.tensor_add(
            mk.xt[:, i * D:(i + 1) * D], mk.xt[:, i * D:(i + 1) * D],
            mk.sA[:, 0:1024]), waits=[trl] + mk.w(f"xt{i}"), inc="v")
        mk.setw(f"xt{i}", last)
        mk.setw("sA_free2", last)
    mk.setw("xt", last)
    mk.setw("sA_free", last)
    mk.setw("sAgb2", last)
    return [last]


def pack_l2_attn_b(inp, core):
    par = core % 2
    hb = par * 8
    d = {"c_tril": np.triu(np.ones((P, P))).astype(BF),
         "c_offs": np.array([[par * TH, par * TH + P * 4 * S, par * TH + P * 8 * S] + [0] * 5], np.int32),
         "l2b_gf": inp["L2_attn_b_gf"].reshape(1, D).astype(np.float32),
         "l2b_bef": inp["L2_attn_b_bef"].reshape(1, D).astype(np.float32)}
    for l in range(4):
        pre = f"l2b{l}"
        d[pre + "_wq"] = np.ascontiguousarray(inp["L2_attn_b_Wq"][l][:, hb * 64:(hb + 8) * 64])
        d[pre + "_wk"] = np.ascontiguousarray(inp["L2_attn_b_Wk"][l][:, hb * 64:(hb + 8) * 64])
        d[pre + "_wv"] = np.ascontiguousarray(inp["L2_attn_b_Wv"][l][:, hb * 64:(hb + 8) * 64])
        d[pre + "_bqkv"] = np.stack([inp["L2_attn_b_bq"][l][hb * 64:(hb + 8) * 64],
                                     inp["L2_attn_b_bk"][l][hb * 64:(hb + 8) * 64],
                                     inp["L2_attn_b_bv"][l][hb * 64:(hb + 8) * 64]])[None]
        d[pre + "_wo"] = np.ascontiguousarray(inp["L2_attn_b_Wo"][l])
        d[pre + "_bo"] = inp["L2_attn_b_bo"][l].reshape(1, D)
        d[pre + "_w1"] = np.ascontiguousarray(inp["L2_attn_b_W1"][l])
        d[pre + "_b1pk"] = np.ascontiguousarray(inp["L2_attn_b_b1"][l].reshape(32, P).T)
        d[pre + "_w2"] = np.ascontiguousarray(inp["L2_attn_b_W2"][l])
        d[pre + "_b2"] = inp["L2_attn_b_b2"][l].reshape(1, D)
        d[pre + "_g1"] = inp["L2_attn_b_g1"][l].reshape(1, D).astype(np.float32)
        d[pre + "_be1"] = inp["L2_attn_b_be1"][l].reshape(1, D).astype(np.float32)
        d[pre + "_g2"] = inp["L2_attn_b_g2"][l].reshape(1, D).astype(np.float32)
        d[pre + "_be2"] = inp["L2_attn_b_be2"][l].reshape(1, D).astype(np.float32)
    return d


register("l2_attn_b", emit_l2_attn_b, pack_l2_attn_b)



def emit_l2_attn_a(mk, tag):
    b, ps, cc = mk.b, mk.ps, mk.cc
    ln_and_transpose(mk)
    gather_xln_full(mk, tag)
    gAv = mk.gA
    gBf = mk.sB
    xbf = mk.xb.bitcast(F32)
    sAb = mk.sA.bitcast(BF16)
    xtb = mk.xt.bitcast(BF16)
    xsp = mk.dram_internal("xspill3", (TH, D), F32)
    xspv = xsp.ap().rearrange("(n p) c -> n p c", p=P)
    sp_toks = []
    for i in range(NT):
        sp_toks.append(dma(b, "sync", xspv[i], mk.xt[:, i * D:(i + 1) * D],
                           sem="dxsp3",
                           waits=mk.w("xt") + mk.w(f"xt{i}") + join_all(mk)))
    t_cc = dma(b, "gp", xbf[:, 128:200], mk.inp("l2aa_cst"), sem="dcst",
               waits=mk.w("xb_free") + join_all(mk))
    t_al = dma(b, "gp", xbf[:, 0:128], mk.inp("l2aa_alog"), sem="dal",
               waits=mk.w("xb_free"))
    t_a2 = b.emit("scalar", lambda e: e.activation(xbf[:, 0:128], xbf[:, 0:128],
                                                   AF.Exp), waits=[t_al], inc="s")
    t_a3 = b.emit("vector", lambda e: e.tensor_scalar_mul(
        xbf[:, 0:128], xbf[:, 0:128], -1.0), waits=[t_a2], inc="v")
    t_bxp = dma(b, "gp", xbf[0:32, 200:201], mk.inp("l2aa_bxp"), sem="dbxp",
                waits=mk.w("xb_free"))
    carry0 = b.emit("vector", lambda e: e.memset(xbf[:, 208:336], 0.0),
                    waits=mk.w("xb_free"), inc="v")
    w_in_v = mk.inp("l2aa_win").rearrange("(k p) f -> p k f", p=P)
    xc_toks = []
    tmm_a = None
    for dch in range(8):
        twl = dma(b, "gp",
                  mk.wA[:, dch * 1024:(dch + 1) * 1024].rearrange(
                      "p (k f) -> p k f", k=NC),
                  w_in_v[:, :, dch * P:(dch + 1) * P],
                  sem="dwin", waits=mk.w("wA_free"))
        et = []
        for tch in range(8):
            pk = ps.take_f()
            mms = [lambda e, o, s, p_, k=k, dch=dch, tch=tch: e.matmul(
                o[:, :], mk.wA[:, dch * 1024 + k * P:dch * 1024 + (k + 1) * P],
                gAv[:, k * S + tch * 512:k * S + (tch + 1) * 512],
                start=s, stop=p_) for k in range(NC)]
            tmm_a = mm_group(b, ps, pk, mms,
                             extra_waits=[twl] + mk.w("xlnT_full"))
            tev = b.emit("scalar", lambda e, kt=ps.t(pk), tch=tch, dch=dch:
                         e.activation(gBf[:, tch * 512:(tch + 1) * 512],
                                      kt[:, :], AF.Identity,
                                      bias=xbf[:, 176 + dch:177 + dch]),
                         waits=ps.read_waits(pk) + [t_cc] + mk.w("gBxin")
                         + mk.w("gB_free"), inc="s")
            ps.set_free(pk, tev)
            et.append(tev)
        c0 = b.emit("vector", lambda e, dch=dch: e.tensor_scalar(
            gBf[:, 4096:8192], gBf[:, 0:4096],
            xbf[:, 136 + dch * 4 + 3:136 + dch * 4 + 4], None, ALU.mult),
            waits=et + [t_cc] + mk.w("gBcv"), inc="v")
        for kk_ in range(3):
            sh = 3 - kk_
            c0 = b.emit("vector", lambda e, kk_=kk_, sh=sh, dch=dch:
                        e.scalar_tensor_tensor(
                            gBf[:, 4096 + sh:8192], gBf[:, 0:4096 - sh],
                            xbf[:, 136 + dch * 4 + kk_:137 + dch * 4 + kk_],
                            gBf[:, 4096 + sh:8192], ALU.mult, ALU.add),
                        waits=[c0], inc="v")
        tsl = b.emit("scalar", lambda e, dch=dch: e.activation(
            xtb[:, dch * S:(dch + 1) * S], gBf[:, 4096:8192], AF.Silu,
            bias=xbf[:, 168 + dch:169 + dch]),
            waits=[c0, sp_toks[-1]], inc="s")
        mk.setw("gBxin", tsl)
        mk.setw("gBcv", tsl)
        xc_toks.append(tsl)
    mk.setw("wA_free", tmm_a)
    xc_in = mk.dram_internal(f"xc_in_{tag}", (P, 8 * S), BF16)
    xc_out = mk.dram_internal(f"xc_out_{tag}", (2 * P, 8 * S), BF16)
    tg1 = dma(b, "sync", xc_in.ap(), xtb[:, 0:8 * S], sem="dxcg", waits=xc_toks)
    tg2 = cc.allgather_pair(xc_in, xc_out, tg1)
    xcv = xc_out.ap().rearrange("(r p) (dt t) -> p r dt t", p=P, dt=8)
    bc_st = mk.dram_internal(f"bc_{tag}", (32, S), F32)
    dl_st = mk.dram_internal(f"dl_{tag}", (1024, S), BF16)
    t_wxp = dma(b, "gp", mk.wB[:, 0:512].rearrange("p (k f) -> p k f", k=16),
                mk.inp("l2aa_wxp").rearrange("(k p) f -> p k f", p=P),
                sem="dwxp", waits=mk.w("wB_free"))
    wdt_v = mk.inp("l2aa_wdt").rearrange("(k p) f -> p k f", p=P)
    mk.setw("gBrhs", xc_toks[-1])
    tbm = tdm = tbd_last = tdd_last = None
    for tch in range(8):
        for r_ in range(2):
            trh = dma(b, "sync",
                      mk.gB[:, r_ * 4096:(r_ + 1) * 4096].rearrange(
                          "p (dt f) -> p dt f", dt=8),
                      xcv[:, r_, :, tch * 512:(tch + 1) * 512],
                      sem="dxcr", waits=[tg2] + mk.w("gBrhs"))
        bk = ps.take_f()
        mms = [lambda e, o, s, p_, k=k: e.matmul(
            o[0:32, :], mk.wB[:, k * 32:(k + 1) * 32],
            mk.gB[:, k * 512:(k + 1) * 512], start=s, stop=p_)
            for k in range(16)]
        tbm = mm_group(b, ps, bk, mms, extra_waits=[t_wxp, trh])
        tbe = b.emit("scalar", lambda e, kt=ps.t(bk): e.activation(
            mk.sA[0:32, 0:512], kt[0:32, :], AF.Identity,
            bias=xbf[0:32, 200:201]),
            waits=ps.read_waits(bk) + [t_bxp] + mk.w("sAbc"), inc="s")
        ps.set_free(bk, tbe)
        tbd = dma(b, "sync", bc_st.ap()[:, tch * 512:(tch + 1) * 512],
                  mk.sA[0:32, 0:512], sem="dbc", waits=[tbe])
        mk.setw("sAbc", tbd)
        tbd_last = tbd
        for dch in range(8):
            slot = dch % 4
            twd = dma(b, "gp",
                      mk.wA[:, slot * 2048:(slot + 1) * 2048].rearrange(
                          "p (k f) -> p k f", k=16),
                      wdt_v[:, :, dch * P:(dch + 1) * P],
                      sem="dwdt",
                      waits=mk.w(f"wdtslot{slot}") + ([tmm_a] if tch == 0 and dch < 4 else []))
            dk = ps.take_f()
            mms = [lambda e, o, s, p_, k=k, slot=slot: e.matmul(
                o[:, :], mk.wA[:, slot * 2048 + k * P:slot * 2048 + (k + 1) * P],
                mk.gB[:, k * 512:(k + 1) * 512], start=s, stop=p_)
                for k in range(16)]
            tdm = mm_group(b, ps, dk, mms, extra_waits=[twd, trh])
            mk.setw(f"wdtslot{slot}", tdm)
            sp1 = b.emit("scalar", lambda e, kt=ps.t(dk), dch=dch: e.activation(
                mk.sA[:, 512:1024], kt[:, :], AF.Identity,
                bias=xbf[:, 184 + dch:185 + dch]),
                waits=ps.read_waits(dk) + [t_cc] + mk.w("sAdl"), inc="s")
            ps.set_free(dk, sp1)
            sp2 = b.emit("scalar", lambda e: e.activation(
                mk.sA[:, 1024:1536], mk.sA[:, 512:1024], AF.Abs),
                waits=[sp1], inc="s")
            sp3 = b.emit("scalar", lambda e: e.activation(
                mk.sA[:, 1024:1536], mk.sA[:, 1024:1536], AF.Exp, scale=-1.0),
                waits=[sp2], inc="s")
            sp4 = b.emit("scalar", lambda e: e.activation(
                mk.sA[:, 1024:1536], mk.sA[:, 1024:1536], AF.Ln,
                bias=cf(mk, 1.0)), waits=[sp3] + mk.w("consts3"), inc="s")
            sp5 = b.emit("vector", lambda e: e.tensor_scalar_max(
                mk.sA[:, 512:1024], mk.sA[:, 512:1024], 0.0),
                waits=[sp4], inc="v")
            sp6 = b.emit("vector", lambda e: e.tensor_add(
                sAb[:, 3072:3584], mk.sA[:, 512:1024], mk.sA[:, 1024:1536]),
                waits=[sp5], inc="v")
            tdd = dma(b, "sync",
                      dl_st.ap().rearrange("(dc p) t -> dc p t", p=P)[dch][
                          :, tch * 512:(tch + 1) * 512],
                      sAb[:, 3072:3584], sem="ddl", waits=[sp6])
            mk.setw("sAdl", tdd)
            tdd_last = tdd
        mk.setw("gBrhs", tdm)
    mk.setw("wB_free", tbm)
    t_wz = dma(b, "gp", mk.wA[:, 0:8192].rearrange("p (k f) -> p k f", k=NC),
               mk.inp("l2aa_wz").rearrange("(k p) f -> p k f", p=P),
               sem="dwz", waits=[tdm])
    bc_v = bc_st.ap()
    dlv = dl_st.ap().rearrange("(dc p) t -> dc p t", p=P)
    y_in = mk.dram_internal(f"y_in_{tag}", (P, 8 * S), BF16)
    y_inv = y_in.ap().rearrange("p (dc t) -> p dc t", dc=8)
    y_toks = []
    p_d1 = p_tu = p_d2 = p_tyd = p_ty2 = p_trm = None
    rep_war = [tdm]
    tzm = None
    for tq in range(8):
        tq0 = tq * 512
        rep_toks = []
        for s_i in range(16):
            for bc in range(2):
                trow = dma(b, "gp", mk.sA[0:1, 0:512],
                           bc_v[bc * 16 + s_i:bc * 16 + s_i + 1,
                                tq0:tq0 + 512],
                           sem="dbrow",
                           waits=mk.w("sArow") + ([tbd_last, p_d1, p_tu]
                                                  if s_i == 0 and bc == 0 else []))
                rk = ps.take_f()
                trm = mm_group(b, ps, rk, [lambda e, o, s, p_: e.matmul(
                    o[:, :], mk.ones1f[0:1, :], mk.sA[0:1, 0:512],
                    start=s, stop=p_)],
                    extra_waits=[trow] + mk.w("consts3"))
                p_trm = trm
                dst = mk.gB[:, bc * 8192 + s_i * 512:bc * 8192 + (s_i + 1) * 512]
                tre = b.emit("scalar", lambda e, kt=ps.t(rk), dst=dst:
                             e.activation(dst, kt[:, :], AF.Copy),
                             waits=ps.read_waits(rk) + rep_war, inc="s")
                ps.set_free(rk, tre)
                mk.setw("sArow", tre)
                rep_toks.append(tre)
        for dch in range(8):
            tds = dma(b, "sync", sAb[:, 0:512], dlv[dch][:, tq0:tq0 + 512],
                      sem="ddls",
                      waits=[tdd_last, p_d1, p_tu] + ([p_trm] if dch == 0 else []))
            tu = b.emit("vector", lambda e, dch=dch, tq0=tq0: e.tensor_mul(
                mk.sA[:, 512:1024], sAb[:, 0:512],
                xtb[:, dch * S + tq0:dch * S + tq0 + 512]),
                waits=[tds, xc_toks[-1], p_ty2], inc="v")
            p_tu = tu
            ykey = ps.reserve_f()
            ywar = ps.write_waits(ykey)
            ytok = None
            p_d3 = p_d4 = p_d5 = None
            for s_i in range(16):
                A_ap = xbf[:, dch * 16 + s_i:dch * 16 + s_i + 1]
                car = xbf[:, 208 + dch * 16 + s_i:209 + dch * 16 + s_i]
                d1 = b.emit("scalar", lambda e, A_ap=A_ap: e.activation(
                    mk.sA[:, 1024:1536], sAb[:, 0:512], AF.Exp, scale=A_ap),
                    waits=[tds, t_a3, p_d5, p_d4]
                    + ([p_ty2] if s_i == 0 else []), inc="s")
                p_d1 = d1
                d2 = b.emit("gpsimd", lambda e, s_i=s_i: e.tensor_mul(
                    mk.sA[:, 1536:2048], mk.sA[:, 512:1024],
                    mk.gB[:, s_i * 512:(s_i + 1) * 512]),
                    waits=[tu, rep_toks[-1], p_d3]
                    + ([p_tyd] if s_i == 0 else []), inc="g")
                p_d2 = d2
                d3 = b.emit("vector", lambda e, car=car: e.tensor_tensor_scan(
                    mk.sA[:, 1024:1536], mk.sA[:, 1024:1536],
                    mk.sA[:, 1536:2048], car, ALU.mult, ALU.add),
                    waits=[d1, d2, carry0], inc="v")
                p_d3 = d3
                d4 = b.emit("vector", lambda e, car=car: e.tensor_copy(
                    car, mk.sA[:, 1535:1536]), waits=[d3], inc="v")
                p_d4 = d4
                buf = (s_i % 2) * 512
                d5 = b.emit("vector", lambda e, s_i=s_i, buf=buf: e.tensor_mul(
                    mk.tl[:, buf:buf + 512], mk.sA[:, 1024:1536],
                    mk.gB[:, 8192 + s_i * 512:8192 + (s_i + 1) * 512]),
                    waits=[d3, rep_toks[-1]] + mk.w(f"tlhc{buf}"), inc="v")
                p_d5 = d5
                fn = (lambda e, buf=buf, s=(s_i == 0), p2=(s_i == 15), yk=ykey:
                      e.matmul(ps.t(yk)[:, :], mk.ident[:, :],
                               mk.tl[:, buf:buf + 512], start=s, stop=p2))
                w_ = [d5] + (ywar if s_i == 0 else []) + mk.w("consts1")
                ymm = b.emit("tensor", fn, waits=w_, inc="t")
                mk.setw(f"tlhc{buf}", ymm)
            ps.set_ready(ykey, ymm)
            zk = ps.take_f()
            mms = [lambda e, o, s, p_, k=k, dch=dch, tq0=tq0: e.matmul(
                o[:, :], mk.wA[:, k * 1024 + dch * P:k * 1024 + (dch + 1) * P],
                gAv[:, k * S + tq0:k * S + tq0 + 512], start=s, stop=p_)
                for k in range(NC)]
            tzm = mm_group(b, ps, zk, mms,
                           extra_waits=[t_wz] + mk.w("xlnT_full"))
            tze = b.emit("scalar", lambda e, kt=ps.t(zk), dch=dch: e.activation(
                mk.sA[:, 512:1024], kt[:, :], AF.Silu,
                bias=xbf[:, 192 + dch:193 + dch]),
                waits=ps.read_waits(zk) + [t_cc, p_d2], inc="s")
            ps.set_free(zk, tze)
            ty1 = b.emit("vector", lambda e, yk=ykey, dch=dch, tq0=tq0:
                         e.scalar_tensor_tensor(
                             mk.sA[:, 1024:1536],
                             xtb[:, dch * S + tq0:dch * S + tq0 + 512],
                             xbf[:, 128 + dch:129 + dch],
                             ps.t(yk)[:, :], ALU.mult, ALU.add),
                         waits=ps.read_waits(ykey) + [t_cc], inc="v")
            ps.set_free(ykey, ty1)
            ps.release_f(ykey)
            ty2 = b.emit("vector", lambda e: e.tensor_mul(
                sAb[:, 3072:3584], mk.sA[:, 1024:1536], mk.sA[:, 512:1024]),
                waits=[ty1, tze], inc="v")
            p_ty2 = ty2
            tyd = dma(b, "sync", y_inv[:, dch, tq0:tq0 + 512],
                      sAb[:, 3072:3584], sem="dyin", waits=[ty2])
            p_tyd = tyd
            y_toks.append(tyd)
        rep_war = [p_d2, p_d5]
    y_out = mk.dram_internal(f"y_out_{tag}", (2 * P, 8 * S), BF16)
    tga = cc.allgather_pair(y_in, y_out, y_toks[-1])
    get_offreg(mk, "sync", 0)
    get_offreg(mk, "sync", 2)
    tyb = None
    for r_ in range(2):
        dstv = mk.gA[:, r_ * 8 * TH:(r_ + 1) * 8 * TH].rearrange(
            "p (dc t) -> p dc t", dc=8)

        def fnf(e, y_out=y_out, r_=r_, dstv=dstv):
            src = bass.AP(y_out.ap().tensor, mk._regs[("sync", 2 * r_)],
                          [[8 * S, P], [S, 8], [1, TH]])
            return e.dma_start(out=dstv, in_=src)
        tyb = b.emit("sync", fnf, waits=[tga, tzm], inc="dyb",
                     inc_n=16)
    rl_toks = []
    for i in range(NT):
        rl_toks.append(dma(b, "sync", mk.xt[:, i * D:(i + 1) * D], xspv[i],
                           sem="dxrl5", waits=[p_ty2]))
    mk.setw("xt", rl_toks[-1])
    for i in range(NT):
        mk.setw(f"xt{i}", rl_toks[-1])
    t_wo = dma(b, "gp", mk.gB[:, 0:16384].rearrange("p (k f) -> p k f", k=16),
               mk.inp("l2aa_wout").rearrange("(k p) f -> p k f", p=P),
               sem="dwo", waits=[p_ty2])
    t_bo = dma(b, "gp", mk.tl[0:1, 0:1024], mk.inp("l2aa_bout"), sem="dbo",
               waits=mk.w("tlhc0") + mk.w("tlhc512"))
    ln_stats(mk, mk.xt, NT)
    t_ab = bcast_row_to_sA(mk, "l2aa_alpha", 1024, prev=[p_ty2])
    p_txa = None
    xt_toks = []
    tom = None
    for i in range(NT):
        t2s = []
        for cch in range(2):
            ok = ps.take_f()
            mms = [lambda e, o, s, p_, k=k, i=i, cch=cch: e.matmul(
                o[:, :], mk.gA[:, k * TH + i * P:k * TH + (i + 1) * P],
                mk.gB[:, k * 1024 + cch * 512:k * 1024 + (cch + 1) * 512],
                start=s, stop=p_) for k in range(16)]
            mms.append(lambda e, o, s, p_, cch=cch: e.matmul(
                o[:, :], mk.ones1[0:1, 0:P],
                mk.tl[0:1, cch * 512:(cch + 1) * 512], start=s, stop=p_))
            tom = mm_group(b, ps, ok, mms,
                           extra_waits=[tyb, t_wo, t_bo] + mk.w("consts2"))
            t1 = b.emit("vector", lambda e, i=i, cch=cch: e.tensor_scalar(
                mk.sA[:, cch * 512:(cch + 1) * 512],
                mk.xt[:, i * D + cch * 512:i * D + (cch + 1) * 512],
                mk.sm[:, 2 * i:2 * i + 1], mk.sm[:, 2 * i + 1:2 * i + 2],
                ALU.subtract, ALU.mult),
                waits=mk.w("lnstat") + [p_txa], inc="v")
            t2 = b.emit("vector", lambda e, cch=cch, kt=ps.t(ok): e.tensor_add(
                mk.sA[:, cch * 512:(cch + 1) * 512],
                mk.sA[:, cch * 512:(cch + 1) * 512], kt[:, :]),
                waits=[t1] + ps.read_waits(ok), inc="v")
            ps.set_free(ok, t2)
            t2s.append(t2)
        b1 = b.emit("vector", lambda e: e.bn_stats(mk.sm2[:, 200:206],
                                                   mk.sA[:, 0:512]),
                    waits=t2s, inc="v")
        b2 = b.emit("vector", lambda e: e.bn_stats(mk.sm2[:, 206:212],
                                                   mk.sA[:, 512:1024]),
                    waits=t2s + [b1], inc="v")
        b3 = b.emit("vector", lambda e: e.bn_aggr(mk.sm[:, 40:42],
                                                  mk.sm2[:, 200:212]),
                    waits=[b2], inc="v")
        s1 = b.emit("scalar", lambda e: e.activation(
            mk.sm[:, 41:42], mk.sm[:, 41:42], AF.Sqrt,
            scale=float(D) / float(D - 1)), waits=[b3], inc="s")
        s2 = b.emit("vector", lambda e: e.tensor_scalar_add(
            mk.sm[:, 41:42], mk.sm[:, 41:42], 1e-6), waits=[s1], inc="v")
        s3 = b.emit("vector", lambda e: e.reciprocal(
            mk.sm[:, 41:42], mk.sm[:, 41:42]), waits=[s2], inc="v")
        t4 = b.emit("vector", lambda e: e.tensor_scalar(
            mk.sA[:, 0:1024], mk.sA[:, 0:1024], mk.sm[:, 40:41],
            mk.sm[:, 41:42], ALU.subtract, ALU.mult), waits=[s3], inc="v")
        t5 = b.emit("vector", lambda e: e.tensor_mul(
            mk.sA[:, 0:1024], mk.sA[:, 0:1024], mk.sA[:, 1024:2048]),
            waits=[t4, t_ab], inc="v")
        txa = b.emit("vector", lambda e, i=i: e.tensor_add(
            mk.xt[:, i * D:(i + 1) * D], mk.xt[:, i * D:(i + 1) * D],
            mk.sA[:, 0:1024]), waits=[t5], inc="v")
        p_txa = txa
        xt_toks.append(txa)
    t_bb = bcast_row_to_sA(mk, "l2aa_beta", 1024, prev=[p_txa])
    last = None
    for i in range(NT):
        tb2 = b.emit("vector", lambda e, i=i: e.tensor_add(
            mk.xt[:, i * D:(i + 1) * D], mk.xt[:, i * D:(i + 1) * D],
            mk.sA[:, 1024:2048]), waits=[t_bb, xt_toks[i]], inc="v")
        mk.setw(f"xt{i}", tb2)
        last = tb2
    mk.setw("xt", last)
    mk.tok["xt_all"] = [last]
    mk.setw("wA_free", tzm)
    mk.setw("gB_free", tom)
    mk.setw("sA_free", last)
    mk.setw("tl_free", tom)
    mk.setw("xb_free", p_ty2)
    return [last]


def pack_l2_attn_a(inp, core):
    par = core % 2
    sl = slice(par * 1024, (par + 1) * 1024)
    W_in = inp["L2_attn_a_W_in"]
    b_in = inp["L2_attn_a_b_in"]
    dsk = inp["L2_attn_a_D_skip"][sl].reshape(8, P).T
    cw = inp["L2_attn_a_conv_w"][sl].reshape(8, P, 4).transpose(1, 0, 2).reshape(P, 32)
    cb = inp["L2_attn_a_conv_b"][sl].reshape(8, P).T
    binx = b_in[par * 1024:par * 1024 + 1024].reshape(8, P).T
    bdt = inp["L2_attn_a_b_dt"][sl].reshape(8, P).T
    binz = b_in[2048 + par * 1024:2048 + par * 1024 + 1024].reshape(8, P).T
    cst = np.concatenate([dsk, cw, cb, binx, bdt, binz], axis=1)
    alog = inp["L2_attn_a_A_log"][sl].reshape(8, P, 16).transpose(1, 0, 2).reshape(P, P)
    return {
        "l2aa_cst": np.ascontiguousarray(cst.astype(np.float32)),
        "l2aa_alog": np.ascontiguousarray(alog.astype(np.float32)),
        "l2aa_bxp": np.ascontiguousarray(
            inp["L2_attn_a_b_xp"].reshape(32, 1).astype(np.float32)),
        "l2aa_win": np.ascontiguousarray(W_in[:, par * 1024:par * 1024 + 1024]),
        "l2aa_wz": np.ascontiguousarray(
            W_in[:, 2048 + par * 1024:2048 + par * 1024 + 1024]),
        "l2aa_wxp": np.ascontiguousarray(inp["L2_attn_a_W_xp"]),
        "l2aa_wdt": np.ascontiguousarray(inp["L2_attn_a_W_dt"][:, sl]),
        "l2aa_wout": np.ascontiguousarray(inp["L2_attn_a_W_out"]),
        "l2aa_bout": np.ascontiguousarray(
            inp["L2_attn_a_b_out"].reshape(1, D)),
        "l2aa_alpha": np.ascontiguousarray(
            inp["L2_attn_a_alpha"].reshape(1, D).astype(np.float32)),
        "l2aa_beta": np.ascontiguousarray(
            inp["L2_attn_a_beta"].reshape(1, D).astype(np.float32)),
        "c_offs": np.array([[par * TH, par * TH + P * 4 * S, par * TH + P * 8 * S] + [0] * 5],
                           np.int32),
    }


register("l2_attn_a", emit_l2_attn_a, pack_l2_attn_a)


CHAIN = ["l0_attn_a", "l0_ffn", "l0_attn_b", "l0_ffn",
         "l1_attn_a", "l1_ffn", "l1_attn_b", "l1_ffn",
         "l2_attn_a", "l2_ffn", "l2_attn_b", "l2_ffn",
         "l3_attn_a", "l3_ffn", "l3_attn_b", "l3_ffn"]


def kernel(**inputs):
    x = np.asarray(inputs["L0_attn_a_x"], dtype=np.float32)
    return run_subchain(CHAIN, inputs, x)
